# version 07
# speedup vs baseline: 1.0551x; 1.0034x over previous
import os
import numpy as np
import ml_dtypes
from contextlib import ExitStack

A_LIMIT = int(os.environ.get("A_LIMIT", "9"))
P_SECT = int(os.environ.get("P_SECT", "9"))
T_LIM = int(os.environ.get("T_LIM", "99"))

import concourse.bass as bass
import concourse.mybir as mybir
from concourse.bass_utils import run_bass_kernel_spmd

F32 = mybir.dt.float32
BF16 = mybir.dt.bfloat16
AF = mybir.ActivationFunctionType
ALU = mybir.AluOpType
AXX = mybir.AxisListType.X

S = 4096
E = 1024
NH = 16
NHP = 8
HID = 4096
NEXP = 7
TB = 512
NTB = S // TB
EC = E // 128
HCN = HID // 128
LNEPS = 1e-5

CAP = 1024
BIGSLOT = 60000.0
SPARSE = os.environ.get("SPARSE", "1") == "1"
OUT_STAGE = 4


_UN = [0]


def _u(name):
    _UN[0] += 1
    return f"{name}_{_UN[0]}"


class Sem:
    __slots__ = ("h", "n", "name")

    def __init__(self, ctx, nc, name):
        self.h = ctx.enter_context(nc.semaphore(name))
        self.n = 0
        self.name = name

    def inc(self, instr, k=1):
        instr.then_inc(self.h, k)
        self.n += k
        return self.n


_SEM_ALIAS = {
    "a_mm": "f_mz", "a_ev": "f_ev2", "a_st0": "f_o0", "a_st1": "f_o1",
    "a_sv0": "f_sv0", "a_sv1": "f_sv1",
    "n_ld0": "f_l0", "n_ld1": "f_l1", "n_sq": "f_si", "n_nm": "f_gm",
    "n_rd": "f_rk", "n_ra": "f_ix", "n_fin": "f_wc", "n_fin2": "f_sr",
    "t_l": "f_hl", "t_m": "f_sm", "t_qk": "f_ro", "t_ex": "f_re",
    "t_mk": "f_gt", "t_pv": "f_mo", "t_rc": "f_sc2", "t_bc": "f_bc",
    "t_qn": "f_yb", "t_cp": "f_cv", "t_h1": "f_th1",
    "t_k0": "f_ge", "t_k1": "f_cs", "t_k2": "f_mg", "t_k3": "f_mu",
    "t_v0": "f_pp", "t_v1": "f_rx",
    "e_sx": "f_sm2", "e_sq": "f_sj", "e_sm": "f_g2", "e_ac": "f_sc",
    "e_bc": "f_sbc", "e_ap": "f_oht", "e_wo": "f_om", "e_xl0": "f_e0",
    "e_xl1": "f_e1", "e_xa": "f_xa", "e_xs0": "f_xs0", "e_xs1": "f_xs1",
    "e_ch": "f_pe", "dbg": "f_sd",
}
for _suf in ("ld", "bn", "mv", "rs", "ln", "tp", "lt"):
    _SEM_ALIAS["a_" + _suf] = "f_" + _suf


class P:
    def __init__(self, nc, ctx):
        self.nc = nc
        self.ctx = ctx
        self.D = {}
        self.sem = {}

    def s(self, name):
        name = _SEM_ALIAS.get(name, name)
        if name not in self.sem:
            self.sem[name] = Sem(self.ctx, self.nc, name)
        return self.sem[name]

    def base(self, *names):
        return {n: self.s(n).n for n in names}



def emit_attn_a(p, x_src, x_dst):
    nc, D = p.nc, p.D
    ident_sb, ones_bf, ones_f32 = p.ident_sb, p.ones_bf, p.ones_f32

    c = ExitStack()
    EN = c.enter_context
    wo_sb = EN(nc.sbuf_tensor("wo_sb", [128, EC, E], BF16))
    gam_sb = EN(nc.sbuf_tensor("gam_sb", [128, EC], F32))
    bet_sb = EN(nc.sbuf_tensor("bet_sb", [128, EC], F32))
    betb_sb = EN(nc.sbuf_tensor("betb_sb", [128, EC], BF16))
    brow_sb = EN(nc.sbuf_tensor("brow_sb", [1, E], BF16))
    negB = EN(nc.sbuf_tensor("negB", [128, NH], F32))
    cw = ExitStack()
    EW = cw.enter_context
    wq_sb = EW(nc.sbuf_tensor("wq_sb", [128, EC, E], BF16))
    wk_sb = EW(nc.sbuf_tensor("wk_sb", [128, EC, E], BF16))
    wv_sb = EW(nc.sbuf_tensor("wv_sb", [128, EC, E], BF16))
    wstg = EW(nc.sbuf_tensor("wstg", [128, EC, E], F32))

    sw, sc_, spp, sw2 = p.s("g_w"), p.s("g_c"), p.s("g_p"), p.s("g_w2")
    w0, c0, p0, w20 = sw.n, sc_.n, spp.n, sw2.n
    with ExitStack() as pc:
        ps_a = pc.enter_context(nc.psum_tensor("ps_a", [1, E], F32))
        with nc.Block(no_gpsimd_drain=True) as blk:
            @blk.sync
            def _(eng):
                for i, name in enumerate(("wqT", "wkT", "wvpT", "woT")):
                    w3 = D[name].rearrange("(c k) f -> k c f", k=128)
                    if i >= 1:
                        eng.wait_ge(sc_.h, c0 + i)
                    sw.inc(eng.dma_start(out=wstg[:], in_=w3[:]), 16)
                sw2.inc(eng.dma_start(out=gam_sb[:], in_=D["gamma_fm"][:]), 16)
                sw2.inc(eng.dma_start(out=bet_sb[:], in_=D["beta_fm"][:]), 16)

            @blk.vector
            def _(eng):
                dsts = (wq_sb, wk_sb, wv_sb, wo_sb)
                for i in range(4):
                    eng.wait_ge(sw.h, w0 + 16 * (i + 1))
                    if i == 1:
                        sc_.inc(eng.tensor_scalar(out=wk_sb[:], in0=wstg[:],
                                                  scalar1=0.125, scalar2=None,
                                                  op0=ALU.mult))
                    else:
                        sc_.inc(eng.tensor_copy(out=dsts[i][:], in_=wstg[:]))
                eng.wait_ge(sw2.h, w20 + 32)
                sc_.inc(eng.tensor_copy(out=betb_sb[:], in_=bet_sb[:]))

            @blk.tensor
            def _(eng):
                eng.wait_ge(sc_.h, c0 + 5)
                for half in range(2):
                    for e in range(EC):
                        mm = eng.matmul(ps_a[0:1, half * TB:(half + 1) * TB],
                                        betb_sb[:, e:e + 1],
                                        wo_sb[:, e, half * TB:(half + 1) * TB],
                                        start=(e == 0), stop=(e == EC - 1))
                spp.inc(mm)

            @blk.scalar
            def _(eng):
                eng.wait_ge(spp.h, p0 + 1)
                spp.inc(eng.copy(out=brow_sb[:], in_=ps_a[0:1, 0:E]))

        with nc.Block(no_gpsimd_drain=True) as blk:
            @blk.vector
            def _(eng):
                for e in range(EC):
                    eng.tensor_scalar(out=wo_sb[:, e, :], in0=wo_sb[:, e, :],
                                      scalar1=gam_sb[:, e:e + 1], scalar2=None,
                                      op0=ALU.mult)

    if A_LIMIT < 1:
        cw.close(); c.close()
        return

    xt = EW(nc.sbuf_tensor("xt", [128, 4, E], F32))
    lnb = EW(nc.sbuf_tensor("lnb", [128, 4, E], BF16))
    lnT = EW(nc.sbuf_tensor("lnT", [128, EC, TB], BF16))
    bn6 = EW(nc.sbuf_tensor("bn6", [128, 2, 6], F32))
    mvv = EW(nc.sbuf_tensor("mvv", [128, 2], F32))
    lnv = EW(nc.sbuf_tensor("lnv", [128, 1], F32))
    rsc = EW(nc.sbuf_tensor("rsc", [128, 1], F32))
    qkbf = EW(nc.sbuf_tensor("qkbf", [128, 2, TB], BF16))
    vstg = EW(nc.sbuf_tensor("vstg", [128, 2, 4, 130], BF16))

    with nc.Block(no_gpsimd_drain=True) as blk:
        @blk.gpsimd
        def _(eng):
            eng.memset(vstg[:, :, :, 64:65], 1.0)
            eng.memset(vstg[:, :, :, 129:130], 1.0)

    x3 = x_src.rearrange("(n q) e -> n q e", q=128)
    with ExitStack() as pc:
        ps_t = pc.enter_context(nc.psum_tensor("ps_t", [128, 2, 1024], BF16))
        ps_q = pc.enter_context(nc.psum_tensor("ps_q", [128, 2, TB], F32))
        for tb in range(min(NTB, T_LIM)):
            sld = p.s("a_ld"); sbn = p.s("a_bn"); smv = p.s("a_mv")
            srs = p.s("a_rs"); sln = p.s("a_ln"); stp = p.s("a_tp")
            slt = p.s("a_lt"); smm = p.s("a_mm"); sev = p.s("a_ev")
            sst = (p.s("a_st0"), p.s("a_st1")); ssv = (p.s("a_sv0"), p.s("a_sv1"))
            b = p.base("a_ld", "a_bn", "a_mv", "a_rs", "a_ln", "a_tp", "a_lt",
                       "a_mm", "a_ev", "a_st0", "a_st1", "a_sv0", "a_sv1")
            with nc.Block(no_gpsimd_drain=True) as blk:
                @blk.sync
                def _(eng, tb=tb, b=b):
                    eng.wait_ge(sln.h, b["a_ln"])
                    for tc in range(4):
                        sld.inc(eng.dma_start(out=xt[:, tc, :],
                                              in_=x3[tb * 4 + tc]), 16)

                @blk.vector
                def _(eng, tb=tb, b=b):
                    eng.wait_ge(stp.h, b["a_tp"])
                    for tc in range(4):
                        eng.wait_ge(sld.h, b["a_ld"] + 64)
                        if tc >= 1:
                            eng.wait_ge(srs.h, b["a_rs"] + 2 * tc)
                        sbn.inc(eng.bn_stats(out=bn6[:, 0, :], in_=xt[:, tc, 0:512]))
                        sbn.inc(eng.bn_stats(out=bn6[:, 1, :], in_=xt[:, tc, 512:1024]))
                        eng.wait_ge(sbn.h, b["a_bn"] + 2 * (tc + 1))
                        smv.inc(eng.bn_aggr(out=mvv[:], in_=bn6[:].rearrange("q a v -> q (a v)")))
                        eng.wait_ge(srs.h, b["a_rs"] + 2 * (tc + 1))
                        sln.inc(eng.tensor_scalar(out=lnb[:, tc, :], in0=xt[:, tc, :],
                                                  scalar1=mvv[:, 0:1], op0=ALU.subtract,
                                                  scalar2=rsc[:, 0:1], op1=ALU.mult))
                    if P_SECT < 2:
                        return
                    eng.wait_ge(smm.h, b["a_mm"])
                    for k in range(32):
                        tc, e = divmod(k, EC)
                        eng.wait_ge(stp.h, b["a_tp"] + k + 1)
                        slt.inc(eng.tensor_copy(out=lnT[:, e, tc * 128:(tc + 1) * 128],
                                                in_=ps_t[:, k % 2, 0:128]))
                    if P_SECT < 4:
                        return
                    for i in range(24):
                        eng.wait_ge(smm.h, b["a_mm"] + i + 1)
                        if i < 16:
                            if i >= 2:
                                eng.wait_ge(sst[i % 2].h,
                                            b["a_st%d" % (i % 2)] + 16 * (i // 2))
                            sev.inc(eng.tensor_copy(out=qkbf[:, i % 2, :],
                                                    in_=ps_q[:, i % 2, :]))
                        else:
                            j = i - 16
                            if j >= 2:
                                eng.wait_ge(ssv[j % 2].h,
                                            b["a_sv%d" % (j % 2)] + 64 * (j // 2))
                            pv = ps_q[:, i % 2, :].rearrange("q (w r) -> q w r", w=4)
                            eng.tensor_copy(out=vstg[:, j % 2, :, 0:64],
                                            in_=pv[:, :, 0:64])
                            sev.inc(eng.tensor_copy(out=vstg[:, j % 2, :, 65:129],
                                                    in_=pv[:, :, 64:128]))

                @blk.scalar
                def _(eng, tb=tb, b=b):
                    for tc in range(4):
                        eng.wait_ge(smv.h, b["a_mv"] + tc + 1)
                        if tc >= 1:
                            eng.wait_ge(srs.h, b["a_rs"] + 2 * tc)
                        srs.inc(eng.activation(out=lnv[:], in_=mvv[:, 1:2], func=AF.Ln,
                                               bias=p.eps_col[:, 0:1], scale=1.0))
                        eng.wait_ge(srs.h, b["a_rs"] + 2 * tc + 1)
                        eng.wait_ge(sln.h, b["a_ln"] + tc)
                        srs.inc(eng.activation(out=rsc[:], in_=lnv[:], func=AF.Exp,
                                               scale=-0.5))

                @blk.tensor
                def _(eng, tb=tb, b=b):
                    if P_SECT < 2:
                        return
                    for k in range(32):
                        tc, e = divmod(k, EC)
                        eng.wait_ge(sln.h, b["a_ln"] + tc + 1)
                        if k >= 2:
                            eng.wait_ge(slt.h, b["a_lt"] + k - 1)
                        stp.inc(eng.transpose(ps_t[:, k % 2, 0:128],
                                              lnb[:, tc, e * 128:(e + 1) * 128],
                                              ident_sb[:]))
                    if P_SECT < 3:
                        return
                    eng.wait_ge(slt.h, b["a_lt"] + 32)
                    if P_SECT == 3:
                        for e in range(EC):
                            mm = eng.matmul(ps_q[:, 0, :], wq_sb[:, e, 0:128],
                                            lnT[:, e, :], start=(e == 0), stop=(e == EC - 1))
                        smm.inc(mm)
                        smm.n += 23
                        return
                    nmm = 0
                    for wsb in (wq_sb, wk_sb):
                        for fc in range(EC):
                            if nmm >= 2:
                                eng.wait_ge(sev.h, b["a_ev"] + nmm - 1)
                            for e in range(EC):
                                mm = eng.matmul(ps_q[:, nmm % 2, :],
                                                wsb[:, e, fc * 128:(fc + 1) * 128],
                                                lnT[:, e, :],
                                                start=(e == 0), stop=(e == EC - 1))
                            smm.inc(mm)
                            nmm += 1
                    for tc in range(4):
                        for half in range(2):
                            eng.wait_ge(sev.h, b["a_ev"] + nmm - 1)
                            for e in range(EC):
                                mm = eng.matmul(ps_q[:, nmm % 2, :],
                                                lnT[:, e, tc * 128:(tc + 1) * 128],
                                                wv_sb[:, e, half * TB:(half + 1) * TB],
                                                start=(e == 0), stop=(e == EC - 1))
                            smm.inc(mm)
                            nmm += 1

                @blk.gpsimd
                def _(eng, tb=tb, b=b):
                    if P_SECT < 4:
                        return
                    for i in range(24):
                        eng.wait_ge(sev.h, b["a_ev"] + i + 1)
                        if i < 8:
                            sst[i % 2].inc(eng.dma_start(
                                out=D["qT_d"][i * 128:(i + 1) * 128, tb * TB:(tb + 1) * TB],
                                in_=qkbf[:, i % 2, :]), 16)
                        elif i < 16:
                            f = i - 8
                            sst[i % 2].inc(eng.dma_start(
                                out=D["kT_d"][f * 128:(f + 1) * 128, tb * TB:(tb + 1) * TB],
                                in_=qkbf[:, i % 2, :]), 16)
                        else:
                            j = i - 16
                            tc, half = divmod(j, 2)
                            t0 = tb * TB + tc * 128
                            for hpi in range(4):
                                ssv[j % 2].inc(eng.dma_start(
                                    out=D["v_d"][half * 4 + hpi, t0:t0 + 128, :],
                                    in_=vstg[:, j % 2, hpi, :]), 16)
                    eng.wait_ge(sst[0].h, b["a_st0"] + 128)
                    eng.wait_ge(sst[1].h, b["a_st1"] + 128)
                    eng.wait_ge(ssv[0].h, b["a_sv0"] + 256)
                    eng.wait_ge(ssv[1].h, b["a_sv1"] + 256)

            if P_SECT >= 9:
                assert sld.n - b["a_ld"] == 64 and smm.n - b["a_mm"] == 24
                assert sev.n - b["a_ev"] == 24 and stp.n - b["a_tp"] == 32
                assert srs.n - b["a_rs"] == 8

    cw.close()
    if A_LIMIT < 2:
        c.close()
        return

    cn = ExitStack()
    ENN = cn.enter_context
    qmax2 = ENN(nc.sbuf_tensor("qmax2", [1, 2, NH], F32))
    sqin = ENN(nc.sbuf_tensor("sqin", [128, 2, S], BF16))
    sqsq = ENN(nc.sbuf_tensor("sqsq", [128, 2, S], BF16))
    nrow = ENN(nc.sbuf_tensor("nrow", [1, 4], F32))
    nbrow = ENN(nc.sbuf_tensor("nbrow", [1, NH], F32))
    nbrow_b = ENN(nc.sbuf_tensor("nbrow_b", [1, NH], F32))

    with nc.Block(no_gpsimd_drain=True) as blk:
        @blk.gpsimd
        def _(eng):
            eng.memset(qmax2[:], 0.0)

    sld = (p.s("n_ld0"), p.s("n_ld1")); ssq = p.s("n_sq"); snm = p.s("n_nm")
    srd = p.s("n_rd"); sra = p.s("n_ra")
    b = p.base("n_ld0", "n_ld1", "n_sq", "n_nm", "n_rd", "n_ra")
    with ExitStack() as pc:
        ps_n = pc.enter_context(nc.psum_tensor("ps_n", [1, 4, TB], F32))
        with nc.Block(no_gpsimd_drain=True) as blk:
            @blk.sync
            def _(eng):
                for i in range(16):
                    src = D["qT_d"] if i < 8 else D["kT_d"]
                    f = (i % 8) * 128
                    if i >= 2:
                        eng.wait_ge(ssq.h, b["n_sq"] + i - 1)
                    sld[i % 2].inc(eng.dma_start(out=sqin[:, i % 2, :],
                                                 in_=src[f:f + 128, :]), 16)

            @blk.scalar
            def _(eng):
                for i in range(16):
                    eng.wait_ge(sld[i % 2].h, b["n_ld%d" % (i % 2)] + 16 * (i // 2 + 1))
                    if i >= 2:
                        eng.wait_ge(snm.h, b["n_nm"] + 16 * (i - 1))
                    ssq.inc(eng.activation(out=sqsq[:, i % 2, :], in_=sqin[:, i % 2, :],
                                           func=AF.Square))

            @blk.tensor
            def _(eng):
                for i in range(16):
                    eng.wait_ge(ssq.h, b["n_sq"] + i + 1)
                    for sl in range(8):
                        for hh in range(2):
                            k = sl * 2 + hh
                            g = i * 16 + k
                            if g >= 4:
                                eng.wait_ge(sra.h, b["n_ra"] + g - 3)
                            snm.inc(eng.matmul(
                                ps_n[0:1, k % 4, :],
                                ones_bf[hh * 64:hh * 64 + 64, 0:1],
                                sqsq[hh * 64:(hh + 1) * 64, i % 2, sl * TB:(sl + 1) * TB],
                                start=True, stop=True))

            @blk.vector
            def _(eng):
                for i in range(16):
                    which = i // 8
                    for sl in range(8):
                        for hh in range(2):
                            k = sl * 2 + hh
                            g = i * 16 + k
                            h = (i % 8) * 2 + hh
                            eng.wait_ge(snm.h, b["n_nm"] + g + 1)
                            if g >= 4:
                                eng.wait_ge(sra.h, b["n_ra"] + g - 3)
                            srd.inc(eng.tensor_reduce(out=nrow[0:1, k % 4:k % 4 + 1],
                                                      in_=ps_n[0:1, k % 4, :],
                                                      axis=AXX, op=ALU.max))
                            eng.wait_ge(srd.h, b["n_rd"] + g + 1)
                            if g >= 2:
                                eng.wait_ge(sra.h, b["n_ra"] + g - 1)
                            sra.inc(eng.tensor_tensor(out=qmax2[0:1, which, h:h + 1],
                                                      in0=qmax2[0:1, which, h:h + 1],
                                                      in1=nrow[0:1, k % 4:k % 4 + 1],
                                                      op=ALU.max))

    sx = p.s("n_fin")
    b_sx = sx.n
    with ExitStack() as pc:
        ps_bx = pc.enter_context(nc.psum_tensor("ps_bx", [128, NH], F32))
        with nc.Block(no_gpsimd_drain=True) as blk:
            @blk.vector
            def _(eng):
                sx.inc(eng.tensor_tensor(out=nbrow[:], in0=qmax2[0:1, 0, :],
                                         in1=qmax2[0:1, 1, :], op=ALU.mult))
                eng.wait_ge(sx.h, b_sx + 3)
                eng.tensor_scalar(out=negB[:], in0=ps_bx[:], scalar1=-1.0,
                                  scalar2=None, op0=ALU.mult)

            @blk.scalar
            def _(eng):
                eng.wait_ge(sx.h, b_sx + 1)
                sx.inc(eng.activation(out=nbrow_b[:], in_=nbrow[:], func=AF.Sqrt,
                                      scale=1.0))

            @blk.tensor
            def _(eng):
                eng.wait_ge(sx.h, b_sx + 2)
                sx.inc(eng.matmul(ps_bx[:], ones_f32[0:1, 0:128], nbrow_b[:],
                                  start=True, stop=True))

    cn.close()
    if A_LIMIT < 3:
        c.close()
        return

    kt_sb = EN(nc.sbuf_tensor("kt_sb", [128, S], BF16))
    vt_sb = EN(nc.sbuf_tensor("vt_sb", [128, 32, 130], BF16))
    qt_sb = EN(nc.sbuf_tensor("qt_sb", [128, TB], BF16))
    pT_sb = EN(nc.sbuf_tensor("pT_sb", [128, 4, TB], BF16))
    qkvFM = EN(nc.sbuf_tensor("qkvFM", [128, EC, TB], BF16))
    linv_sb = EN(nc.sbuf_tensor("linv_sb", [128, 2, TB], F32))
    qtmp_sb = EN(nc.sbuf_tensor("qtmp_sb", [128, 2, TB], F32))
    h1stg = EN(nc.sbuf_tensor("h1stg", [128, TB], BF16))
    cm_sb = EN(nc.sbuf_tensor("cm_sb", [128, 4, TB], BF16))
    sqf = EN(nc.sbuf_tensor("sqf", [128, EC, TB], BF16))
    lnq = EN(nc.sbuf_tensor("lnq", [128, EC, TB], BF16))
    rowst = EN(nc.sbuf_tensor("rowst", [1, 6, TB], F32))
    xe_sb = EN(nc.sbuf_tensor("xe_sb", [128, 2, E], F32))
    xo_sb = EN(nc.sbuf_tensor("xo_sb", [128, 2, E], F32))

    sldm = p.s("t_m")
    with nc.Block(no_gpsimd_drain=True) as blk:
        @blk.sync
        def _(eng):
            sldm.inc(eng.dma_start(out=cm_sb[:], in_=D["cmask"][:]), 16)
            eng.wait_ge(sldm.h, 16)

    xd3 = x_dst.rearrange("(n q) e -> n q e", q=128)
    for qb in range(NTB):
        ncs = 4 * (qb + 1)
        for hp in range(NHP):
            sld = p.s("t_l"); sqk = p.s("t_qk"); sex = p.s("t_ex")
            smk = p.s("t_mk"); spv = p.s("t_pv"); src_ = p.s("t_rc")
            sbc = p.s("t_bc"); sqn = p.s("t_qn"); sh1 = p.s("t_h1")
            scp = p.s("t_cp")
            skt = [p.s("t_k%d" % i) for i in range(4)]
            svt = [p.s("t_v%d" % i) for i in range(2)]
            b = p.base("t_l", "t_qk", "t_ex", "t_mk", "t_pv", "t_rc", "t_bc",
                       "t_qn", "t_h1", "t_cp", "t_k0", "t_k1", "t_k2", "t_k3",
                       "t_v0", "t_v1")
            with ExitStack() as pc:
                ps_s = pc.enter_context(nc.psum_tensor(_u("ps_s"), [128, 4, TB], F32))
                ps_v = pc.enter_context(nc.psum_tensor(_u("ps_v"), [128, 2, TB], F32))
                ps_b = pc.enter_context(nc.psum_tensor(_u("ps_b"), [128, 2, TB], F32))
                with nc.Block(no_gpsimd_drain=True) as blk:
                    @blk.sync
                    def _(eng, qb=qb, hp=hp, ncs=ncs, b=b):
                        f = hp * 128
                        sld.inc(eng.dma_start(out=qt_sb[:],
                                              in_=D["qT_d"][f:f + 128, qb * TB:(qb + 1) * TB]), 16)
                        v4 = D["v_d"][hp].rearrange("(cq q) f -> q cq f", q=128)
                        qsz = (qb + 1) * 128
                        hsz = ncs // 2
                        skt[0].inc(eng.dma_start(out=kt_sb[:, 0:qsz],
                                                 in_=D["kT_d"][f:f + 128, 0:qsz]), 16)
                        svt[0].inc(eng.dma_start(out=vt_sb[:, 0:hsz, :],
                                                 in_=v4[:, 0:hsz, :]), 16)
                        skt[1].inc(eng.dma_start(out=kt_sb[:, qsz:2 * qsz],
                                                 in_=D["kT_d"][f:f + 128, qsz:2 * qsz]), 16)
                        svt[1].inc(eng.dma_start(out=vt_sb[:, hsz:ncs, :],
                                                 in_=v4[:, hsz:ncs, :]), 16)
                        skt[2].inc(eng.dma_start(out=kt_sb[:, 2 * qsz:3 * qsz],
                                                 in_=D["kT_d"][f:f + 128, 2 * qsz:3 * qsz]), 16)
                        skt[3].inc(eng.dma_start(out=kt_sb[:, 3 * qsz:4 * qsz],
                                                 in_=D["kT_d"][f:f + 128, 3 * qsz:4 * qsz]), 16)

                    @blk.tensor
                    def _(eng, qb=qb, hp=hp, ncs=ncs, b=b):
                        eng.wait_ge(sld.h, b["t_l"] + 16)

                        def pv(sc):
                            diag = sc >= 4 * qb
                            if sc % (ncs // 2) == 0:
                                eng.wait_ge(svt[sc // (ncs // 2)].h,
                                            b["t_v%d" % (sc // (ncs // 2))] + 16)
                            for hh in range(2):
                                if diag:
                                    eng.wait_ge(smk.h, b["t_mk"] + 2 * (sc - 4 * qb) + hh + 1)
                                else:
                                    eng.wait_ge(sex.h, b["t_ex"] + 2 * sc + hh + 1)
                                slot = (sc % 2) * 2 + hh
                                spv.inc(eng.matmul(ps_v[0:65, hh, :],
                                                   vt_sb[:, sc, hh * 65:(hh + 1) * 65],
                                                   pT_sb[:, slot, :],
                                                   start=(sc == 0), stop=(sc == ncs - 1)))

                        for sc in range(ncs):
                            if sc % (qb + 1) == 0:
                                eng.wait_ge(skt[sc // (qb + 1)].h,
                                            b["t_k%d" % (sc // (qb + 1))] + 16)
                            for hh in range(2):
                                if sc >= 2:
                                    eng.wait_ge(sex.h, b["t_ex"] + 2 * sc - 2)
                                slot = (sc % 2) * 2 + hh
                                sqk.inc(eng.matmul(ps_s[:, slot, :],
                                                   kt_sb[hh * 64:(hh + 1) * 64, sc * 128:(sc + 1) * 128],
                                                   qt_sb[hh * 64:(hh + 1) * 64, :],
                                                   start=True, stop=True,
                                                   tile_position=(hh * 64, 0)))
                            if sc >= 1:
                                pv(sc - 1)
                        pv(ncs - 1)
                        for hh in range(2):
                            eng.wait_ge(src_.h, b["t_rc"] + hh + 1)
                            sbc.inc(eng.matmul(ps_b[0:64, hh, :],
                                               ones_f32[64:65, 0:64],
                                               linv_sb[64:65, hh, :],
                                               start=True, stop=True))

                    @blk.scalar
                    def _(eng, qb=qb, hp=hp, ncs=ncs, b=b):
                        for sc in range(ncs):
                            for hh in range(2):
                                h = hp * 2 + hh
                                eng.wait_ge(sqk.h, b["t_qk"] + 2 * sc + hh + 1)
                                if 2 * sc + hh >= 4:
                                    eng.wait_ge(spv.h, b["t_pv"] + 2 * sc + hh - 3)
                                slot = (sc % 2) * 2 + hh
                                sex.inc(eng.activation(out=pT_sb[:, slot, :],
                                                       in_=ps_s[:, slot, :],
                                                       func=AF.Exp,
                                                       bias=negB[:, h:h + 1],
                                                       scale=1.0))
                        eng.wait_ge(spv.h, b["t_pv"] + 2 * ncs)
                        for hh in range(2):
                            scp.inc(eng.copy(out=qtmp_sb[0:64, hh, :],
                                             in_=ps_v[0:64, hh, :]))

                    @blk.vector
                    def _(eng, qb=qb, hp=hp, ncs=ncs, b=b):
                        for sc in range(4 * qb, ncs):
                            r = sc - 4 * qb
                            for hh in range(2):
                                eng.wait_ge(sex.h, b["t_ex"] + 2 * sc + hh + 1)
                                slot = (sc % 2) * 2 + hh
                                smk.inc(eng.tensor_tensor(out=pT_sb[:, slot, :],
                                                          in0=pT_sb[:, slot, :],
                                                          in1=cm_sb[:, r, :],
                                                          op=ALU.mult))
                        eng.wait_ge(spv.h, b["t_pv"] + 2 * ncs)
                        for hh in range(2):
                            src_.inc(eng.reciprocal(out=linv_sb[64:65, hh, :],
                                                    in_=ps_v[64:65, hh, :]))
                        eng.wait_ge(sbc.h, b["t_bc"] + 1)
                        eng.wait_ge(scp.h, b["t_cp"] + 1)
                        sqn.inc(eng.tensor_tensor(out=qkvFM[0:64, hp, :],
                                                  in0=qtmp_sb[0:64, 0, :],
                                                  in1=ps_b[0:64, 0, :], op=ALU.mult))
                        eng.wait_ge(sbc.h, b["t_bc"] + 2)
                        eng.wait_ge(scp.h, b["t_cp"] + 2)
                        sqn.inc(eng.tensor_tensor(out=h1stg[0:64, :],
                                                  in0=qtmp_sb[0:64, 1, :],
                                                  in1=ps_b[0:64, 1, :], op=ALU.mult))

                    @blk.gpsimd
                    def _(eng, qb=qb, hp=hp, ncs=ncs, b=b):
                        eng.wait_ge(sqn.h, b["t_qn"] + 2)
                        sh1.inc(eng.dma_start(out=qkvFM[64:128, hp, :],
                                              in_=h1stg[0:64, :]), 16)
                        eng.wait_ge(sh1.h, b["t_h1"] + 16)

            assert sqk.n - b["t_qk"] == 2 * ncs and spv.n - b["t_pv"] == 2 * ncs
            assert sex.n - b["t_ex"] == 2 * ncs and sqn.n - b["t_qn"] == 2

        ssx = p.s("e_sx"); ssq2 = p.s("e_sq"); ssm = p.s("e_sm")
        sac = p.s("e_ac"); sbc2 = p.s("e_bc"); sap = p.s("e_ap")
        swo = p.s("e_wo"); sxa = p.s("e_xa"); ech = p.s("e_ch")
        sxl = (p.s("e_xl0"), p.s("e_xl1")); sxs = (p.s("e_xs0"), p.s("e_xs1"))
        b = p.base("e_sx", "e_sq", "e_sm", "e_ac", "e_bc", "e_ap", "e_wo",
                   "e_xl0", "e_xl1", "e_xa", "e_xs0", "e_xs1", "e_ch")
        with ExitStack() as pc:
            ps_r = pc.enter_context(nc.psum_tensor(_u("ps_r"), [1, 2, TB], F32))
            ps_bc = pc.enter_context(nc.psum_tensor(_u("ps_bc"), [128, 2, TB], F32))
            ps_o = pc.enter_context(nc.psum_tensor(_u("ps_o"), [128, 2, TB], F32))
            with nc.Block(no_gpsimd_drain=True) as blk:
                @blk.scalar
                def _(eng, qb=qb, b=b):
                    for e in range(EC):
                        ssq2.inc(eng.activation(out=sqf[:, e, :], in_=qkvFM[:, e, :],
                                                func=AF.Square))
                    eng.wait_ge(ssx.h, b["e_sx"] + 1)
                    sac.inc(eng.activation(out=rowst[0:1, 5, :], in_=rowst[0:1, 1, :],
                                           func=AF.Ln, bias=p.eps_col[0:1, 0:1], scale=1.0))
                    eng.wait_ge(sac.h, b["e_ac"] + 1)
                    sac.inc(eng.activation(out=rowst[0:1, 2, :], in_=rowst[0:1, 5, :],
                                           func=AF.Exp, scale=-0.5))

                @blk.tensor
                def _(eng, qb=qb, b=b):
                    for e in range(EC):
                        eng.matmul(ps_r[0:1, 0, :], ones_bf[:, 0:1], qkvFM[:, e, :],
                                   start=(e == 0), stop=(e == EC - 1))
                    for e in range(EC):
                        eng.wait_ge(ssq2.h, b["e_sq"] + e + 1)
                        mm = eng.matmul(ps_r[0:1, 1, :], ones_bf[:, 0:1], sqf[:, e, :],
                                        start=(e == 0), stop=(e == EC - 1))
                    ssm.inc(mm)
                    eng.wait_ge(ssx.h, b["e_sx"] + 2)
                    eng.matmul(ps_bc[:, 0, :], ones_f32[0:1, 0:128],
                               rowst[0:1, 2, :], start=True, stop=True)
                    sbc2.inc(eng.matmul(ps_bc[:, 1, :], ones_f32[0:1, 0:128],
                                        rowst[0:1, 3, :], start=True, stop=True))
                    eng.wait_ge(sap.h, b["e_ap"] + EC)
                    for tc in range(4):
                        if tc >= 1:
                            eng.wait_ge(sxa.h, b["e_xa"] + 2 * tc)
                        for half in range(2):
                            for e in range(EC):
                                eng.matmul(ps_o[:, half, :],
                                           lnq[:, e, tc * 128:(tc + 1) * 128],
                                           wo_sb[:, e, half * TB:(half + 1) * TB],
                                           start=(e == 0), stop=False)
                            mm = eng.matmul(ps_o[:, half, :], ones_bf[0:1, 0:128],
                                            brow_sb[0:1, half * TB:(half + 1) * TB],
                                            start=False, stop=True)
                        swo.inc(mm)

                @blk.vector
                def _(eng, qb=qb, b=b):
                    eng.wait_ge(ssm.h, b["e_sm"] + 1)
                    ech.inc(eng.tensor_scalar(out=rowst[0:1, 0, :], in0=ps_r[0:1, 0, :],
                                              scalar1=1.0 / E, scalar2=None, op0=ALU.mult))
                    ech.inc(eng.tensor_scalar(out=rowst[0:1, 1, :], in0=ps_r[0:1, 1, :],
                                              scalar1=1.0 / E, scalar2=None, op0=ALU.mult))
                    eng.wait_ge(ech.h, b["e_ch"] + 2)
                    ech.inc(eng.tensor_tensor(out=rowst[0:1, 4, :], in0=rowst[0:1, 0, :],
                                              in1=rowst[0:1, 0, :], op=ALU.mult))
                    eng.wait_ge(ech.h, b["e_ch"] + 3)
                    ssx.inc(eng.tensor_tensor(out=rowst[0:1, 1, :], in0=rowst[0:1, 1, :],
                                              in1=rowst[0:1, 4, :], op=ALU.subtract))
                    eng.wait_ge(sac.h, b["e_ac"] + 2)
                    ech.inc(eng.tensor_tensor(out=rowst[0:1, 3, :], in0=rowst[0:1, 0, :],
                                              in1=rowst[0:1, 2, :], op=ALU.mult))
                    eng.wait_ge(ech.h, b["e_ch"] + 4)
                    ssx.inc(eng.tensor_scalar(out=rowst[0:1, 3, :], in0=rowst[0:1, 3, :],
                                              scalar1=-1.0, scalar2=None, op0=ALU.mult))
                    eng.wait_ge(sbc2.h, b["e_bc"] + 1)
                    for e in range(EC):
                        ech.inc(eng.tensor_tensor(out=lnq[:, e, :], in0=qkvFM[:, e, :],
                                                  in1=ps_bc[:, 0, :], op=ALU.mult))
                        eng.wait_ge(ech.h, b["e_ch"] + 5 + e)
                        sap.inc(eng.tensor_tensor(out=lnq[:, e, :], in0=lnq[:, e, :],
                                                  in1=ps_bc[:, 1, :], op=ALU.add))
                    for tc in range(4):
                        eng.wait_ge(swo.h, b["e_wo"] + tc + 1)
                        eng.wait_ge(sxl[tc % 2].h, b["e_xl%d" % (tc % 2)] + 16 * (tc // 2 + 1))
                        if tc >= 2:
                            eng.wait_ge(sxs[tc % 2].h, b["e_xs%d" % (tc % 2)] + 16 * (tc // 2))
                        for half in range(2):
                            sxa.inc(eng.tensor_tensor(
                                out=xo_sb[:, tc % 2, half * TB:(half + 1) * TB],
                                in0=xe_sb[:, tc % 2, half * TB:(half + 1) * TB],
                                in1=ps_o[:, half, :], op=ALU.add))

                @blk.sync
                def _(eng, qb=qb, b=b):
                    for tc in range(4):
                        if tc >= 2:
                            eng.wait_ge(sxa.h, b["e_xa"] + 2 * (tc - 1))
                        sxl[tc % 2].inc(eng.dma_start(out=xe_sb[:, tc % 2, :],
                                                      in_=x3[qb * 4 + tc]), 16)

                @blk.gpsimd
                def _(eng, qb=qb, b=b):
                    if os.environ.get("DBG"):
                        sdbg = p.s("dbg")
                        for e in range(EC):
                            sdbg.inc(eng.dma_start(
                                out=D["qkv_dbg"][e * 128:(e + 1) * 128, qb * TB:(qb + 1) * TB],
                                in_=qkvFM[:, e, :]), 16)
                        eng.wait_ge(sap.h, b["e_ap"] + EC)
                        for e in range(EC):
                            sdbg.inc(eng.dma_start(
                                out=D["lnq_dbg"][e * 128:(e + 1) * 128, qb * TB:(qb + 1) * TB],
                                in_=lnq[:, e, :]), 16)
                        eng.wait_ge(sdbg.h, sdbg.n)
                    for tc in range(4):
                        eng.wait_ge(sxa.h, b["e_xa"] + 2 * (tc + 1))
                        sxs[tc % 2].inc(eng.dma_start(out=xd3[qb * 4 + tc],
                                                      in_=xo_sb[:, tc % 2, :]), 16)
                    eng.wait_ge(sxs[0].h, b["e_xs0"] + 32)
                    eng.wait_ge(sxs[1].h, b["e_xs1"] + 32)

        assert sap.n - b["e_ap"] == EC and swo.n - b["e_wo"] == 4
        assert sxa.n - b["e_xa"] == 8

    c.close()



def emit_ffn(p, x_src, x_dst, tag, final, sparse=SPARSE):
    nc, D = p.nc, p.D
    ident_sb, ones_bf = p.ident_sb, p.ones_bf

    c = ExitStack()
    EN = c.enter_context
    scale_sb = EN(nc.sbuf_tensor(_u("f_scale"), [128, 32], F32))
    negidx_sb = EN(nc.sbuf_tensor(_u("f_negidx"), [128, 32], F32))
    sj_sb = EN(nc.sbuf_tensor(_u("f_sj"), [128, 32], F32))
    wr_sb = EN(nc.sbuf_tensor(_u("f_wr"), [128, EC, 8], BF16))
    wrstg = EN(nc.sbuf_tensor(_u("f_wrstg"), [128, EC, 8], F32))
    bmi_sb = EN(nc.sbuf_tensor(_u("f_bmi"), [128, 8], F32))
    slot_f32 = EN(nc.sbuf_tensor(_u("f_slot"), [128, 32], F32))
    srowb = EN(nc.sbuf_tensor(_u("f_srowb"), [1, TB], F32))
    slotT_sb = EN(nc.sbuf_tensor(_u("f_slotT"), [32, 128], F32))
    iotac_sb = EN(nc.sbuf_tensor(_u("f_iotac"), [128, CAP], F32))
    iota128_sb = EN(nc.sbuf_tensor(_u("f_io128"), [128, 1], F32))
    ln_tm = EN(nc.sbuf_tensor(_u("f_lntm"), [128, 32, E], BF16))
    offrow = EN(nc.sbuf_tensor(_u("f_off"), [1, 7], F32))
    negj_sb = EN(nc.sbuf_tensor(_u("f_negj"), [128, 7], F32))
    thr_sb = EN(nc.sbuf_tensor(_u("f_thr"), [128, 7], F32))
    ltri_sb = EN(nc.sbuf_tensor(_u("f_ltri"), [128, 128], BF16))
    oh_sb = EN(nc.sbuf_tensor(_u("f_oh"), [128, 7], BF16))
    t7a = EN(nc.sbuf_tensor(_u("f_t7a"), [128, 7], F32))
    t7b = EN(nc.sbuf_tensor(_u("f_t7b"), [128, 7], F32))
    ccol = EN(nc.sbuf_tensor(_u("f_ccol"), [128, 3], F32))
    c_ln = ExitStack()
    lnT = c_ln.enter_context(nc.sbuf_tensor(_u("f_lnT"), [128, EC, S], BF16))

    x3 = x_src.rearrange("(n q) e -> n q e", q=128)
    xd3 = x_dst.rearrange("(n q) e -> n q e", q=128)

    cb = ExitStack()
    EB = cb.enter_context
    xt = EB(nc.sbuf_tensor(_u("f_xt"), [128, 4, E], F32))
    bn6 = EB(nc.sbuf_tensor(_u("f_bn6"), [128, 2, 6], F32))
    mvv = EB(nc.sbuf_tensor(_u("f_mvv"), [128, 2], F32))
    lnv = EB(nc.sbuf_tensor(_u("f_lnv"), [128, 1], F32))
    rsc = EB(nc.sbuf_tensor(_u("f_rsc"), [128, 1], F32))
    rcol = EB(nc.sbuf_tensor(_u("f_rcol"), [128, 6], F32))
    esb = EB(nc.sbuf_tensor(_u("f_esb"), [128, 8], F32))
    eqsb = EB(nc.sbuf_tensor(_u("f_eqsb"), [128, 8], F32))

    sw = p.s(tag + "w")
    b_w = sw.n
    with nc.Block(no_gpsimd_drain=True) as blk:
        @blk.sync
        def _(eng):
            w3 = D["wr"].rearrange("(c k) j -> k c j", k=128)
            sw.inc(eng.dma_start(out=wrstg[:], in_=w3[:]), 16)
            sw.inc(eng.dma_start(out=bmi_sb[:], in_=D["bmi"][:]), 16)
            sw.inc(eng.dma_start(out=negj_sb[:], in_=D["negj"][:]), 16)
            sw.inc(eng.dma_start(out=thr_sb[:], in_=D["thr"][:]), 16)
            sw.inc(eng.dma_start(out=ltri_sb[:], in_=D["ltri"][:]), 16)
            sw.inc(eng.dma_start(out=offrow[:], in_=D["baserow"][:]), 16)
            sw.inc(eng.dma_start(out=iotac_sb[:], in_=D["iotac"][:]), 16)
            sw.inc(eng.dma_start(out=iota128_sb[:], in_=D["iota128"][:]), 16)

        @blk.vector
        def _(eng):
            eng.wait_ge(sw.h, b_w + 128)
            sw.inc(eng.tensor_copy(out=wr_sb[:], in_=wrstg[:]))

    with ExitStack() as pc:
        ps_t = pc.enter_context(nc.psum_tensor(_u("f_pst"), [128, 2, 1024], BF16))
        ps_l = pc.enter_context(nc.psum_tensor(_u("f_psl"), [128, 2, 512], F32))
        if sparse:
            ps_rk = pc.enter_context(nc.psum_tensor(_u("f_psrk"), [128, 2, 512], F32))
            ps_ct = pc.enter_context(nc.psum_tensor(_u("f_psct"), [1, 2, 512], F32))
        for tb in range(NTB):
            sld = p.s(tag + "ld"); sbn = p.s(tag + "bn"); smv = p.s(tag + "mv")
            srs = p.s(tag + "rs"); sln = p.s(tag + "ln"); stp = p.s(tag + "tp")
            slt = p.s(tag + "lt"); sro = p.s(tag + "ro"); srx = p.s(tag + "rx")
            sre = p.s(tag + "re")
            six = p.s(tag + "ix"); srk = p.s(tag + "rk"); ssct = p.s(tag + "sct")
            b = p.base(*(tag + nm for nm in ("ld", "bn", "mv", "rs", "ln", "tp",
                                             "lt", "ro", "rx", "re", "ix", "rk",
                                             "sct")))

            def B(nm, b=b):
                return b[tag + nm]

            with nc.Block(no_gpsimd_drain=True) as blk:
                @blk.sync
                def _(eng, tb=tb, B=B):
                    eng.wait_ge(sln.h, B("ln"))
                    for tc in range(4):
                        sld.inc(eng.dma_start(out=xt[:, tc, :],
                                              in_=x3[tb * 4 + tc]), 16)

                @blk.vector
                def _(eng, tb=tb, B=B):
                    eng.wait_ge(stp.h, B("tp"))
                    for tc in range(4):
                        eng.wait_ge(sld.h, B("ld") + 64)
                        if tc >= 1:
                            eng.wait_ge(srs.h, B("rs") + 2 * tc)
                        sbn.inc(eng.bn_stats(out=bn6[:, 0, :], in_=xt[:, tc, 0:512]))
                        sbn.inc(eng.bn_stats(out=bn6[:, 1, :], in_=xt[:, tc, 512:1024]))
                        eng.wait_ge(sbn.h, B("bn") + 2 * (tc + 1))
                        smv.inc(eng.bn_aggr(out=mvv[:], in_=bn6[:].rearrange("q a v -> q (a v)")))
                        eng.wait_ge(srs.h, B("rs") + 2 * (tc + 1))
                        sln.inc(eng.tensor_scalar(out=ln_tm[:, tb * 4 + tc, :],
                                                  in0=xt[:, tc, :],
                                                  scalar1=mvv[:, 0:1], op0=ALU.subtract,
                                                  scalar2=rsc[:, 0:1], op1=ALU.mult))
                    for k in range(32):
                        tc, e = divmod(k, EC)
                        eng.wait_ge(stp.h, B("tp") + k + 1)
                        slt.inc(eng.tensor_copy(
                            out=lnT[:, e, tb * TB + tc * 128:tb * TB + (tc + 1) * 128],
                            in_=ps_t[:, k % 2, 0:128]))
                    for tc in range(4):
                        tcg = tb * 4 + tc
                        eng.wait_ge(sro.h, B("ro") + tc + 1)
                        if tc >= 1:
                            eng.wait_ge(sre.h, B("re") + tc)
                        srx.inc(eng.tensor_reduce(out=rcol[:, 0:1], in_=ps_l[:, tc % 2, 0:8],
                                                  axis=AXX, op=ALU.max, negate=True))
                        srx.inc(eng.tensor_reduce(out=rcol[:, 1:2], in_=ps_l[:, tc % 2, 0:8],
                                                  axis=AXX, op=ALU.max))
                        eng.wait_ge(sre.h, B("re") + tc + 1)
                        srx.inc(eng.tensor_reduce(out=rcol[:, 2:3], in_=esb[:],
                                                  axis=AXX, op=ALU.add))
                        eng.wait_ge(srx.h, B("rx") + 8 * tc + 3)
                        srx.inc(eng.reciprocal(out=rcol[:, 3:4], in_=rcol[:, 2:3]))
                        eng.wait_ge(srx.h, B("rx") + 8 * tc + 4)
                        srx.inc(eng.tensor_scalar(out=scale_sb[:, tcg:tcg + 1],
                                                  in0=rcol[:, 3:4], scalar1=2.0,
                                                  scalar2=None, op0=ALU.mult))
                        srx.inc(eng.tensor_scalar(out=eqsb[:], in0=ps_l[:, tc % 2, 0:8],
                                                  scalar1=rcol[:, 1:2], op0=ALU.is_ge,
                                                  scalar2=None))
                        eng.wait_ge(srx.h, B("rx") + 8 * tc + 6)
                        srx.inc(eng.tensor_tensor(out=eqsb[:], in0=eqsb[:],
                                                  in1=bmi_sb[:], op=ALU.mult))
                        eng.wait_ge(srx.h, B("rx") + 8 * tc + 7)
                        srx.inc(eng.tensor_reduce(out=negidx_sb[:, tcg:tcg + 1],
                                                  in_=eqsb[:], axis=AXX, op=ALU.max))
                        eng.wait_ge(srx.h, B("rx") + 8 * tc + 8)
                        six.inc(eng.tensor_scalar(out=negidx_sb[:, tcg:tcg + 1],
                                                  in0=negidx_sb[:, tcg:tcg + 1],
                                                  scalar1=-64.0, scalar2=None,
                                                  op0=ALU.add))
                        if sparse:
                            IX = 12
                            bx = B("ix") + IX * tc
                            eng.wait_ge(six.h, bx + 1)
                            if tc >= 1:
                                eng.wait_ge(srk.h, B("rk") + 2 * tc - 1)
                            six.inc(eng.tensor_scalar(out=oh_sb[:], in0=negj_sb[:],
                                                      scalar1=negidx_sb[:, tcg:tcg + 1],
                                                      scalar2=None, op0=ALU.is_equal))
                            eng.wait_ge(srk.h, B("rk") + 2 * tc + 1)
                            six.inc(eng.tensor_tensor(out=t7a[:],
                                                      in0=ps_rk[:, tc % 2, 0:7],
                                                      in1=thr_sb[:], op=ALU.is_ge))
                            eng.wait_ge(six.h, bx + 3)
                            six.inc(eng.tensor_scalar(out=t7a[:], in0=t7a[:],
                                                      scalar1=-1.0, op0=ALU.mult,
                                                      scalar2=1.0, op1=ALU.add))
                            eng.wait_ge(six.h, bx + 4)
                            six.inc(eng.tensor_tensor(out=t7b[:], in0=oh_sb[:],
                                                      in1=t7a[:], op=ALU.mult))
                            eng.wait_ge(six.h, bx + 5)
                            six.inc(eng.tensor_tensor(out=t7a[:],
                                                      in0=ps_rk[:, tc % 2, 0:7],
                                                      in1=t7b[:], op=ALU.mult))
                            eng.wait_ge(six.h, bx + 6)
                            six.inc(eng.tensor_reduce(out=ccol[:, 0:1], in_=t7a[:],
                                                      axis=AXX, op=ALU.add))
                            eng.wait_ge(six.h, bx + 7)
                            six.inc(eng.tensor_reduce(out=ccol[:, 1:2], in_=t7b[:],
                                                      axis=AXX, op=ALU.add))
                            eng.wait_ge(six.h, bx + 8)
                            six.inc(eng.scalar_tensor_tensor(out=ccol[:, 2:3],
                                                             in0=ccol[:, 1:2],
                                                             scalar=-BIGSLOT,
                                                             in1=ccol[:, 0:1],
                                                             op0=ALU.mult, op1=ALU.add))
                            eng.wait_ge(six.h, bx + 9)
                            six.inc(eng.tensor_scalar(out=ccol[:, 2:3], in0=ccol[:, 2:3],
                                                      scalar1=BIGSLOT, scalar2=None,
                                                      op0=ALU.add))
                            eng.wait_ge(six.h, bx + 10)
                            six.inc(eng.tensor_copy(out=slot_f32[:, tcg:tcg + 1],
                                                    in_=ccol[:, 2:3]))
                            eng.wait_ge(srk.h, B("rk") + 2 * tc + 2)
                            eng.wait_ge(six.h, bx + 11)
                            six.inc(eng.tensor_tensor(out=offrow[:],
                                                      in0=offrow[:],
                                                      in1=ps_ct[0:1, tc % 2, 0:7],
                                                      op=ALU.add))

                @blk.scalar
                def _(eng, tb=tb, B=B):
                    for tc in range(4):
                        eng.wait_ge(smv.h, B("mv") + tc + 1)
                        if tc >= 1:
                            eng.wait_ge(srs.h, B("rs") + 2 * tc)
                        srs.inc(eng.activation(out=lnv[:], in_=mvv[:, 1:2], func=AF.Ln,
                                               bias=p.eps_col[:, 0:1], scale=1.0))
                        eng.wait_ge(srs.h, B("rs") + 2 * tc + 1)
                        eng.wait_ge(sln.h, B("ln") + tc)
                        srs.inc(eng.activation(out=rsc[:], in_=lnv[:], func=AF.Exp,
                                               scale=-0.5))
                    for tc in range(4):
                        eng.wait_ge(srx.h, B("rx") + 8 * tc + 1)
                        if tc >= 1:
                            eng.wait_ge(srx.h, B("rx") + 8 * (tc - 1) + 3)
                        sre.inc(eng.activation(out=esb[:], in_=ps_l[:, tc % 2, 0:8],
                                               func=AF.Exp, bias=rcol[:, 0:1],
                                               scale=1.0))

                @blk.tensor
                def _(eng, tb=tb, B=B):
                    for k in range(32):
                        tc, e = divmod(k, EC)
                        eng.wait_ge(sln.h, B("ln") + tc + 1)
                        if k >= 2:
                            eng.wait_ge(slt.h, B("lt") + k - 1)
                        stp.inc(eng.transpose(ps_t[:, k % 2, 0:128],
                                              ln_tm[:, tb * 4 + tc, e * 128:(e + 1) * 128],
                                              ident_sb[:]))
                    eng.wait_ge(slt.h, B("lt") + 32)
                    for tc in range(4):
                        tcg = tb * 4 + tc
                        if tc >= 2:
                            eng.wait_ge(srx.h, B("rx") + 8 * (tc - 1))
                        for e in range(EC):
                            mm = eng.matmul(ps_l[:, tc % 2, 0:8],
                                            lnT[:, e, tcg * 128:(tcg + 1) * 128],
                                            wr_sb[:, e, :],
                                            start=(e == 0), stop=(e == EC - 1))
                        sro.inc(mm)
                        if sparse:
                            eng.wait_ge(six.h, B("ix") + 12 * tc + 2)
                            if tc >= 2:
                                eng.wait_ge(six.h, B("ix") + 12 * (tc - 2) + 6)
                            eng.matmul(ps_rk[:, tc % 2, 0:7], ltri_sb[:], oh_sb[:],
                                       start=True, stop=False)
                            srk.inc(eng.matmul(ps_rk[:, tc % 2, 0:7],
                                               p.ones_f32[0:1, 0:128], offrow[:],
                                               start=False, stop=True))
                            srk.inc(eng.matmul(ps_ct[0:1, tc % 2, 0:7],
                                               ones_bf[:, 0:1], oh_sb[:],
                                               start=True, stop=True))

            assert srx.n - b[tag + "rx"] == 32 and sro.n - b[tag + "ro"] == 4
            if sparse:
                assert six.n - b[tag + "ix"] == 48 and srk.n - b[tag + "rk"] == 8

    if sparse:
        ssd = p.s(tag + "sd")
        ssd2 = p.s(tag + "pp")
        b_sd, b_sd2 = ssd.n, ssd2.n
        with ExitStack() as pc:
            ps_sd = pc.enter_context(nc.psum_tensor(_u("f_pssd"), [32, TB], F32))
            with nc.Block(no_gpsimd_drain=True) as blk:
                @blk.tensor
                def _(eng):
                    ssd2.inc(eng.transpose(ps_sd[0:32, 0:128], slot_f32[:],
                                           p.ident_f32[:]))

                @blk.vector
                def _(eng):
                    eng.wait_ge(ssd2.h, b_sd2 + 1)
                    ssd2.inc(eng.tensor_copy(out=slotT_sb[:],
                                             in_=ps_sd[0:32, 0:128]))

                @blk.gpsimd
                def _(eng):
                    eng.wait_ge(ssd2.h, b_sd2 + 2)
                    sd2 = D["slot_d"].rearrange("(a q) -> a q", q=128)
                    ssd.inc(eng.dma_start(out=sd2, in_=slotT_sb[:]), 16)
                    eng.wait_ge(ssd.h, b_sd + 16)

    cb.close()

    js = ([NEXP] + list(range(NEXP))) if sparse else list(range(NEXP + 1))
    for j in js:
        shared = (j == NEXP)
        if sparse and not shared:
            _emit_expert_sparse(p, D, tag, j, ln_tm, slot_f32, scale_sb, srowb,
                                iotac_sb, iota128_sb, x3, xd3,
                                final and (j == NEXP - 1))
            continue
        wu_src = D["wus"] if shared else D["wu"][j]
        wg_src = D["wgs"] if shared else D["wg"][j]
        wd_src = D["wds"] if shared else D["wd"][j]

        c1 = ExitStack()
        E1 = c1.enter_context
        wustg = E1(nc.sbuf_tensor(_u("f_wustg"), [128, 2, EC, 128], F32))
        wgstg = E1(nc.sbuf_tensor(_u("f_wgstg"), [128, 2, EC, 128], F32))
        wub = E1(nc.sbuf_tensor(_u("f_wub"), [128, 2, EC, 128], BF16))
        wgb = E1(nc.sbuf_tensor(_u("f_wgb"), [128, 2, EC, 128], BF16))
        gsil = E1(nc.sbuf_tensor(_u("f_gsil"), [128, 2, TB], BF16))
        hbf = E1(nc.sbuf_tensor(_u("f_hbf"), [128, 2, TB], BF16))
        s_l0 = p.s(tag + "l0"); s_l1 = p.s(tag + "l1")
        s_cs = p.s(tag + "cs"); s_mu = p.s(tag + "mu"); s_mg = p.s(tag + "mg")
        s_si = p.s(tag + "si"); s_hb = p.s(tag + "hb")
        s_h0 = p.s(tag + "h0"); s_h1 = p.s(tag + "h1")
        b = p.base(*(tag + nm for nm in ("l0", "l1", "cs", "mu", "mg", "si",
                                         "hb", "h0", "h1")))

        def B(nm, b=b):
            return b[tag + nm]

        with ExitStack() as pc:
            ps_u = pc.enter_context(nc.psum_tensor(_u("f_psu"), [128, 2, TB], F32))
            ps_g = pc.enter_context(nc.psum_tensor(_u("f_psg"), [128, 2, TB], F32))
            with nc.Block(no_gpsimd_drain=True) as blk:
                @blk.sync
                def _(eng, B=B):
                    sl = (s_l0, s_l1)
                    wu3 = wu_src.rearrange("(c k) h -> k c h", k=128)
                    wg3 = wg_src.rearrange("(c k) h -> k c h", k=128)
                    for hc in range(HCN):
                        if hc >= 2:
                            eng.wait_ge(s_cs.h, B("cs") + 2 * (hc - 1))
                        sl[hc % 2].inc(eng.dma_start(
                            out=wustg[:, hc % 2, :, :],
                            in_=wu3[:, :, hc * 128:(hc + 1) * 128]), 16)
                        sl[hc % 2].inc(eng.dma_start(
                            out=wgstg[:, hc % 2, :, :],
                            in_=wg3[:, :, hc * 128:(hc + 1) * 128]), 16)

                @blk.vector
                def _(eng, B=B):
                    sl = (s_l0, s_l1)
                    for hc in range(HCN):
                        eng.wait_ge(sl[hc % 2].h, B("l%d" % (hc % 2)) + 32 * (hc // 2 + 1))
                        if hc >= 2:
                            eng.wait_ge(s_mg.h, B("mg") + 8 * (hc - 1))
                        s_cs.inc(eng.tensor_copy(out=wub[:, hc % 2, :, :],
                                                  in_=wustg[:, hc % 2, :, :]))
                        s_cs.inc(eng.tensor_copy(out=wgb[:, hc % 2, :, :],
                                                 in_=wgstg[:, hc % 2, :, :]))
                        for tb in range(NTB):
                            g = hc * 8 + tb
                            eng.wait_ge(s_mu.h, B("mu") + g + 1)
                            eng.wait_ge(s_si.h, B("si") + g + 1)
                            if g >= 2:
                                eng.wait_ge(s_h0.h if g % 2 == 0 else s_h1.h,
                                            B("h%d" % (g % 2)) + 16 * (g // 2))
                            s_hb.inc(eng.tensor_tensor(out=hbf[:, g % 2, :],
                                                       in0=ps_u[:, tb % 2, :],
                                                       in1=gsil[:, g % 2, :],
                                                       op=ALU.mult))

                @blk.scalar
                def _(eng, B=B):
                    for hc in range(HCN):
                        for tb in range(NTB):
                            g = hc * 8 + tb
                            eng.wait_ge(s_mg.h, B("mg") + g + 1)
                            if g >= 2:
                                eng.wait_ge(s_hb.h, B("hb") + g - 1)
                            s_si.inc(eng.activation(out=gsil[:, g % 2, :],
                                                    in_=ps_g[:, tb % 2, :],
                                                    func=AF.Silu))

                @blk.tensor
                def _(eng, B=B):
                    for hc in range(HCN):
                        eng.wait_ge(s_cs.h, B("cs") + 2 * (hc + 1))
                        for tb in range(NTB):
                            g = hc * 8 + tb
                            if g >= 2:
                                eng.wait_ge(s_hb.h, B("hb") + g - 1)
                            for e in range(EC):
                                mm = eng.matmul(ps_u[:, tb % 2, :],
                                                wub[:, hc % 2, e, :],
                                                lnT[:, e, tb * TB:(tb + 1) * TB],
                                                start=(e == 0), stop=(e == EC - 1))
                            s_mu.inc(mm)
                            if g >= 2:
                                eng.wait_ge(s_si.h, B("si") + g - 1)
                            for e in range(EC):
                                mm = eng.matmul(ps_g[:, tb % 2, :],
                                                wgb[:, hc % 2, e, :],
                                                lnT[:, e, tb * TB:(tb + 1) * TB],
                                                start=(e == 0), stop=(e == EC - 1))
                            s_mg.inc(mm)

                @blk.gpsimd
                def _(eng, B=B):
                    sh = (s_h0, s_h1)
                    for hc in range(HCN):
                        for tb in range(NTB):
                            g = hc * 8 + tb
                            eng.wait_ge(s_hb.h, B("hb") + g + 1)
                            sh[g % 2].inc(eng.dma_start(
                                out=D["hid_d"][hc, :, tb * TB:(tb + 1) * TB],
                                in_=hbf[:, g % 2, :]), 16)
                    eng.wait_ge(s_h0.h, B("h0") + 16 * 128)
                    eng.wait_ge(s_h1.h, B("h1") + 16 * 128)

        c1.close()
        if sparse and shared:
            c_ln.close()
        assert s_mu.n - b[tag + "mu"] == 256 and s_hb.n - b[tag + "hb"] == 256

        c2 = ExitStack()
        E2 = c2.enter_context
        wdstg = E2(nc.sbuf_tensor(_u("f_wdstg"), [128, 2, 2, E], F32))
        wd_sb = E2(nc.sbuf_tensor(_u("f_wdsb"), [128, HCN, E], BF16))
        hid_sb = E2(nc.sbuf_tensor(_u("f_hid"), [128, HCN, TB], BF16))
        xe_sb = E2(nc.sbuf_tensor(_u("f_xe"), [128, 2, E], F32))
        xo_sb = E2(nc.sbuf_tensor(_u("f_xo"), [128, 2, E], F32))
        s_w0 = p.s(tag + "w0"); s_w1 = p.s(tag + "w1"); s_wc = p.s(tag + "wc")
        s_hl = p.s(tag + "hl"); s_md = p.s(tag + "md"); s_sj = p.s(tag + "sj")
        s_xa = p.s(tag + "xa")
        s_e0 = p.s(tag + "e0"); s_e1 = p.s(tag + "e1")
        s_o0 = p.s(tag + "o0"); s_o1 = p.s(tag + "o1")
        b = p.base(*(tag + nm for nm in ("w0", "w1", "wc", "hl", "md", "sj",
                                         "xa", "e0", "e1", "o0", "o1")))

        def B(nm, b=b):
            return b[tag + nm]

        with ExitStack() as pc:
            ps_d = pc.enter_context(nc.psum_tensor(_u("f_psd"), [128, 4, TB], F32))
            with nc.Block(no_gpsimd_drain=True) as blk:
                @blk.sync
                def _(eng, B=B):
                    swp = (s_w0, s_w1)
                    wd3 = wd_src.rearrange("(c k) e -> k c e", k=128)
                    for r in range(16):
                        if r >= 2:
                            eng.wait_ge(s_wc.h, B("wc") + r - 1)
                        swp[r % 2].inc(eng.dma_start(out=wdstg[:, r % 2, :, :],
                                                     in_=wd3[:, r * 2:r * 2 + 2, :]), 16)
                    for tb in range(NTB):
                        if tb >= 1:
                            eng.wait_ge(s_md.h, B("md") + 4 * tb)
                        for hc in range(HCN):
                            s_hl.inc(eng.dma_start(
                                out=hid_sb[:, hc, :],
                                in_=D["hid_d"][hc, :, tb * TB:(tb + 1) * TB]), 16)

                @blk.vector
                def _(eng, B=B):
                    swp = (s_w0, s_w1)
                    for r in range(16):
                        eng.wait_ge(swp[r % 2].h, B("w%d" % (r % 2)) + 16 * (r // 2 + 1))
                        s_wc.inc(eng.tensor_copy(out=wd_sb[:, r * 2:r * 2 + 2, :],
                                                 in_=wdstg[:, r % 2, :, :]))
                    if not shared:
                        s_sj.inc(eng.tensor_scalar(out=sj_sb[:], in0=negidx_sb[:],
                                                   scalar1=float(-j), scalar2=None,
                                                   op0=ALU.is_equal))
                        eng.wait_ge(s_sj.h, B("sj") + 1)
                        s_sj.inc(eng.tensor_tensor(out=sj_sb[:], in0=sj_sb[:],
                                                   in1=scale_sb[:], op=ALU.mult))
                    sxl = (s_e0, s_e1); sxs = (s_o0, s_o1)
                    for tb in range(NTB):
                        for tc in range(4):
                            gt = tb * 4 + tc
                            tcg = gt
                            eng.wait_ge(s_md.h, B("md") + gt + 1)
                            eng.wait_ge(sxl[gt % 2].h, B("e%d" % (gt % 2)) + 16 * (gt // 2 + 1))
                            if gt >= 2:
                                eng.wait_ge(sxs[gt % 2].h, B("o%d" % (gt % 2)) + 16 * (gt // 2))
                            for half in range(2):
                                slot = (tc % 2) * 2 + half
                                if shared:
                                    s_xa.inc(eng.tensor_tensor(
                                        out=xo_sb[:, gt % 2, half * TB:(half + 1) * TB],
                                        in0=xe_sb[:, gt % 2, half * TB:(half + 1) * TB],
                                        in1=ps_d[:, slot, :], op=ALU.add))
                                else:
                                    s_xa.inc(eng.scalar_tensor_tensor(
                                        out=xo_sb[:, gt % 2, half * TB:(half + 1) * TB],
                                        in0=ps_d[:, slot, :],
                                        scalar=sj_sb[:, tcg:tcg + 1],
                                        in1=xe_sb[:, gt % 2, half * TB:(half + 1) * TB],
                                        op0=ALU.mult, op1=ALU.add))

                @blk.tensor
                def _(eng, B=B):
                    eng.wait_ge(s_wc.h, B("wc") + 16)
                    if not shared:
                        eng.wait_ge(s_sj.h, B("sj") + 2)
                    for tb in range(NTB):
                        eng.wait_ge(s_hl.h, B("hl") + 16 * HCN * (tb + 1))
                        for tc in range(4):
                            gt = tb * 4 + tc
                            if gt >= 2:
                                eng.wait_ge(s_xa.h, B("xa") + 2 * (gt - 1))
                            for half in range(2):
                                slot = (tc % 2) * 2 + half
                                for hc in range(HCN):
                                    mm = eng.matmul(ps_d[:, slot, :],
                                                    hid_sb[:, hc, tc * 128:(tc + 1) * 128],
                                                    wd_sb[:, hc, half * TB:(half + 1) * TB],
                                                    start=(hc == 0), stop=(hc == HCN - 1))
                            s_md.inc(mm)

                @blk.scalar
                def _(eng, B=B):
                    sxl = (s_e0, s_e1)
                    for tb in range(NTB):
                        for tc in range(4):
                            gt = tb * 4 + tc
                            if gt >= 2:
                                eng.wait_ge(s_xa.h, B("xa") + 2 * (gt - 1))
                            sxl[gt % 2].inc(eng.dma_start(out=xe_sb[:, gt % 2, :],
                                                          in_=x3[gt]), 16)

                @blk.gpsimd
                def _(eng, B=B):
                    sxs = (s_o0, s_o1)
                    dst = xd3 if (shared and not sparse) else x3
                    for tb in range(NTB):
                        for tc in range(4):
                            gt = tb * 4 + tc
                            eng.wait_ge(s_xa.h, B("xa") + 2 * (gt + 1))
                            sxs[gt % 2].inc(eng.dma_start(out=dst[gt],
                                                          in_=xo_sb[:, gt % 2, :]), 16)
                    eng.wait_ge(s_o0.h, B("o0") + 16 * 16)
                    eng.wait_ge(s_o1.h, B("o1") + 16 * 16)

        c2.close()
        assert s_md.n - b[tag + "md"] == 32 and s_xa.n - b[tag + "xa"] == 64

    if not sparse:
        c_ln.close()
    c.close()


def _emit_expert_sparse(p, D, tag, j, ln_tm, slot_f32, scale_sb, srowb,
                        iotac_sb, iota128_sb, x3, xd3, final):
    nc = p.nc
    ones_f32 = p.ones_f32
    base = float(j * CAP)
    NTC = CAP // 128
    NB2 = CAP // TB

    ce = ExitStack()
    EE = ce.enter_context
    elnT = EE(nc.sbuf_tensor(_u("fs_elnT"), [128, EC, CAP], BF16))
    hid_sb = EE(nc.sbuf_tensor(_u("fs_hid"), [128, HCN, CAP], BF16))
    eout_sb = EE(nc.sbuf_tensor(_u("fs_eout"), [128, NTC, E], BF16))

    s_ohb = p.s(tag + "ohb"); s_gm = p.s(tag + "gm"); s_ge = p.s(tag + "ge")
    b = p.base(tag + "ohb", tag + "gm", tag + "ge")

    def B1_(nm, b=b):
        return b[tag + nm]

    c1 = ExitStack()
    oh_t = c1.enter_context(nc.sbuf_tensor(_u("fs_oh"), [128, 2, TB], BF16))
    with ExitStack() as pc:
        ps_gi = pc.enter_context(nc.psum_tensor(_u("fs_psgi"), [128, 8, TB], F32))
        with nc.Block(no_gpsimd_drain=True) as blk:
            @blk.vector
            def _(eng, B=B1_):
                for sh in range(2):
                    for tt in range(32):
                        k = sh * 32 + tt
                        if k >= 2:
                            eng.wait_ge(s_gm.h, B("gm") + k - 1)
                        s_ohb.inc(eng.tensor_scalar(
                            out=oh_t[:, k % 2, :],
                            in0=iotac_sb[:, sh * TB:(sh + 1) * TB],
                            scalar1=slot_f32[:, tt:tt + 1], op0=ALU.subtract,
                            scalar2=base, op1=ALU.add))
                        eng.wait_ge(s_ohb.h, B("ohb") + 2 * k + 1)
                        s_ohb.inc(eng.tensor_scalar(
                            out=oh_t[:, k % 2, :], in0=oh_t[:, k % 2, :],
                            scalar1=0.0, scalar2=None, op0=ALU.is_equal))
                    eng.wait_ge(s_gm.h, B("gm") + 32 * (sh + 1))
                    for ec in range(EC):
                        s_ge.inc(eng.tensor_copy(
                            out=elnT[:, ec, sh * TB:(sh + 1) * TB],
                            in_=ps_gi[:, ec, :]))

            @blk.tensor
            def _(eng, B=B1_):
                for sh in range(2):
                    if sh == 1:
                        eng.wait_ge(s_ge.h, B("ge") + EC)
                    for tt in range(32):
                        k = sh * 32 + tt
                        eng.wait_ge(s_ohb.h, B("ohb") + 2 * k + 2)
                        for ec in range(EC):
                            mm = eng.matmul(ps_gi[:, ec, :],
                                            ln_tm[:, tt, ec * 128:(ec + 1) * 128],
                                            oh_t[:, k % 2, :],
                                            start=(tt == 0), stop=(tt == 31))
                        s_gm.inc(mm)

    c1.close()
    if os.environ.get("DBG2"):
        sdb = p.s(tag + "xs0")
        with nc.Block(no_gpsimd_drain=True) as blk:
            @blk.gpsimd
            def _(eng, j=j):
                for ec in range(EC):
                    sdb.inc(eng.dma_start(
                        out=D["eln_dbg"][j, ec * 128:(ec + 1) * 128, :],
                        in_=elnT[:, ec, :]), 16)
                eng.wait_ge(sdb.h, sdb.n)
    assert s_gm.n - b[tag + "gm"] == 64 and s_ge.n - b[tag + "ge"] == 16

    c2 = ExitStack()
    E2_ = c2.enter_context
    wustg = E2_(nc.sbuf_tensor(_u("fs_wustg"), [128, 2, EC, 128], F32))
    wgstg = E2_(nc.sbuf_tensor(_u("fs_wgstg"), [128, 2, EC, 128], F32))
    wub = E2_(nc.sbuf_tensor(_u("fs_wub"), [128, 2, EC, 128], BF16))
    wgb = E2_(nc.sbuf_tensor(_u("fs_wgb"), [128, 2, EC, 128], BF16))
    gsil = E2_(nc.sbuf_tensor(_u("fs_gsil"), [128, 2, TB], BF16))
    s_l0 = p.s(tag + "l0"); s_l1 = p.s(tag + "l1")
    s_cs = p.s(tag + "cs"); s_mu = p.s(tag + "mu"); s_mg = p.s(tag + "mg")
    s_si = p.s(tag + "si"); s_hb = p.s(tag + "hb")
    b = p.base(*(tag + nm for nm in ("l0", "l1", "cs", "mu", "mg", "si", "hb")))

    def B2_(nm, b=b):
        return b[tag + nm]

    wu_src = D["wu"][j]
    wg_src = D["wg"][j]
    with ExitStack() as pc:
        ps_u = pc.enter_context(nc.psum_tensor(_u("fs_psu"), [128, 2, TB], F32))
        ps_g = pc.enter_context(nc.psum_tensor(_u("fs_psg"), [128, 2, TB], F32))
        with nc.Block(no_gpsimd_drain=True) as blk:
            @blk.sync
            def _(eng, B=B2_):
                sl = (s_l0, s_l1)
                wu3 = wu_src.rearrange("(c k) h -> k c h", k=128)
                wg3 = wg_src.rearrange("(c k) h -> k c h", k=128)
                for hc in range(HCN):
                    if hc >= 2:
                        eng.wait_ge(s_cs.h, B("cs") + 2 * (hc - 1))
                    sl[hc % 2].inc(eng.dma_start(
                        out=wustg[:, hc % 2, :, :],
                        in_=wu3[:, :, hc * 128:(hc + 1) * 128]), 16)
                    sl[hc % 2].inc(eng.dma_start(
                        out=wgstg[:, hc % 2, :, :],
                        in_=wg3[:, :, hc * 128:(hc + 1) * 128]), 16)

            @blk.vector
            def _(eng, B=B2_):
                sl = (s_l0, s_l1)
                for hc in range(HCN):
                    eng.wait_ge(sl[hc % 2].h, B("l%d" % (hc % 2)) + 32 * (hc // 2 + 1))
                    if hc >= 2:
                        eng.wait_ge(s_mg.h, B("mg") + NB2 * (hc - 1))
                    s_cs.inc(eng.tensor_copy(out=wub[:, hc % 2, :, :],
                                             in_=wustg[:, hc % 2, :, :]))
                    s_cs.inc(eng.tensor_copy(out=wgb[:, hc % 2, :, :],
                                             in_=wgstg[:, hc % 2, :, :]))
                    for tblk in range(NB2):
                        g = hc * NB2 + tblk
                        eng.wait_ge(s_mu.h, B("mu") + g + 1)
                        eng.wait_ge(s_si.h, B("si") + g + 1)
                        s_hb.inc(eng.tensor_tensor(
                            out=hid_sb[:, hc, tblk * TB:(tblk + 1) * TB],
                            in0=ps_u[:, tblk % 2, :],
                            in1=gsil[:, g % 2, :], op=ALU.mult))

            @blk.scalar
            def _(eng, B=B2_):
                for hc in range(HCN):
                    for tblk in range(NB2):
                        g = hc * NB2 + tblk
                        eng.wait_ge(s_mg.h, B("mg") + g + 1)
                        if g >= 2:
                            eng.wait_ge(s_hb.h, B("hb") + g - 1)
                        s_si.inc(eng.activation(out=gsil[:, g % 2, :],
                                                in_=ps_g[:, tblk % 2, :],
                                                func=AF.Silu))

            @blk.tensor
            def _(eng, B=B2_):
                for hc in range(HCN):
                    eng.wait_ge(s_cs.h, B("cs") + 2 * (hc + 1))
                    for tblk in range(NB2):
                        g = hc * NB2 + tblk
                        if g >= 2:
                            eng.wait_ge(s_hb.h, B("hb") + g - 1)
                        for e in range(EC):
                            mm = eng.matmul(ps_u[:, tblk % 2, :],
                                            wub[:, hc % 2, e, :],
                                            elnT[:, e, tblk * TB:(tblk + 1) * TB],
                                            start=(e == 0), stop=(e == EC - 1))
                        s_mu.inc(mm)
                        if g >= 2:
                            eng.wait_ge(s_si.h, B("si") + g - 1)
                        for e in range(EC):
                            mm = eng.matmul(ps_g[:, tblk % 2, :],
                                            wgb[:, hc % 2, e, :],
                                            elnT[:, e, tblk * TB:(tblk + 1) * TB],
                                            start=(e == 0), stop=(e == EC - 1))
                        s_mg.inc(mm)

    c2.close()

    c3 = ExitStack()
    E3_ = c3.enter_context
    wdstg = E3_(nc.sbuf_tensor(_u("fs_wdstg"), [128, 2, TB], F32))
    wdb = E3_(nc.sbuf_tensor(_u("fs_wdb"), [128, 2, TB], BF16))
    s_w0 = p.s(tag + "w0"); s_w1 = p.s(tag + "w1"); s_wc = p.s(tag + "wc")
    s_md = p.s(tag + "md"); s_ev = p.s(tag + "ev2")
    b = p.base(*(tag + nm for nm in ("w0", "w1", "wc", "md", "ev2")))

    def B3_(nm, b=b):
        return b[tag + nm]

    wd_src = D["wd"][j]
    with ExitStack() as pc:
        ps_d = pc.enter_context(nc.psum_tensor(_u("fs_psd"), [128, 8, TB], F32))
        with nc.Block(no_gpsimd_drain=True) as blk:
            @blk.sync
            def _(eng, B=B3_):
                swp = (s_w0, s_w1)
                wd3 = wd_src.rearrange("(c k) e -> k c e", k=128)
                for r in range(64):
                    eh, hc = divmod(r, HCN)
                    if r >= 2:
                        eng.wait_ge(s_wc.h, B("wc") + r - 1)
                    swp[r % 2].inc(eng.dma_start(
                        out=wdstg[:, r % 2, :],
                        in_=wd3[:, hc, eh * TB:(eh + 1) * TB]), 16)

            @blk.vector
            def _(eng, B=B3_):
                swp = (s_w0, s_w1)
                for eh in range(2):
                    for hc in range(HCN):
                        r = eh * HCN + hc
                        eng.wait_ge(swp[r % 2].h, B("w%d" % (r % 2)) + 16 * (r // 2 + 1))
                        if r >= 2:
                            eng.wait_ge(s_md.h, B("md") + r - 1)
                        s_wc.inc(eng.tensor_copy(out=wdb[:, r % 2, :],
                                                 in_=wdstg[:, r % 2, :]))
                    eng.wait_ge(s_md.h, B("md") + 32 * (eh + 1))
                    for tc in range(NTC):
                        s_ev.inc(eng.tensor_copy(
                            out=eout_sb[:, tc, eh * TB:(eh + 1) * TB],
                            in_=ps_d[:, tc, :]))

            @blk.tensor
            def _(eng, B=B3_):
                for eh in range(2):
                    if eh == 1:
                        eng.wait_ge(s_ev.h, B("ev2") + NTC)
                    for hc in range(HCN):
                        r = eh * HCN + hc
                        eng.wait_ge(s_wc.h, B("wc") + r + 1)
                        for tc in range(NTC):
                            mm = eng.matmul(ps_d[:, tc, :],
                                            hid_sb[:, hc, tc * 128:(tc + 1) * 128],
                                            wdb[:, r % 2, :],
                                            start=(hc == 0), stop=(hc == HCN - 1))
                        s_md.inc(mm)

    c3.close()
    if os.environ.get("DBG2"):
        sdb = p.s(tag + "xs0")
        with nc.Block(no_gpsimd_drain=True) as blk:
            @blk.gpsimd
            def _(eng, j=j):
                for tc in range(NTC):
                    sdb.inc(eng.dma_start(
                        out=D["eout_dbg"][j, tc * 128:(tc + 1) * 128, :],
                        in_=eout_sb[:, tc, :]), 16)
                eng.wait_ge(sdb.h, sdb.n)
    assert s_md.n - b[tag + "md"] == 64 and s_ev.n - b[tag + "ev2"] == 16

    c4 = ExitStack()
    E4_ = c4.enter_context
    oht = E4_(nc.sbuf_tensor(_u("fs_oht"), [128, 4, 128], BF16))
    xe_sb = E4_(nc.sbuf_tensor(_u("fs_xe"), [128, 2, E], F32))
    xo_sb = E4_(nc.sbuf_tensor(_u("fs_xo"), [128, 2, E], F32))
    s_sr = p.s(tag + "sr"); s_sbc = p.s(tag + "sbc"); s_oht = p.s(tag + "oht")
    s_om = p.s(tag + "om"); s_xa = p.s(tag + "xa")
    s_e0 = p.s(tag + "e0"); s_e1 = p.s(tag + "e1")
    s_o0 = p.s(tag + "o0"); s_o1 = p.s(tag + "o1")
    b = p.base(*(tag + nm for nm in ("sr", "sbc", "oht", "om", "xa",
                                     "e0", "e1", "o0", "o1")))

    def B4_(nm, b=b):
        return b[tag + nm]

    dst = xd3 if final else x3
    with ExitStack() as pc:
        ps_sb = pc.enter_context(nc.psum_tensor(_u("fs_pssb"), [128, 2, TB], F32))
        ps_o = pc.enter_context(nc.psum_tensor(_u("fs_pso"), [128, 4, TB], F32))
        with nc.Block(no_gpsimd_drain=True) as blk:
            @blk.sync
            def _(eng, B=B4_):
                for tb in range(8):
                    if tb >= 1:
                        eng.wait_ge(s_sbc.h, B("sbc") + 4 * tb)
                    s_sr.inc(eng.dma_start(out=srowb[:],
                                           in_=D["slot_d"][tb * TB:(tb + 1) * TB].unsqueeze(0)), 16)

            @blk.tensor
            def _(eng, B=B4_):
                for tt in range(32):
                    eng.wait_ge(s_sr.h, B("sr") + 16 * (tt // 4 + 1))
                    if tt >= 2:
                        eng.wait_ge(s_oht.h, B("oht") + 8 * (tt - 1))
                    s_sbc.inc(eng.matmul(ps_sb[:, tt % 2, 0:128],
                                         ones_f32[0:1, 0:128],
                                         srowb[0:1, (tt % 4) * 128:(tt % 4 + 1) * 128],
                                         start=True, stop=True))
                    if tt >= 2:
                        eng.wait_ge(s_xa.h, B("xa") + 2 * (tt - 1))
                    for sc in range(NTC):
                        q = tt * 8 + sc
                        eng.wait_ge(s_oht.h, B("oht") + q + 1)
                        for eh in range(2):
                            mm = eng.matmul(ps_o[:, (tt % 2) * 2 + eh, :],
                                            oht[:, q % 4, :],
                                            eout_sb[:, sc, eh * TB:(eh + 1) * TB],
                                            start=(sc == 0), stop=(sc == NTC - 1))
                        s_om.inc(mm)

            @blk.vector
            def _(eng, B=B4_):
                se = (s_e0, s_e1)
                so = (s_o0, s_o1)
                for tt in range(32):
                    eng.wait_ge(s_sbc.h, B("sbc") + tt + 1)
                    for sc in range(NTC):
                        q = tt * 8 + sc
                        if q >= 4:
                            eng.wait_ge(s_om.h, B("om") + q - 3)
                        s_oht.inc(eng.tensor_scalar(
                            out=oht[:, q % 4, :], in0=ps_sb[:, tt % 2, 0:128],
                            scalar1=iota128_sb[:, 0:1], op0=ALU.subtract,
                            scalar2=base + sc * 128.0, op1=ALU.is_equal))
                    eng.wait_ge(s_om.h, B("om") + 8 * (tt + 1))
                    eng.wait_ge(se[tt % 2].h, B("e%d" % (tt % 2)) + 16 * (tt // 2 + 1))
                    if tt >= 2:
                        eng.wait_ge(so[tt % 2].h, B("o%d" % (tt % 2)) + 16 * (tt // 2))
                    for eh in range(2):
                        s_xa.inc(eng.scalar_tensor_tensor(
                            out=xo_sb[:, tt % 2, eh * TB:(eh + 1) * TB],
                            in0=ps_o[:, (tt % 2) * 2 + eh, :],
                            scalar=scale_sb[:, tt:tt + 1],
                            in1=xe_sb[:, tt % 2, eh * TB:(eh + 1) * TB],
                            op0=ALU.mult, op1=ALU.add))

            @blk.scalar
            def _(eng, B=B4_):
                se = (s_e0, s_e1)
                for tt in range(32):
                    if tt >= 2:
                        eng.wait_ge(s_xa.h, B("xa") + 2 * (tt - 1))
                    se[tt % 2].inc(eng.dma_start(out=xe_sb[:, tt % 2, :],
                                                 in_=x3[tt]), 16)

            @blk.gpsimd
            def _(eng, B=B4_):
                so = (s_o0, s_o1)
                for tt in range(32):
                    eng.wait_ge(s_xa.h, B("xa") + 2 * (tt + 1))
                    so[tt % 2].inc(eng.dma_start(out=dst[tt],
                                                 in_=xo_sb[:, tt % 2, :]), 16)
                eng.wait_ge(s_o0.h, s_o0.n)
                eng.wait_ge(s_o1.h, s_o1.n)

    c4.close()
    assert s_oht.n - b[tag + "oht"] == 256 and s_om.n - b[tag + "om"] == 256
    assert s_xa.n - b[tag + "xa"] == 64
    ce.close()




def emit_attn_b(p, x_src, x_dst, final):
    nc, D = p.nc, p.D
    ident_sb, ones_bf, ones_f32 = p.ident_sb, p.ones_bf, p.ones_f32
    tag = "f_"

    c = ExitStack()
    EN = c.enter_context
    win_sb = EN(nc.sbuf_tensor(_u("c_win"), [128, EC, HID], BF16))
    wout_sb = EN(nc.sbuf_tensor(_u("c_wout"), [128, 16, E], BF16))
    wssm_sb = EN(nc.sbuf_tensor(_u("c_wssm"), [128, 16, 32], BF16))
    convw_sb = EN(nc.sbuf_tensor(_u("c_convw"), [128, 16, 4], F32))
    convb_sb = EN(nc.sbuf_tensor(_u("c_convb"), [128, 16], F32))
    dv_sb = EN(nc.sbuf_tensor(_u("c_dv"), [128, 16], F32))
    bout_sb = EN(nc.sbuf_tensor(_u("c_bout"), [1, E], BF16))
    boutf_sb = EN(nc.sbuf_tensor(_u("c_boutf"), [1, E], F32))
    amat_sb = EN(nc.sbuf_tensor(_u("c_amat"), [32, 32], F32))
    amatT_sb = EN(nc.sbuf_tensor(_u("c_amatT"), [32, 32], F32))
    bvc_sb = EN(nc.sbuf_tensor(_u("c_bvc"), [32, 2], F32))
    bssm_sb = EN(nc.sbuf_tensor(_u("c_bssm"), [32, 1], F32))
    binr_sb = EN(nc.sbuf_tensor(_u("c_binr"), [1, HID], BF16))
    binrf_sb = EN(nc.sbuf_tensor(_u("c_binrf"), [1, TB], F32))
    pw_M = EN(nc.sbuf_tensor(_u("c_pwM"), [32, 10, 32], F32))
    pw_L = EN(nc.sbuf_tensor(_u("c_pwL"), [32, 10, 32], F32))
    dcols = EN(nc.sbuf_tensor(_u("c_dcols"), [32, TB], F32))
    hc_sb = EN(nc.sbuf_tensor(_u("c_hc"), [32, 2], F32))
    g_sb = EN(nc.sbuf_tensor(_u("c_g"), [32, TB], F32))
    srow_sb = EN(nc.sbuf_tensor(_u("c_srow"), [1, TB], F32))
    halo = EN(nc.sbuf_tensor(_u("c_halo"), [128, 16, 3], BF16))
    stats_sb = EN(nc.sbuf_tensor(_u("c_stats"), [128, 4, 2], F32))

    x3 = x_src.rearrange("(n q) e -> n q e", q=128)
    xd3 = x_dst.rearrange("(n q) e -> n q e", q=128)

    cs = ExitStack()
    wstg = cs.enter_context(nc.sbuf_tensor(_u("c_wstg"), [128, 2, EC * TB], F32))
    swp = (p.s(tag + "l0"), p.s(tag + "l1"))
    swc = p.s(tag + "wc")
    b0 = p.base(tag + "l0", tag + "l1", tag + "wc")
    with nc.Block(no_gpsimd_drain=True) as blk:
        @blk.sync
        def _(eng):
            w3 = D["winT"].rearrange("(c k) h -> k c h", k=128)
            wo3 = D["woutT"].rearrange("(c k) e -> k c e", k=128)
            for q in range(12):
                if q >= 2:
                    eng.wait_ge(swc.h, b0[tag + "wc"] + q - 1)
                if q < EC:
                    swp[q % 2].inc(eng.dma_start(
                        out=wstg[:, q % 2, :].rearrange("k (c t) -> k c t", c=EC),
                        in_=w3[:, :, q * TB:(q + 1) * TB]), 16)
                else:
                    r = q - EC
                    swp[q % 2].inc(eng.dma_start(
                        out=wstg[:, q % 2, :].rearrange("k (c t) -> k c t", c=4),
                        in_=wo3[:, r * 4:(r + 1) * 4, :]), 16)

        @blk.vector
        def _(eng):
            for q in range(12):
                eng.wait_ge(swp[q % 2].h, b0[tag + "l%d" % (q % 2)] + 16 * (q // 2 + 1))
                if q < EC:
                    swc.inc(eng.tensor_copy(
                        out=win_sb[:, :, q * TB:(q + 1) * TB],
                        in_=wstg[:, q % 2, :].rearrange("k (c t) -> k c t", c=EC)))
                else:
                    r = q - EC
                    swc.inc(eng.tensor_copy(
                        out=wout_sb[:, r * 4:(r + 1) * 4, :],
                        in_=wstg[:, q % 2, :].rearrange("k (c t) -> k c t", c=4)))

    cs.close()

    s_sm = p.s(tag + "sm"); s_sc = p.s(tag + "sc")
    s_pp = p.s(tag + "pp"); s_pe = p.s(tag + "pe")
    b1 = p.base(tag + "sm", tag + "sc", tag + "pp", tag + "pe")
    cs2 = ExitStack()
    smstg = cs2.enter_context(nc.sbuf_tensor(_u("c_smstg"), [128, 16, 32], F32))
    with ExitStack() as pc:
        ps_p = pc.enter_context(nc.psum_tensor(_u("c_psp"), [32, 3, TB], F32))
        with nc.Block(no_gpsimd_drain=True) as blk:
            @blk.sync
            def _(eng):
                ws3 = D["wssmT"].rearrange("(c k) n -> k c n", k=128)
                s_sm.inc(eng.dma_start(out=smstg[:], in_=ws3[:]), 16)
                s_sm.inc(eng.dma_start(out=convw_sb[:], in_=D["convw_fm"][:]), 16)
                s_sm.inc(eng.dma_start(out=convb_sb[:], in_=D["convb_fm"][:]), 16)
                s_sm.inc(eng.dma_start(out=dv_sb[:], in_=D["Dv_fm"][:]), 16)
                s_sm.inc(eng.dma_start(out=boutf_sb[:], in_=D["b_out"].unsqueeze(0)), 16)
                s_sm.inc(eng.dma_start(out=amat_sb[:], in_=D["Amat"][:]), 16)
                s_sm.inc(eng.dma_start(out=amatT_sb[:], in_=D["AmatT"][:]), 16)
                s_sm.inc(eng.dma_start(out=bvc_sb[:], in_=D["BvC"][:]), 16)
                s_sm.inc(eng.dma_start(out=bssm_sb[:], in_=D["b_ssm"].unsqueeze(1)), 16)
                for r in range(8):
                    eng.wait_ge(s_sc.h, b1[tag + "sc"] + 4 + r)
                    s_sm.inc(eng.dma_start(out=binrf_sb[:],
                                           in_=D["b_in"][r * TB:(r + 1) * TB].unsqueeze(0)), 16)

            @blk.vector
            def _(eng):
                eng.wait_ge(s_sm.h, b1[tag + "sm"] + 144)
                s_sc.inc(eng.tensor_copy(out=wssm_sb[:], in_=smstg[:]))
                s_sc.inc(eng.tensor_copy(out=pw_M[:, 0, :], in_=amat_sb[:]))
                s_sc.inc(eng.tensor_copy(out=pw_L[:, 0, :], in_=amatT_sb[:]))
                s_sc.inc(eng.tensor_copy(out=bout_sb[:], in_=boutf_sb[:]))
                for r in range(8):
                    eng.wait_ge(s_sm.h, b1[tag + "sm"] + 144 + 16 * (r + 1))
                    s_sc.inc(eng.tensor_copy(out=binr_sb[0:1, r * TB:(r + 1) * TB],
                                             in_=binrf_sb[:]))
                eng.wait_ge(s_pe.h, b1[tag + "pe"] + 1)
                s_pp.inc(eng.tensor_copy(out=dcols[:, 0:1], in_=ps_p[0:32, 2, 0:1]))
                for s in range(9):
                    n = 1 << s
                    eng.wait_ge(s_pe.h, b1[tag + "pe"] + 3 * s + 3)
                    s_pp.inc(eng.tensor_copy(out=pw_M[:, s + 1, :],
                                             in_=ps_p[0:32, 0, 0:32]))
                    s_pp.inc(eng.tensor_copy(out=pw_L[:, s + 1, :],
                                             in_=ps_p[0:32, 1, 0:32]))
                    eng.wait_ge(s_pe.h, b1[tag + "pe"] + 3 * s + 4)
                    s_pp.inc(eng.tensor_copy(out=dcols[:, n:2 * n],
                                             in_=ps_p[0:32, 2, 0:n]))

            @blk.gpsimd
            def _(eng):
                eng.memset(hc_sb[:], 0.0)
                eng.memset(halo[:], 0.0)

            @blk.tensor
            def _(eng):
                eng.wait_ge(s_sc.h, b1[tag + "sc"] + 12)
                s_pe.inc(eng.matmul(ps_p[0:32, 2, 0:1], amat_sb[:],
                                    bvc_sb[:, 1:2], start=True, stop=True))
                for s in range(9):
                    n = 1 << s
                    eng.wait_ge(s_pp.h, b1[tag + "pp"] + 3 * s + 1)
                    s_pe.inc(eng.matmul(ps_p[0:32, 0, 0:32], pw_L[:, s, :],
                                        pw_M[:, s, :], start=True, stop=True))
                    s_pe.inc(eng.matmul(ps_p[0:32, 1, 0:32], pw_M[:, s, :],
                                        pw_L[:, s, :], start=True, stop=True))
                    s_pe.inc(eng.matmul(ps_p[0:32, 2, 0:n], pw_M[:, s, :],
                                        dcols[:, 0:n], start=True, stop=True))

    cs2.close()
    assert s_pe.n - b1[tag + "pe"] == 28 and s_pp.n - b1[tag + "pp"] == 28

    lnT = EN(nc.sbuf_tensor(_u("c_lnT"), [128, EC, TB], BF16))
    xt = EN(nc.sbuf_tensor(_u("c_xt"), [128, 4, E], F32))
    lnb = EN(nc.sbuf_tensor(_u("c_lnb"), [128, 4, E], BF16))
    bn6 = EN(nc.sbuf_tensor(_u("c_bn6"), [128, 2, 6], F32))
    mvv = EN(nc.sbuf_tensor(_u("c_mvv"), [128, 2], F32))
    lnv = EN(nc.sbuf_tensor(_u("c_lnv"), [128, 1], F32))
    xzh = EN(nc.sbuf_tensor(_u("c_xzh"), [128, 2, 515], BF16))
    cvt = EN(nc.sbuf_tensor(_u("c_cvt"), [128, 2, TB], F32))
    xh_act = EN(nc.sbuf_tensor(_u("c_xha"), [128, 16, TB], BF16))
    gate_sil = EN(nc.sbuf_tensor(_u("c_gsl"), [128, 16, TB], BF16))
    gtmp = EN(nc.sbuf_tensor(_u("c_gtmp"), [128, 2, TB], F32))
    xo_sb = EN(nc.sbuf_tensor(_u("c_xo"), [128, 2, E], F32))

    for tb in range(NTB):
        names_a = ("ld", "bn", "mv", "rs", "ln", "tp", "lt", "mz", "ez", "cv",
                   "sh", "sm2", "g2", "sc2")
        sems = {nm: p.s(tag + nm) for nm in names_a}
        b = {nm: sems[nm].n for nm in names_a}
        sld = sems["ld"]; sbn = sems["bn"]; smv = sems["mv"]; srs = sems["rs"]
        sln = sems["ln"]; stp = sems["tp"]; slt = sems["lt"]; smz = sems["mz"]
        sez = sems["ez"]; scv = sems["cv"]; ssh = sems["sh"]
        ssm2 = sems["sm2"]; sg2 = sems["g2"]; ssc2 = sems["sc2"]

        act_ops = []
        for cc in range(32):
            act_ops.append(("evac", cc))
            if 2 <= cc <= 17:
                act_ops.append(("silu", cc - 2))
        evac_done = {}
        silu_done = {}
        k = 0
        for op, i in act_ops:
            k += 1
            if op == "evac":
                evac_done[i] = k
            else:
                silu_done[i] = k

        with ExitStack() as pc:
            ps_t = pc.enter_context(nc.psum_tensor(_u("c_pst"), [128, 2, 1024], BF16))
            ps_z = pc.enter_context(nc.psum_tensor(_u("c_psz"), [128, 2, TB], F32))
            ps_sm = pc.enter_context(nc.psum_tensor(_u("c_pssm"), [32, 1, TB], F32))
            ps_ms = pc.enter_context(nc.psum_tensor(_u("c_psms"), [32, 3, TB], F32))
            with nc.Block(no_gpsimd_drain=True) as blk:
                @blk.sync
                def _(eng, tb=tb, b=b):
                    eng.wait_ge(sln.h, b["ln"])
                    for tc in range(4):
                        sld.inc(eng.dma_start(out=xt[:, tc, :],
                                              in_=x3[tb * 4 + tc]), 16)

                @blk.vector
                def _(eng, tb=tb, b=b, evac_done=evac_done, silu_done=silu_done):
                    eng.wait_ge(stp.h, b["tp"])
                    for tc in range(4):
                        eng.wait_ge(sld.h, b["ld"] + 64)
                        if tc >= 1:
                            eng.wait_ge(srs.h, b["rs"] + 2 * tc)
                        sbn.inc(eng.bn_stats(out=bn6[:, 0, :], in_=xt[:, tc, 0:512]))
                        sbn.inc(eng.bn_stats(out=bn6[:, 1, :], in_=xt[:, tc, 512:1024]))
                        eng.wait_ge(sbn.h, b["bn"] + 2 * (tc + 1))
                        smv.inc(eng.bn_aggr(out=mvv[:], in_=bn6[:].rearrange("q a v -> q (a v)")))
                        eng.wait_ge(smv.h, b["mv"] + tc + 1)
                        eng.tensor_copy(out=stats_sb[:, tc, 0:1], in_=mvv[:, 0:1])
                        eng.wait_ge(srs.h, b["rs"] + 2 * (tc + 1))
                        sln.inc(eng.tensor_scalar(out=lnb[:, tc, :], in0=xt[:, tc, :],
                                                  scalar1=mvv[:, 0:1], op0=ALU.subtract,
                                                  scalar2=stats_sb[:, tc, 1:2], op1=ALU.mult))
                    for k in range(32):
                        tc, e = divmod(k, EC)
                        eng.wait_ge(stp.h, b["tp"] + k + 1)
                        slt.inc(eng.tensor_copy(out=lnT[:, e, tc * 128:(tc + 1) * 128],
                                                in_=ps_t[:, k % 2, 0:128]))
                    for cc in range(16):
                        eng.wait_ge(sez.h, b["ez"] + evac_done[cc])
                        if cc >= 2:
                            eng.wait_ge(sez.h, b["ez"] + silu_done[cc - 2])
                        w = convw_sb[:, cc, :]
                        sl = cc % 2
                        scv.inc(eng.tensor_copy(out=xzh[:, sl, 0:3],
                                                in_=halo[:, cc, :]))
                        eng.wait_ge(scv.h, b["cv"] + 5 * cc + 1)
                        scv.inc(eng.tensor_scalar(out=cvt[:, sl, :],
                                                  in0=xzh[:, sl, 0:512],
                                                  scalar1=w[:, 0:1], scalar2=None,
                                                  op0=ALU.mult))
                        for kk in range(1, 4):
                            eng.wait_ge(scv.h, b["cv"] + 5 * cc + kk + 1)
                            scv.inc(eng.scalar_tensor_tensor(
                                out=cvt[:, sl, :], in0=xzh[:, sl, kk:512 + kk],
                                scalar=w[:, kk:kk + 1], in1=cvt[:, sl, :],
                                op0=ALU.mult, op1=ALU.add))
                        eng.wait_ge(scv.h, b["cv"] + 5 * cc + 5)
                        ssh.inc(eng.tensor_copy(out=halo[:, cc, :],
                                                in_=xzh[:, sl, 512:515]))
                    eng.wait_ge(ssm2.h, b["sm2"] + 1)
                    sg2.inc(eng.tensor_scalar(out=g_sb[:], in0=ps_sm[0:32, 0, :],
                                              scalar1=bssm_sb[:, 0:1], op0=ALU.add,
                                              scalar2=bvc_sb[:, 0:1], op1=ALU.mult))
                    for s in range(9):
                        n = 1 << s
                        eng.wait_ge(ssc2.h, b["sc2"] + s + 1)
                        sg2.inc(eng.tensor_tensor(out=g_sb[:, n:TB],
                                                  in0=g_sb[:, n:TB],
                                                  in1=ps_ms[0:32, s % 2, 0:TB - n],
                                                  op=ALU.add))
                    eng.wait_ge(ssc2.h, b["sc2"] + 11)
                    sg2.inc(eng.tensor_copy(out=srow_sb[:], in_=ps_ms[0:1, 2, :]))
                    sg2.inc(eng.tensor_tensor(out=hc_sb[:, (tb + 1) % 2:(tb + 1) % 2 + 1],
                                              in0=ps_ms[0:32, 1, 0:1],
                                              in1=g_sb[:, TB - 1:TB], op=ALU.add))

                @blk.scalar
                def _(eng, tb=tb, b=b, act_ops=act_ops):
                    for tc in range(4):
                        eng.wait_ge(smv.h, b["mv"] + tc + 1)
                        if tc >= 1:
                            eng.wait_ge(srs.h, b["rs"] + 2 * tc)
                        srs.inc(eng.activation(out=lnv[:], in_=mvv[:, 1:2], func=AF.Ln,
                                               bias=p.eps_col[:, 0:1], scale=1.0))
                        eng.wait_ge(srs.h, b["rs"] + 2 * tc + 1)
                        eng.wait_ge(sln.h, b["ln"] + tc)
                        srs.inc(eng.activation(out=stats_sb[:, tc, 1:2], in_=lnv[:],
                                               func=AF.Exp, scale=-0.5))
                    for op, i in act_ops:
                        if op == "evac":
                            cc = i
                            eng.wait_ge(smz.h, b["mz"] + cc + 1)
                            if cc < 16:
                                if cc >= 2:
                                    eng.wait_ge(ssh.h, b["sh"] + cc - 1)
                                sez.inc(eng.copy(out=xzh[:, cc % 2, 3:515],
                                                 in_=ps_z[:, cc % 2, :]))
                            else:
                                sez.inc(eng.activation(out=gate_sil[:, cc - 16, :],
                                                       in_=ps_z[:, cc % 2, :],
                                                       func=AF.Silu))
                        else:
                            j = i
                            eng.wait_ge(scv.h, b["cv"] + 5 * (j + 1))
                            sez.inc(eng.activation(out=xh_act[:, j, :],
                                                   in_=cvt[:, j % 2, :], func=AF.Silu,
                                                   bias=convb_sb[:, j:j + 1]))

                @blk.tensor
                def _(eng, tb=tb, b=b, evac_done=evac_done, silu_done=silu_done):
                    for k in range(32):
                        tc, e = divmod(k, EC)
                        eng.wait_ge(sln.h, b["ln"] + tc + 1)
                        if k >= 2:
                            eng.wait_ge(slt.h, b["lt"] + k - 1)
                        stp.inc(eng.transpose(ps_t[:, k % 2, 0:128],
                                              lnb[:, tc, e * 128:(e + 1) * 128],
                                              ident_sb[:]))
                    eng.wait_ge(slt.h, b["lt"] + 32)
                    for cc in range(32):
                        if cc >= 2:
                            eng.wait_ge(sez.h, b["ez"] + evac_done[cc - 2])
                        for e in range(EC):
                            eng.matmul(ps_z[:, cc % 2, :],
                                       win_sb[:, e, cc * 128:(cc + 1) * 128],
                                       lnT[:, e, :],
                                       start=(e == 0), stop=False)
                        mm = eng.matmul(ps_z[:, cc % 2, :],
                                        binr_sb[0:1, cc * 128:(cc + 1) * 128],
                                        ones_bf[0:1, 0:TB],
                                        start=False, stop=True)
                        smz.inc(mm)
                    for cc in range(16):
                        eng.wait_ge(sez.h, b["ez"] + silu_done[cc])
                        mm = eng.matmul(ps_sm[0:32, 0, :], wssm_sb[:, cc, :],
                                        xh_act[:, cc, :],
                                        start=(cc == 0), stop=(cc == 15))
                    ssm2.inc(mm)
                    for s in range(9):
                        n = 1 << s
                        eng.wait_ge(sg2.h, b["g2"] + s + 1)
                        ssc2.inc(eng.matmul(ps_ms[0:32, s % 2, 0:TB - n],
                                            pw_L[:, s, :], g_sb[:, 0:TB - n],
                                            start=True, stop=True))
                    eng.wait_ge(sg2.h, b["g2"] + 10)
                    eng.matmul(ps_ms[0:1, 2, :], bvc_sb[:, 1:2], g_sb[:],
                               start=True, stop=False)
                    ssc2.inc(eng.matmul(ps_ms[0:1, 2, :],
                                        hc_sb[:, tb % 2:tb % 2 + 1],
                                        dcols[:], start=False, stop=True))
                    ssc2.inc(eng.matmul(ps_ms[0:32, 1, 0:1], pw_L[:, 9, :],
                                        hc_sb[:, tb % 2:tb % 2 + 1],
                                        start=True, stop=True))

            assert smz.n - b["mz"] == 32 and sez.n - b["ez"] == 48
            assert scv.n - b["cv"] == 80 and ssc2.n - b["sc2"] == 11
            assert sg2.n - b["g2"] == 12

        names_b = ("bc", "gt", "yb", "mo", "xp", "xa", "xs0", "xs1")
        semsb = {nm: p.s(tag + nm) for nm in names_b}
        bb = {nm: semsb[nm].n for nm in names_b}
        sbc = semsb["bc"]; sgt = semsb["gt"]; syb = semsb["yb"]
        smo = semsb["mo"]; sxp = semsb["xp"]; sxa = semsb["xa"]
        sxs = (semsb["xs0"], semsb["xs1"])
        with ExitStack() as pc:
            ps_bc = pc.enter_context(nc.psum_tensor(_u("c_psbc"), [128, TB], F32))
            ps_o = pc.enter_context(nc.psum_tensor(_u("c_pso"), [128, 4, TB], F32))
            with nc.Block(no_gpsimd_drain=True) as blk:
                @blk.tensor
                def _(eng, tb=tb, bb=bb):
                    sbc.inc(eng.matmul(ps_bc[:], ones_f32[0:1, 0:128], srow_sb[:],
                                       start=True, stop=True))
                    eng.wait_ge(syb.h, bb["yb"] + 16)
                    for tc in range(4):
                        if tc >= 2:
                            eng.wait_ge(sxa.h, bb["xa"] + 2 * (tc - 1))
                        for half in range(2):
                            slot = (tc % 2) * 2 + half
                            for cc in range(16):
                                mm = eng.matmul(ps_o[:, slot, :],
                                                gate_sil[:, cc, tc * 128:(tc + 1) * 128],
                                                wout_sb[:, cc, half * TB:(half + 1) * TB],
                                                start=(cc == 0), stop=False)
                            mm = eng.matmul(ps_o[:, slot, :], ones_bf[0:1, 0:128],
                                            bout_sb[0:1, half * TB:(half + 1) * TB],
                                            start=False, stop=True)
                        smo.inc(mm)

                @blk.vector
                def _(eng, tb=tb, bb=bb):
                    eng.wait_ge(sbc.h, bb["bc"] + 1)
                    for cc in range(16):
                        sl = cc % 2
                        if cc >= 2:
                            eng.wait_ge(syb.h, bb["yb"] + cc - 1)
                        sgt.inc(eng.tensor_scalar(out=gtmp[:, sl, :], in0=ps_bc[:],
                                                  scalar1=dv_sb[:, cc:cc + 1],
                                                  scalar2=None, op0=ALU.add))
                        eng.wait_ge(sgt.h, bb["gt"] + 2 * cc + 1)
                        sgt.inc(eng.tensor_tensor(out=gtmp[:, sl, :],
                                                  in0=gtmp[:, sl, :],
                                                  in1=xh_act[:, cc, :], op=ALU.mult))
                        eng.wait_ge(sgt.h, bb["gt"] + 2 * cc + 2)
                        syb.inc(eng.tensor_tensor(out=gate_sil[:, cc, :],
                                                  in0=gtmp[:, sl, :],
                                                  in1=gate_sil[:, cc, :], op=ALU.mult))
                    xpl_sb = gtmp[:].rearrange("q a t -> q (a t)")
                    eng.wait_ge(syb.h, bb["yb"] + 16)
                    for tc in range(4):
                        if tc >= 1:
                            eng.wait_ge(sxa.h, bb["xa"] + 2 * tc)
                        sxp.inc(eng.tensor_scalar(out=xpl_sb[:], in0=xt[:, tc, :],
                                                  scalar1=stats_sb[:, tc, 0:1],
                                                  op0=ALU.subtract,
                                                  scalar2=stats_sb[:, tc, 1:2],
                                                  op1=ALU.mult))
                        eng.wait_ge(sxp.h, bb["xp"] + 2 * tc + 1)
                        sxp.inc(eng.tensor_tensor(out=xpl_sb[:], in0=xpl_sb[:],
                                                  in1=xt[:, tc, :], op=ALU.add))
                        eng.wait_ge(smo.h, bb["mo"] + tc + 1)
                        eng.wait_ge(sxp.h, bb["xp"] + 2 * tc + 2)
                        if tc >= 2:
                            eng.wait_ge(sxs[tc % 2].h, bb["xs%d" % (tc % 2)] + 16 * (tc // 2))
                        for half in range(2):
                            slot = (tc % 2) * 2 + half
                            sxa.inc(eng.tensor_tensor(
                                out=xo_sb[:, tc % 2, half * TB:(half + 1) * TB],
                                in0=xpl_sb[:, half * TB:(half + 1) * TB],
                                in1=ps_o[:, slot, :], op=ALU.add))

                @blk.gpsimd
                def _(eng, tb=tb, bb=bb):
                    for tc in range(4):
                        eng.wait_ge(sxa.h, bb["xa"] + 2 * (tc + 1))
                        sxs[tc % 2].inc(eng.dma_start(out=xd3[tb * 4 + tc],
                                                      in_=xo_sb[:, tc % 2, :]), 16)
                    eng.wait_ge(sxs[0].h, sxs[0].n)
                    eng.wait_ge(sxs[1].h, sxs[1].n)

            assert sgt.n - bb["gt"] == 32 and syb.n - bb["yb"] == 16
            assert sxa.n - bb["xa"] == 8 and smo.n - bb["mo"] == 4

    c.close()



def build_core_program():
    nc = bass.Bass("TRN2", target_bir_lowering=False, debug=False)
    ctx = ExitStack()
    p = P(nc, ctx)
    D = p.D

    def din(name, shape, dtype=F32):
        D[name] = nc.dram_tensor(name, list(shape), dtype, kind="ExternalInput").ap()

    def dint(name, shape, dtype=F32):
        D[name] = nc.dram_tensor(name, list(shape), dtype, kind="Internal").ap()

    din("x", (S, E))
    din("wqT", (E, E)); din("wkT", (E, E)); din("wvpT", (E, E)); din("woT", (E, E))
    din("gamma_fm", (128, EC)); din("beta_fm", (128, EC))
    din("ident", (128, 128), BF16)
    din("identf", (128, 128))
    din("cmask", (128, 4, TB), BF16)
    din("wr", (E, 8)); din("bmi", (128, 8))
    din("wu", (NEXP, E, HID)); din("wg", (NEXP, E, HID)); din("wd", (NEXP, HID, E))
    din("wus", (E, HID)); din("wgs", (E, HID)); din("wds", (HID, E))
    din("winT", (E, HID)); din("woutT", (2048, E)); din("wssmT", (2048, 32))
    din("convw_fm", (128, 16, 4)); din("convb_fm", (128, 16))
    din("Dv_fm", (128, 16)); din("b_in_fm", (128, 32)); din("b_out", (E,))
    din("Amat", (32, 32)); din("AmatT", (32, 32)); din("BvC", (32, 2))
    din("b_in", (HID,))
    din("negj", (128, 7)); din("thr", (128, 7)); din("ltri", (128, 128), BF16)
    din("baserow", (1, 7)); din("iotac", (128, CAP)); din("iota128", (128, 1))
    din("b_ssm", (32,))
    D["y"] = nc.dram_tensor("y", [S, E], F32, kind="ExternalOutput").ap()

    dint("x_cur", (S, E))
    dint("qkv_dbg", (E, S), BF16)
    dint("lnq_dbg", (E, S), BF16)
    dint("qT_d", (E, S), BF16)
    dint("kT_d", (E, S), BF16)
    dint("v_d", (NHP, S, 130), BF16)
    dint("hid_d", (HCN, 128, S), BF16)
    dint("slot_d", (S,))
    dint("eln_dbg", (NEXP, E, CAP), BF16)
    dint("eout_dbg", (NEXP, CAP, E), BF16)

    p.ident_sb = ctx.enter_context(nc.sbuf_tensor("ident_g", [128, 128], BF16))
    p.ident_f32 = ctx.enter_context(nc.sbuf_tensor("identf_g", [128, 128], F32))
    p.eps_col = ctx.enter_context(nc.sbuf_tensor("eps_g", [128, 1], F32))
    p.ones_bf = ctx.enter_context(nc.sbuf_tensor("ones_g", [128, TB], BF16))
    p.ones_f32 = ctx.enter_context(nc.sbuf_tensor("onesf_g", [128, 128], F32))
    g = p.s("g_init")
    with nc.Block(no_gpsimd_drain=True) as blk:
        @blk.sync
        def _(eng):
            g.inc(eng.dma_start(out=p.ident_sb[:], in_=D["ident"][:]), 16)
            g.inc(eng.dma_start(out=p.ident_f32[:], in_=D["identf"][:]), 16)

        @blk.gpsimd
        def _(eng):
            eng.memset(p.ones_bf[:], 1.0)
            eng.memset(p.ones_f32[:], 1.0)
            eng.memset(p.eps_col[:], 1e-5)
            eng.wait_ge(g.h, 32)

    emit_attn_a(p, D["x"], D["y"] if OUT_STAGE == 1 else D["x_cur"])
    if OUT_STAGE >= 2:
        emit_ffn(p, D["x_cur"], D["y"] if OUT_STAGE == 2 else D["x_cur"],
                 tag="f_", final=(OUT_STAGE == 2))
    if OUT_STAGE >= 3:
        emit_attn_b(p, D["x_cur"], D["y"] if OUT_STAGE == 3 else D["x_cur"],
                    final=(OUT_STAGE == 3))
    if OUT_STAGE >= 4:
        emit_ffn(p, D["x_cur"], D["y"], tag="f_", final=True)

    ctx.close()
    return nc



def _host_consts():
    ident = np.eye(128, dtype=ml_dtypes.bfloat16)
    cmask = np.zeros((128, 4, TB), dtype=ml_dtypes.bfloat16)
    for ri, r in enumerate((0, 128, 256, 384)):
        cmask[:, ri, :] = (np.arange(128)[:, None] + r <= np.arange(TB)[None, :])
    return ident, cmask


def host_prep(inp):
    def T(a):
        return np.ascontiguousarray(np.asarray(a, dtype=np.float32).T)

    perm = (np.arange(E) % 64) * 16 + (np.arange(E) // 64)
    wq = np.asarray(inp["L0_attn_a_Wq"], np.float32)
    wk = np.asarray(inp["L0_attn_a_Wk"], np.float32)
    wv = np.asarray(inp["L0_attn_a_Wv"], np.float32)
    wo = np.asarray(inp["L0_attn_a_Wo"], np.float32)
    gamma = np.asarray(inp["L0_attn_a_gamma"], np.float32)
    beta = np.asarray(inp["L0_attn_a_beta"], np.float32)

    ident, cmask = _host_consts()
    common = {
        "wqT": T(wq), "wkT": T(wk), "wvpT": np.ascontiguousarray(wv[perm].T),
        "woT": T(wo),
        "gamma_fm": np.ascontiguousarray(gamma.reshape(EC, 128).T),
        "beta_fm": np.ascontiguousarray(beta.reshape(EC, 128).T),
        "ident": ident, "cmask": cmask,
        "identf": np.eye(128, dtype=np.float32),
        "wr": np.ascontiguousarray(np.asarray(inp["L0_ffn_Wr"], np.float32)),
        "bmi": np.ascontiguousarray(
            np.tile(64.0 - np.arange(8, dtype=np.float32), (128, 1))),
        "wu": np.ascontiguousarray(np.asarray(inp["L0_ffn_Wu"], np.float32)),
        "wg": np.ascontiguousarray(np.asarray(inp["L0_ffn_Wg"], np.float32)),
        "wd": np.ascontiguousarray(np.asarray(inp["L0_ffn_Wd"], np.float32)),
        "wus": np.ascontiguousarray(np.asarray(inp["L0_ffn_Wu_s"], np.float32)),
        "wgs": np.ascontiguousarray(np.asarray(inp["L0_ffn_Wg_s"], np.float32)),
        "wds": np.ascontiguousarray(np.asarray(inp["L0_ffn_Wd_s"], np.float32)),
        "winT": T(inp["L0_attn_b_W_in"]),
        "woutT": T(inp["L0_attn_b_W_out"]),
        "wssmT": T(inp["L0_attn_b_W_ssm"]),
        "convw_fm": np.ascontiguousarray(
            np.asarray(inp["L0_attn_b_conv_w"], np.float32).reshape(16, 128, 4)
            .transpose(1, 0, 2)),
        "convb_fm": np.ascontiguousarray(
            np.asarray(inp["L0_attn_b_conv_b"], np.float32).reshape(16, 128).T),
        "Dv_fm": np.ascontiguousarray(
            np.asarray(inp["L0_attn_b_Dv"], np.float32).reshape(16, 128).T),
        "b_in_fm": np.ascontiguousarray(
            np.asarray(inp["L0_attn_b_b_in"], np.float32).reshape(32, 128).T),
        "b_out": np.ascontiguousarray(np.asarray(inp["L0_attn_b_b_out"], np.float32)),
        "b_in": np.ascontiguousarray(np.asarray(inp["L0_attn_b_b_in"], np.float32)),
        "negj": np.ascontiguousarray(np.tile(-np.arange(7, dtype=np.float32), (128, 1))),
        "thr": np.ascontiguousarray(np.tile((np.arange(7, dtype=np.float32) + 1) * CAP,
                                            (128, 1))),
        "ltri": np.triu(np.ones((128, 128), dtype=ml_dtypes.bfloat16), 1),
        "baserow": np.ascontiguousarray(
            (np.arange(7, dtype=np.float32) * CAP).reshape(1, 7)),
        "iotac": np.ascontiguousarray(
            np.tile(np.arange(CAP, dtype=np.float32), (128, 1))),
        "iota128": np.ascontiguousarray(
            np.arange(128, dtype=np.float32).reshape(128, 1)),
        "Amat": np.ascontiguousarray(np.asarray(inp["L0_attn_b_A"], np.float32)),
        "AmatT": T(inp["L0_attn_b_A"]),
        "BvC": np.ascontiguousarray(np.stack(
            [np.asarray(inp["L0_attn_b_Bv"], np.float32),
             np.asarray(inp["L0_attn_b_Cv"], np.float32)], axis=1)),
        "b_ssm": np.ascontiguousarray(np.asarray(inp["L0_attn_b_b_ssm"], np.float32)),
    }
    return common


def kernel(**inp):
    x = np.ascontiguousarray(inp["L0_attn_a_x"], dtype=np.float32)
    common = host_prep(inp)
    nc = build_core_program()
    in_maps = []
    for bb in range(8):
        m = dict(common)
        m["x"] = np.ascontiguousarray(x[bb])
        in_maps.append(m)
    ncore = int(os.environ.get("N_CORES", "8"))
    res = run_bass_kernel_spmd(nc, in_maps[:ncore], core_ids=list(range(ncore)))
    return np.stack([r["y"] for r in res.results]).astype(np.float32)




# version 08 (final)
# speedup vs baseline: 1.0849x; 1.0282x over previous
import os
import numpy as np
import ml_dtypes
from contextlib import ExitStack

A_LIMIT = int(os.environ.get("A_LIMIT", "9"))
P_SECT = int(os.environ.get("P_SECT", "9"))
T_LIM = int(os.environ.get("T_LIM", "99"))

import concourse.bass as bass
import concourse.mybir as mybir
from concourse.bass_utils import run_bass_kernel_spmd

F32 = mybir.dt.float32
BF16 = mybir.dt.bfloat16
AF = mybir.ActivationFunctionType
ALU = mybir.AluOpType
AXX = mybir.AxisListType.X

S = 4096
E = 1024
NH = 16
NHP = 8
HID = 4096
NEXP = 7
TB = 512
NTB = S // TB
EC = E // 128
HCN = HID // 128
LNEPS = 1e-5

CAP = 1024
BIGSLOT = 60000.0
SPARSE = os.environ.get("SPARSE", "1") == "1"
OUT_STAGE = 4


_UN = [0]


def _u(name):
    _UN[0] += 1
    return f"{name}_{_UN[0]}"


class Sem:
    __slots__ = ("h", "n", "name")

    def __init__(self, ctx, nc, name):
        self.h = ctx.enter_context(nc.semaphore(name))
        self.n = 0
        self.name = name

    def inc(self, instr, k=1):
        instr.then_inc(self.h, k)
        self.n += k
        return self.n


_SEM_ALIAS = {
    "a_mm": "f_mz", "a_ev": "f_ev2", "a_st0": "f_o0", "a_st1": "f_o1",
    "a_sv0": "f_sv0", "a_sv1": "f_sv1",
    "n_ld0": "f_l0", "n_ld1": "f_l1", "n_sq": "f_si", "n_nm": "f_gm",
    "n_rd": "f_rk", "n_ra": "f_ix", "n_fin": "f_wc", "n_fin2": "f_sr",
    "t_l": "f_hl", "t_m": "f_sm", "t_qk": "f_ro", "t_ex": "f_re",
    "t_mk": "f_gt", "t_pv": "f_mo", "t_rc": "f_sc2", "t_bc": "f_bc",
    "t_qn": "f_yb", "t_cp": "f_cv", "t_h1": "f_th1",
    "t_k0": "f_ge", "t_k1": "f_cs", "t_k2": "f_mg", "t_k3": "f_mu",
    "t_v0": "f_pp", "t_v1": "f_rx",
    "e_sx": "f_sm2", "e_sq": "f_sj", "e_sm": "f_g2", "e_ac": "f_sc",
    "e_bc": "f_sbc", "e_ap": "f_oht", "e_wo": "f_om", "e_xl0": "f_e0",
    "e_xl1": "f_e1", "e_xa": "f_xa", "e_xs0": "f_xs0", "e_xs1": "f_xs1",
    "e_ch": "f_pe", "dbg": "f_sd",
}
for _suf in ("ld", "bn", "mv", "rs", "ln", "tp", "lt"):
    _SEM_ALIAS["a_" + _suf] = "f_" + _suf


class P:
    def __init__(self, nc, ctx):
        self.nc = nc
        self.ctx = ctx
        self.D = {}
        self.sem = {}

    def s(self, name):
        name = _SEM_ALIAS.get(name, name)
        if name not in self.sem:
            self.sem[name] = Sem(self.ctx, self.nc, name)
        return self.sem[name]

    def base(self, *names):
        return {n: self.s(n).n for n in names}



def emit_attn_a(p, x_src, x_dst):
    nc, D = p.nc, p.D
    ident_sb, ones_bf, ones_f32 = p.ident_sb, p.ones_bf, p.ones_f32

    c = ExitStack()
    EN = c.enter_context
    wo_sb = EN(nc.sbuf_tensor("wo_sb", [128, EC, E], BF16))
    gam_sb = EN(nc.sbuf_tensor("gam_sb", [128, EC], F32))
    bet_sb = EN(nc.sbuf_tensor("bet_sb", [128, EC], F32))
    betb_sb = EN(nc.sbuf_tensor("betb_sb", [128, EC], BF16))
    brow_sb = EN(nc.sbuf_tensor("brow_sb", [1, E], BF16))
    negB = EN(nc.sbuf_tensor("negB", [128, NH], F32))
    cw = ExitStack()
    EW = cw.enter_context
    wq_sb = EW(nc.sbuf_tensor("wq_sb", [128, EC, E], BF16))
    wk_sb = EW(nc.sbuf_tensor("wk_sb", [128, EC, E], BF16))
    wv_sb = EW(nc.sbuf_tensor("wv_sb", [128, EC, E], BF16))
    wstg = EW(nc.sbuf_tensor("wstg", [128, EC, E], F32))

    sw, sc_, spp, sw2 = p.s("g_w"), p.s("g_c"), p.s("g_p"), p.s("g_w2")
    w0, c0, p0, w20 = sw.n, sc_.n, spp.n, sw2.n
    with ExitStack() as pc:
        ps_a = pc.enter_context(nc.psum_tensor("ps_a", [1, E], F32))
        with nc.Block(no_gpsimd_drain=True) as blk:
            @blk.sync
            def _(eng):
                for i, name in enumerate(("wqT", "wkT", "wvpT", "woT")):
                    w3 = D[name].rearrange("(c k) f -> k c f", k=128)
                    if i >= 1:
                        eng.wait_ge(sc_.h, c0 + i)
                    sw.inc(eng.dma_start(out=wstg[:], in_=w3[:]), 16)
                sw2.inc(eng.dma_start(out=gam_sb[:], in_=D["gamma_fm"][:]), 16)
                sw2.inc(eng.dma_start(out=bet_sb[:], in_=D["beta_fm"][:]), 16)

            @blk.vector
            def _(eng):
                dsts = (wq_sb, wk_sb, wv_sb, wo_sb)
                for i in range(4):
                    eng.wait_ge(sw.h, w0 + 16 * (i + 1))
                    if i == 1:
                        sc_.inc(eng.tensor_scalar(out=wk_sb[:], in0=wstg[:],
                                                  scalar1=0.125, scalar2=None,
                                                  op0=ALU.mult))
                    else:
                        sc_.inc(eng.tensor_copy(out=dsts[i][:], in_=wstg[:]))
                eng.wait_ge(sw2.h, w20 + 32)
                sc_.inc(eng.tensor_copy(out=betb_sb[:], in_=bet_sb[:]))

            @blk.tensor
            def _(eng):
                eng.wait_ge(sc_.h, c0 + 5)
                for half in range(2):
                    for e in range(EC):
                        mm = eng.matmul(ps_a[0:1, half * TB:(half + 1) * TB],
                                        betb_sb[:, e:e + 1],
                                        wo_sb[:, e, half * TB:(half + 1) * TB],
                                        start=(e == 0), stop=(e == EC - 1))
                spp.inc(mm)

            @blk.scalar
            def _(eng):
                eng.wait_ge(spp.h, p0 + 1)
                spp.inc(eng.copy(out=brow_sb[:], in_=ps_a[0:1, 0:E]))

        with nc.Block(no_gpsimd_drain=True) as blk:
            @blk.vector
            def _(eng):
                for e in range(EC):
                    eng.tensor_scalar(out=wo_sb[:, e, :], in0=wo_sb[:, e, :],
                                      scalar1=gam_sb[:, e:e + 1], scalar2=None,
                                      op0=ALU.mult)

    if A_LIMIT < 1:
        cw.close(); c.close()
        return

    xt = EW(nc.sbuf_tensor("xt", [128, 4, E], F32))
    lnb = EW(nc.sbuf_tensor("lnb", [128, 4, E], BF16))
    lnT = EW(nc.sbuf_tensor("lnT", [128, EC, TB], BF16))
    bn6 = EW(nc.sbuf_tensor("bn6", [128, 2, 6], F32))
    mvv = EW(nc.sbuf_tensor("mvv", [128, 2], F32))
    lnv = EW(nc.sbuf_tensor("lnv", [128, 1], F32))
    rsc = EW(nc.sbuf_tensor("rsc", [128, 1], F32))
    qkbf = EW(nc.sbuf_tensor("qkbf", [128, 2, TB], BF16))
    vstg = EW(nc.sbuf_tensor("vstg", [128, 2, 4, 130], BF16))

    with nc.Block(no_gpsimd_drain=True) as blk:
        @blk.gpsimd
        def _(eng):
            eng.memset(vstg[:, :, :, 64:65], 1.0)
            eng.memset(vstg[:, :, :, 129:130], 1.0)

    x3 = x_src.rearrange("(n q) e -> n q e", q=128)
    with ExitStack() as pc:
        ps_t = pc.enter_context(nc.psum_tensor("ps_t", [128, 2, 1024], BF16))
        ps_q = pc.enter_context(nc.psum_tensor("ps_q", [128, 2, TB], F32))
        for tb in range(min(NTB, T_LIM)):
            sld = p.s("a_ld"); sbn = p.s("a_bn"); smv = p.s("a_mv")
            srs = p.s("a_rs"); sln = p.s("a_ln"); stp = p.s("a_tp")
            slt = p.s("a_lt"); smm = p.s("a_mm"); sev = p.s("a_ev")
            sst = (p.s("a_st0"), p.s("a_st1")); ssv = (p.s("a_sv0"), p.s("a_sv1"))
            b = p.base("a_ld", "a_bn", "a_mv", "a_rs", "a_ln", "a_tp", "a_lt",
                       "a_mm", "a_ev", "a_st0", "a_st1", "a_sv0", "a_sv1")
            with nc.Block(no_gpsimd_drain=True) as blk:
                @blk.sync
                def _(eng, tb=tb, b=b):
                    eng.wait_ge(sln.h, b["a_ln"])
                    for tc in range(4):
                        sld.inc(eng.dma_start(out=xt[:, tc, :],
                                              in_=x3[tb * 4 + tc]), 16)

                @blk.vector
                def _(eng, tb=tb, b=b):
                    eng.wait_ge(stp.h, b["a_tp"])
                    for tc in range(4):
                        eng.wait_ge(sld.h, b["a_ld"] + 64)
                        if tc >= 1:
                            eng.wait_ge(srs.h, b["a_rs"] + 2 * tc)
                        sbn.inc(eng.bn_stats(out=bn6[:, 0, :], in_=xt[:, tc, 0:512]))
                        sbn.inc(eng.bn_stats(out=bn6[:, 1, :], in_=xt[:, tc, 512:1024]))
                        eng.wait_ge(sbn.h, b["a_bn"] + 2 * (tc + 1))
                        smv.inc(eng.bn_aggr(out=mvv[:], in_=bn6[:].rearrange("q a v -> q (a v)")))
                        eng.wait_ge(srs.h, b["a_rs"] + 2 * (tc + 1))
                        sln.inc(eng.tensor_scalar(out=lnb[:, tc, :], in0=xt[:, tc, :],
                                                  scalar1=mvv[:, 0:1], op0=ALU.subtract,
                                                  scalar2=rsc[:, 0:1], op1=ALU.mult))
                    if P_SECT < 2:
                        return
                    eng.wait_ge(smm.h, b["a_mm"])
                    for k in range(32):
                        tc, e = divmod(k, EC)
                        eng.wait_ge(stp.h, b["a_tp"] + k + 1)
                        slt.inc(eng.tensor_copy(out=lnT[:, e, tc * 128:(tc + 1) * 128],
                                                in_=ps_t[:, k % 2, 0:128]))
                    if P_SECT < 4:
                        return
                    for i in range(24):
                        eng.wait_ge(smm.h, b["a_mm"] + i + 1)
                        if i < 16:
                            if i >= 2:
                                eng.wait_ge(sst[i % 2].h,
                                            b["a_st%d" % (i % 2)] + 16 * (i // 2))
                            sev.inc(eng.tensor_copy(out=qkbf[:, i % 2, :],
                                                    in_=ps_q[:, i % 2, :]))
                        else:
                            j = i - 16
                            if j >= 2:
                                eng.wait_ge(ssv[j % 2].h,
                                            b["a_sv%d" % (j % 2)] + 64 * (j // 2))
                            pv = ps_q[:, i % 2, :].rearrange("q (w r) -> q w r", w=4)
                            eng.tensor_copy(out=vstg[:, j % 2, :, 0:64],
                                            in_=pv[:, :, 0:64])
                            sev.inc(eng.tensor_copy(out=vstg[:, j % 2, :, 65:129],
                                                    in_=pv[:, :, 64:128]))

                @blk.scalar
                def _(eng, tb=tb, b=b):
                    for tc in range(4):
                        eng.wait_ge(smv.h, b["a_mv"] + tc + 1)
                        if tc >= 1:
                            eng.wait_ge(srs.h, b["a_rs"] + 2 * tc)
                        srs.inc(eng.activation(out=lnv[:], in_=mvv[:, 1:2], func=AF.Ln,
                                               bias=p.eps_col[:, 0:1], scale=1.0))
                        eng.wait_ge(srs.h, b["a_rs"] + 2 * tc + 1)
                        eng.wait_ge(sln.h, b["a_ln"] + tc)
                        srs.inc(eng.activation(out=rsc[:], in_=lnv[:], func=AF.Exp,
                                               scale=-0.5))

                @blk.tensor
                def _(eng, tb=tb, b=b):
                    if P_SECT < 2:
                        return
                    for k in range(32):
                        tc, e = divmod(k, EC)
                        eng.wait_ge(sln.h, b["a_ln"] + tc + 1)
                        if k >= 2:
                            eng.wait_ge(slt.h, b["a_lt"] + k - 1)
                        stp.inc(eng.transpose(ps_t[:, k % 2, 0:128],
                                              lnb[:, tc, e * 128:(e + 1) * 128],
                                              ident_sb[:]))
                    if P_SECT < 3:
                        return
                    eng.wait_ge(slt.h, b["a_lt"] + 32)
                    if P_SECT == 3:
                        for e in range(EC):
                            mm = eng.matmul(ps_q[:, 0, :], wq_sb[:, e, 0:128],
                                            lnT[:, e, :], start=(e == 0), stop=(e == EC - 1))
                        smm.inc(mm)
                        smm.n += 23
                        return
                    nmm = 0
                    for wsb in (wq_sb, wk_sb):
                        for fc in range(EC):
                            if nmm >= 2:
                                eng.wait_ge(sev.h, b["a_ev"] + nmm - 1)
                            for e in range(EC):
                                mm = eng.matmul(ps_q[:, nmm % 2, :],
                                                wsb[:, e, fc * 128:(fc + 1) * 128],
                                                lnT[:, e, :],
                                                start=(e == 0), stop=(e == EC - 1))
                            smm.inc(mm)
                            nmm += 1
                    for tc in range(4):
                        for half in range(2):
                            eng.wait_ge(sev.h, b["a_ev"] + nmm - 1)
                            for e in range(EC):
                                mm = eng.matmul(ps_q[:, nmm % 2, :],
                                                lnT[:, e, tc * 128:(tc + 1) * 128],
                                                wv_sb[:, e, half * TB:(half + 1) * TB],
                                                start=(e == 0), stop=(e == EC - 1))
                            smm.inc(mm)
                            nmm += 1

                @blk.gpsimd
                def _(eng, tb=tb, b=b):
                    if P_SECT < 4:
                        return
                    for i in range(24):
                        eng.wait_ge(sev.h, b["a_ev"] + i + 1)
                        if i < 8:
                            sst[i % 2].inc(eng.dma_start(
                                out=D["qT_d"][i * 128:(i + 1) * 128, tb * TB:(tb + 1) * TB],
                                in_=qkbf[:, i % 2, :]), 16)
                        elif i < 16:
                            f = i - 8
                            sst[i % 2].inc(eng.dma_start(
                                out=D["kT_d"][f * 128:(f + 1) * 128, tb * TB:(tb + 1) * TB],
                                in_=qkbf[:, i % 2, :]), 16)
                        else:
                            j = i - 16
                            tc, half = divmod(j, 2)
                            t0 = tb * TB + tc * 128
                            for hpi in range(4):
                                ssv[j % 2].inc(eng.dma_start(
                                    out=D["v_d"][half * 4 + hpi, t0:t0 + 128, :],
                                    in_=vstg[:, j % 2, hpi, :]), 16)
                    eng.wait_ge(sst[0].h, b["a_st0"] + 128)
                    eng.wait_ge(sst[1].h, b["a_st1"] + 128)
                    eng.wait_ge(ssv[0].h, b["a_sv0"] + 256)
                    eng.wait_ge(ssv[1].h, b["a_sv1"] + 256)

            if P_SECT >= 9:
                assert sld.n - b["a_ld"] == 64 and smm.n - b["a_mm"] == 24
                assert sev.n - b["a_ev"] == 24 and stp.n - b["a_tp"] == 32
                assert srs.n - b["a_rs"] == 8

    cw.close()
    if A_LIMIT < 2:
        c.close()
        return

    cn = ExitStack()
    ENN = cn.enter_context
    qmax2 = ENN(nc.sbuf_tensor("qmax2", [1, 2, NH], F32))
    sqin = ENN(nc.sbuf_tensor("sqin", [128, 2, S], BF16))
    sqsq = ENN(nc.sbuf_tensor("sqsq", [128, 2, S], BF16))
    nrow = ENN(nc.sbuf_tensor("nrow", [1, 4], F32))
    nbrow = ENN(nc.sbuf_tensor("nbrow", [1, NH], F32))
    nbrow_b = ENN(nc.sbuf_tensor("nbrow_b", [1, NH], F32))

    with nc.Block(no_gpsimd_drain=True) as blk:
        @blk.gpsimd
        def _(eng):
            eng.memset(qmax2[:], 0.0)

    sld = (p.s("n_ld0"), p.s("n_ld1")); ssq = p.s("n_sq"); snm = p.s("n_nm")
    srd = p.s("n_rd"); sra = p.s("n_ra")
    b = p.base("n_ld0", "n_ld1", "n_sq", "n_nm", "n_rd", "n_ra")
    with ExitStack() as pc:
        ps_n = pc.enter_context(nc.psum_tensor("ps_n", [1, 4, TB], F32))
        with nc.Block(no_gpsimd_drain=True) as blk:
            @blk.sync
            def _(eng):
                for i in range(16):
                    src = D["qT_d"] if i < 8 else D["kT_d"]
                    f = (i % 8) * 128
                    if i >= 2:
                        eng.wait_ge(ssq.h, b["n_sq"] + i - 1)
                    sld[i % 2].inc(eng.dma_start(out=sqin[:, i % 2, :],
                                                 in_=src[f:f + 128, :]), 16)

            @blk.scalar
            def _(eng):
                for i in range(16):
                    eng.wait_ge(sld[i % 2].h, b["n_ld%d" % (i % 2)] + 16 * (i // 2 + 1))
                    if i >= 2:
                        eng.wait_ge(snm.h, b["n_nm"] + 16 * (i - 1))
                    ssq.inc(eng.activation(out=sqsq[:, i % 2, :], in_=sqin[:, i % 2, :],
                                           func=AF.Square))

            @blk.tensor
            def _(eng):
                for i in range(16):
                    eng.wait_ge(ssq.h, b["n_sq"] + i + 1)
                    for sl in range(8):
                        for hh in range(2):
                            k = sl * 2 + hh
                            g = i * 16 + k
                            if g >= 4:
                                eng.wait_ge(sra.h, b["n_ra"] + g - 3)
                            snm.inc(eng.matmul(
                                ps_n[0:1, k % 4, :],
                                ones_bf[hh * 64:hh * 64 + 64, 0:1],
                                sqsq[hh * 64:(hh + 1) * 64, i % 2, sl * TB:(sl + 1) * TB],
                                start=True, stop=True))

            @blk.vector
            def _(eng):
                for i in range(16):
                    which = i // 8
                    for sl in range(8):
                        for hh in range(2):
                            k = sl * 2 + hh
                            g = i * 16 + k
                            h = (i % 8) * 2 + hh
                            eng.wait_ge(snm.h, b["n_nm"] + g + 1)
                            if g >= 4:
                                eng.wait_ge(sra.h, b["n_ra"] + g - 3)
                            srd.inc(eng.tensor_reduce(out=nrow[0:1, k % 4:k % 4 + 1],
                                                      in_=ps_n[0:1, k % 4, :],
                                                      axis=AXX, op=ALU.max))
                            eng.wait_ge(srd.h, b["n_rd"] + g + 1)
                            if g >= 2:
                                eng.wait_ge(sra.h, b["n_ra"] + g - 1)
                            sra.inc(eng.tensor_tensor(out=qmax2[0:1, which, h:h + 1],
                                                      in0=qmax2[0:1, which, h:h + 1],
                                                      in1=nrow[0:1, k % 4:k % 4 + 1],
                                                      op=ALU.max))

    sx = p.s("n_fin")
    b_sx = sx.n
    with ExitStack() as pc:
        ps_bx = pc.enter_context(nc.psum_tensor("ps_bx", [128, NH], F32))
        with nc.Block(no_gpsimd_drain=True) as blk:
            @blk.vector
            def _(eng):
                sx.inc(eng.tensor_tensor(out=nbrow[:], in0=qmax2[0:1, 0, :],
                                         in1=qmax2[0:1, 1, :], op=ALU.mult))
                eng.wait_ge(sx.h, b_sx + 3)
                eng.tensor_scalar(out=negB[:], in0=ps_bx[:], scalar1=-1.0,
                                  scalar2=None, op0=ALU.mult)

            @blk.scalar
            def _(eng):
                eng.wait_ge(sx.h, b_sx + 1)
                sx.inc(eng.activation(out=nbrow_b[:], in_=nbrow[:], func=AF.Sqrt,
                                      scale=1.0))

            @blk.tensor
            def _(eng):
                eng.wait_ge(sx.h, b_sx + 2)
                sx.inc(eng.matmul(ps_bx[:], ones_f32[0:1, 0:128], nbrow_b[:],
                                  start=True, stop=True))

    cn.close()
    if A_LIMIT < 3:
        c.close()
        return

    kt_sb = EN(nc.sbuf_tensor("kt_sb", [128, S], BF16))
    vt_sb = EN(nc.sbuf_tensor("vt_sb", [128, 32, 130], BF16))
    qt_sb = EN(nc.sbuf_tensor("qt_sb", [128, TB], BF16))
    pT_sb = EN(nc.sbuf_tensor("pT_sb", [128, 4, TB], BF16))
    qkvFM = EN(nc.sbuf_tensor("qkvFM", [128, EC, TB], BF16))
    linv_sb = EN(nc.sbuf_tensor("linv_sb", [128, 2, TB], F32))
    qtmp_sb = EN(nc.sbuf_tensor("qtmp_sb", [128, 2, TB], F32))
    h1stg = EN(nc.sbuf_tensor("h1stg", [128, TB], BF16))
    cm_sb = EN(nc.sbuf_tensor("cm_sb", [128, 4, TB], BF16))
    sqf = EN(nc.sbuf_tensor("sqf", [128, EC, TB], BF16))
    lnq = EN(nc.sbuf_tensor("lnq", [128, EC, TB], BF16))
    rowst = EN(nc.sbuf_tensor("rowst", [1, 6, TB], F32))
    xe_sb = EN(nc.sbuf_tensor("xe_sb", [128, 2, E], F32))
    xo_sb = EN(nc.sbuf_tensor("xo_sb", [128, 2, E], F32))

    sldm = p.s("t_m")
    with nc.Block(no_gpsimd_drain=True) as blk:
        @blk.sync
        def _(eng):
            sldm.inc(eng.dma_start(out=cm_sb[:], in_=D["cmask"][:]), 16)
            eng.wait_ge(sldm.h, 16)

    xd3 = x_dst.rearrange("(n q) e -> n q e", q=128)
    for qb in range(NTB):
        ncs = 4 * (qb + 1)
        for hp in range(NHP):
            sld = p.s("t_l"); sqk = p.s("t_qk"); sex = p.s("t_ex")
            smk = p.s("t_mk"); spv = p.s("t_pv"); src_ = p.s("t_rc")
            sbc = p.s("t_bc"); sqn = p.s("t_qn"); sh1 = p.s("t_h1")
            scp = p.s("t_cp")
            skt = [p.s("t_k%d" % i) for i in range(4)]
            svt = [p.s("t_v%d" % i) for i in range(2)]
            b = p.base("t_l", "t_qk", "t_ex", "t_mk", "t_pv", "t_rc", "t_bc",
                       "t_qn", "t_h1", "t_cp", "t_k0", "t_k1", "t_k2", "t_k3",
                       "t_v0", "t_v1")
            with ExitStack() as pc:
                ps_s = pc.enter_context(nc.psum_tensor(_u("ps_s"), [128, 4, TB], F32))
                ps_v = pc.enter_context(nc.psum_tensor(_u("ps_v"), [128, 2, TB], F32))
                ps_b = pc.enter_context(nc.psum_tensor(_u("ps_b"), [128, 2, TB], F32))
                with nc.Block(no_gpsimd_drain=True) as blk:
                    @blk.sync
                    def _(eng, qb=qb, hp=hp, ncs=ncs, b=b):
                        f = hp * 128
                        sld.inc(eng.dma_start(out=qt_sb[:],
                                              in_=D["qT_d"][f:f + 128, qb * TB:(qb + 1) * TB]), 16)
                        v4 = D["v_d"][hp].rearrange("(cq q) f -> q cq f", q=128)
                        qsz = (qb + 1) * 128
                        hsz = ncs // 2
                        skt[0].inc(eng.dma_start(out=kt_sb[:, 0:qsz],
                                                 in_=D["kT_d"][f:f + 128, 0:qsz]), 16)
                        svt[0].inc(eng.dma_start(out=vt_sb[:, 0:hsz, :],
                                                 in_=v4[:, 0:hsz, :]), 16)
                        skt[1].inc(eng.dma_start(out=kt_sb[:, qsz:2 * qsz],
                                                 in_=D["kT_d"][f:f + 128, qsz:2 * qsz]), 16)
                        svt[1].inc(eng.dma_start(out=vt_sb[:, hsz:ncs, :],
                                                 in_=v4[:, hsz:ncs, :]), 16)
                        skt[2].inc(eng.dma_start(out=kt_sb[:, 2 * qsz:3 * qsz],
                                                 in_=D["kT_d"][f:f + 128, 2 * qsz:3 * qsz]), 16)
                        skt[3].inc(eng.dma_start(out=kt_sb[:, 3 * qsz:4 * qsz],
                                                 in_=D["kT_d"][f:f + 128, 3 * qsz:4 * qsz]), 16)
                        eng.wait_ge(sqn.h, b["t_qn"] + 2)
                        sh1.inc(eng.dma_start(out=qkvFM[64:128, hp, :],
                                              in_=h1stg[0:64, :]), 16)
                        eng.wait_ge(sh1.h, b["t_h1"] + 16)

                    @blk.tensor
                    def _(eng, qb=qb, hp=hp, ncs=ncs, b=b):
                        eng.wait_ge(sld.h, b["t_l"] + 16)

                        def pv(sc):
                            diag = sc >= 4 * qb
                            if sc % (ncs // 2) == 0:
                                eng.wait_ge(svt[sc // (ncs // 2)].h,
                                            b["t_v%d" % (sc // (ncs // 2))] + 16)
                            for hh in range(2):
                                if diag:
                                    eng.wait_ge(smk.h, b["t_mk"] + 2 * (sc - 4 * qb) + hh + 1)
                                else:
                                    eng.wait_ge(sex.h, b["t_ex"] + 2 * sc + hh + 1)
                                slot = (sc % 2) * 2 + hh
                                spv.inc(eng.matmul(ps_v[0:65, hh, :],
                                                   vt_sb[:, sc, hh * 65:(hh + 1) * 65],
                                                   pT_sb[:, slot, :],
                                                   start=(sc == 0), stop=(sc == ncs - 1)))

                        for sc in range(ncs):
                            if sc % (qb + 1) == 0:
                                eng.wait_ge(skt[sc // (qb + 1)].h,
                                            b["t_k%d" % (sc // (qb + 1))] + 16)
                            for hh in range(2):
                                if sc >= 2:
                                    eng.wait_ge(sex.h, b["t_ex"] + 2 * sc - 2)
                                slot = (sc % 2) * 2 + hh
                                sqk.inc(eng.matmul(ps_s[:, slot, :],
                                                   kt_sb[hh * 64:(hh + 1) * 64, sc * 128:(sc + 1) * 128],
                                                   qt_sb[hh * 64:(hh + 1) * 64, :],
                                                   start=True, stop=True,
                                                   tile_position=(hh * 64, 0)))
                            if sc >= 1:
                                pv(sc - 1)
                        pv(ncs - 1)
                        for hh in range(2):
                            eng.wait_ge(src_.h, b["t_rc"] + hh + 1)
                            sbc.inc(eng.matmul(ps_b[0:64, hh, :],
                                               ones_f32[64:65, 0:64],
                                               linv_sb[64:65, hh, :],
                                               start=True, stop=True))

                    @blk.scalar
                    def _(eng, qb=qb, hp=hp, ncs=ncs, b=b):
                        for sc in range(ncs):
                            for hh in range(2):
                                h = hp * 2 + hh
                                eng.wait_ge(sqk.h, b["t_qk"] + 2 * sc + hh + 1)
                                if 2 * sc + hh >= 4:
                                    eng.wait_ge(spv.h, b["t_pv"] + 2 * sc + hh - 3)
                                slot = (sc % 2) * 2 + hh
                                sex.inc(eng.activation(out=pT_sb[:, slot, :],
                                                       in_=ps_s[:, slot, :],
                                                       func=AF.Exp,
                                                       bias=negB[:, h:h + 1],
                                                       scale=1.0))
                        eng.wait_ge(spv.h, b["t_pv"] + 2 * ncs)
                        for hh in range(2):
                            scp.inc(eng.copy(out=qtmp_sb[0:64, hh, :],
                                             in_=ps_v[0:64, hh, :]))

                    @blk.vector
                    def _(eng, qb=qb, hp=hp, ncs=ncs, b=b):
                        for sc in range(4 * qb, ncs):
                            r = sc - 4 * qb
                            for hh in range(2):
                                eng.wait_ge(sex.h, b["t_ex"] + 2 * sc + hh + 1)
                                slot = (sc % 2) * 2 + hh
                                smk.inc(eng.tensor_tensor(out=pT_sb[:, slot, :],
                                                          in0=pT_sb[:, slot, :],
                                                          in1=cm_sb[:, r, :],
                                                          op=ALU.mult))
                        eng.wait_ge(spv.h, b["t_pv"] + 2 * ncs)
                        for hh in range(2):
                            src_.inc(eng.reciprocal(out=linv_sb[64:65, hh, :],
                                                    in_=ps_v[64:65, hh, :]))
                        eng.wait_ge(sbc.h, b["t_bc"] + 1)
                        eng.wait_ge(scp.h, b["t_cp"] + 1)
                        sqn.inc(eng.tensor_tensor(out=qkvFM[0:64, hp, :],
                                                  in0=qtmp_sb[0:64, 0, :],
                                                  in1=ps_b[0:64, 0, :], op=ALU.mult))
                        eng.wait_ge(sbc.h, b["t_bc"] + 2)
                        eng.wait_ge(scp.h, b["t_cp"] + 2)
                        sqn.inc(eng.tensor_tensor(out=h1stg[0:64, :],
                                                  in0=qtmp_sb[0:64, 1, :],
                                                  in1=ps_b[0:64, 1, :], op=ALU.mult))



            assert sqk.n - b["t_qk"] == 2 * ncs and spv.n - b["t_pv"] == 2 * ncs
            assert sex.n - b["t_ex"] == 2 * ncs and sqn.n - b["t_qn"] == 2

        ssx = p.s("e_sx"); ssq2 = p.s("e_sq"); ssm = p.s("e_sm")
        sac = p.s("e_ac"); sbc2 = p.s("e_bc"); sap = p.s("e_ap")
        swo = p.s("e_wo"); sxa = p.s("e_xa"); ech = p.s("e_ch")
        sxl = (p.s("e_xl0"), p.s("e_xl1")); sxs = (p.s("e_xs0"), p.s("e_xs1"))
        b = p.base("e_sx", "e_sq", "e_sm", "e_ac", "e_bc", "e_ap", "e_wo",
                   "e_xl0", "e_xl1", "e_xa", "e_xs0", "e_xs1", "e_ch")
        with ExitStack() as pc:
            ps_r = pc.enter_context(nc.psum_tensor(_u("ps_r"), [1, 2, TB], F32))
            ps_bc = pc.enter_context(nc.psum_tensor(_u("ps_bc"), [128, 2, TB], F32))
            ps_o = pc.enter_context(nc.psum_tensor(_u("ps_o"), [128, 2, TB], F32))
            with nc.Block(no_gpsimd_drain=True) as blk:
                @blk.scalar
                def _(eng, qb=qb, b=b):
                    for e in range(EC):
                        ssq2.inc(eng.activation(out=sqf[:, e, :], in_=qkvFM[:, e, :],
                                                func=AF.Square))
                    eng.wait_ge(ssx.h, b["e_sx"] + 1)
                    sac.inc(eng.activation(out=rowst[0:1, 5, :], in_=rowst[0:1, 1, :],
                                           func=AF.Ln, bias=p.eps_col[0:1, 0:1], scale=1.0))
                    eng.wait_ge(sac.h, b["e_ac"] + 1)
                    sac.inc(eng.activation(out=rowst[0:1, 2, :], in_=rowst[0:1, 5, :],
                                           func=AF.Exp, scale=-0.5))

                @blk.tensor
                def _(eng, qb=qb, b=b):
                    for e in range(EC):
                        eng.matmul(ps_r[0:1, 0, :], ones_bf[:, 0:1], qkvFM[:, e, :],
                                   start=(e == 0), stop=(e == EC - 1))
                    for e in range(EC):
                        eng.wait_ge(ssq2.h, b["e_sq"] + e + 1)
                        mm = eng.matmul(ps_r[0:1, 1, :], ones_bf[:, 0:1], sqf[:, e, :],
                                        start=(e == 0), stop=(e == EC - 1))
                    ssm.inc(mm)
                    eng.wait_ge(ssx.h, b["e_sx"] + 2)
                    eng.matmul(ps_bc[:, 0, :], ones_f32[0:1, 0:128],
                               rowst[0:1, 2, :], start=True, stop=True)
                    sbc2.inc(eng.matmul(ps_bc[:, 1, :], ones_f32[0:1, 0:128],
                                        rowst[0:1, 3, :], start=True, stop=True))
                    eng.wait_ge(sap.h, b["e_ap"] + EC)
                    for tc in range(4):
                        if tc >= 1:
                            eng.wait_ge(sxa.h, b["e_xa"] + 2 * tc)
                        for half in range(2):
                            for e in range(EC):
                                eng.matmul(ps_o[:, half, :],
                                           lnq[:, e, tc * 128:(tc + 1) * 128],
                                           wo_sb[:, e, half * TB:(half + 1) * TB],
                                           start=(e == 0), stop=False)
                            mm = eng.matmul(ps_o[:, half, :], ones_bf[0:1, 0:128],
                                            brow_sb[0:1, half * TB:(half + 1) * TB],
                                            start=False, stop=True)
                        swo.inc(mm)

                @blk.vector
                def _(eng, qb=qb, b=b):
                    eng.wait_ge(ssm.h, b["e_sm"] + 1)
                    ech.inc(eng.tensor_scalar(out=rowst[0:1, 0, :], in0=ps_r[0:1, 0, :],
                                              scalar1=1.0 / E, scalar2=None, op0=ALU.mult))
                    ech.inc(eng.tensor_scalar(out=rowst[0:1, 1, :], in0=ps_r[0:1, 1, :],
                                              scalar1=1.0 / E, scalar2=None, op0=ALU.mult))
                    eng.wait_ge(ech.h, b["e_ch"] + 2)
                    ech.inc(eng.tensor_tensor(out=rowst[0:1, 4, :], in0=rowst[0:1, 0, :],
                                              in1=rowst[0:1, 0, :], op=ALU.mult))
                    eng.wait_ge(ech.h, b["e_ch"] + 3)
                    ssx.inc(eng.tensor_tensor(out=rowst[0:1, 1, :], in0=rowst[0:1, 1, :],
                                              in1=rowst[0:1, 4, :], op=ALU.subtract))
                    eng.wait_ge(sac.h, b["e_ac"] + 2)
                    ech.inc(eng.tensor_tensor(out=rowst[0:1, 3, :], in0=rowst[0:1, 0, :],
                                              in1=rowst[0:1, 2, :], op=ALU.mult))
                    eng.wait_ge(ech.h, b["e_ch"] + 4)
                    ssx.inc(eng.tensor_scalar(out=rowst[0:1, 3, :], in0=rowst[0:1, 3, :],
                                              scalar1=-1.0, scalar2=None, op0=ALU.mult))
                    eng.wait_ge(sbc2.h, b["e_bc"] + 1)
                    for e in range(EC):
                        ech.inc(eng.tensor_tensor(out=lnq[:, e, :], in0=qkvFM[:, e, :],
                                                  in1=ps_bc[:, 0, :], op=ALU.mult))
                        eng.wait_ge(ech.h, b["e_ch"] + 5 + e)
                        sap.inc(eng.tensor_tensor(out=lnq[:, e, :], in0=lnq[:, e, :],
                                                  in1=ps_bc[:, 1, :], op=ALU.add))
                    for tc in range(4):
                        eng.wait_ge(swo.h, b["e_wo"] + tc + 1)
                        eng.wait_ge(sxl[tc % 2].h, b["e_xl%d" % (tc % 2)] + 16 * (tc // 2 + 1))
                        if tc >= 2:
                            eng.wait_ge(sxs[tc % 2].h, b["e_xs%d" % (tc % 2)] + 16 * (tc // 2))
                        for half in range(2):
                            sxa.inc(eng.tensor_tensor(
                                out=xo_sb[:, tc % 2, half * TB:(half + 1) * TB],
                                in0=xe_sb[:, tc % 2, half * TB:(half + 1) * TB],
                                in1=ps_o[:, half, :], op=ALU.add))

                @blk.sync
                def _(eng, qb=qb, b=b):
                    for tc in range(4):
                        if tc >= 2:
                            eng.wait_ge(sxa.h, b["e_xa"] + 2 * (tc - 1))
                        sxl[tc % 2].inc(eng.dma_start(out=xe_sb[:, tc % 2, :],
                                                      in_=x3[qb * 4 + tc]), 16)

                @blk.gpsimd
                def _(eng, qb=qb, b=b):
                    if os.environ.get("DBG"):
                        sdbg = p.s("dbg")
                        for e in range(EC):
                            sdbg.inc(eng.dma_start(
                                out=D["qkv_dbg"][e * 128:(e + 1) * 128, qb * TB:(qb + 1) * TB],
                                in_=qkvFM[:, e, :]), 16)
                        eng.wait_ge(sap.h, b["e_ap"] + EC)
                        for e in range(EC):
                            sdbg.inc(eng.dma_start(
                                out=D["lnq_dbg"][e * 128:(e + 1) * 128, qb * TB:(qb + 1) * TB],
                                in_=lnq[:, e, :]), 16)
                        eng.wait_ge(sdbg.h, sdbg.n)
                    for tc in range(4):
                        eng.wait_ge(sxa.h, b["e_xa"] + 2 * (tc + 1))
                        sxs[tc % 2].inc(eng.dma_start(out=xd3[qb * 4 + tc],
                                                      in_=xo_sb[:, tc % 2, :]), 16)
                    eng.wait_ge(sxs[0].h, b["e_xs0"] + 32)
                    eng.wait_ge(sxs[1].h, b["e_xs1"] + 32)

        assert sap.n - b["e_ap"] == EC and swo.n - b["e_wo"] == 4
        assert sxa.n - b["e_xa"] == 8

    c.close()



def emit_ffn(p, x_src, x_dst, tag, final, sparse=SPARSE):
    nc, D = p.nc, p.D
    ident_sb, ones_bf = p.ident_sb, p.ones_bf

    c = ExitStack()
    EN = c.enter_context
    scale_sb = EN(nc.sbuf_tensor(_u("f_scale"), [128, 32], F32))
    negidx_sb = EN(nc.sbuf_tensor(_u("f_negidx"), [128, 32], F32))
    sj_sb = EN(nc.sbuf_tensor(_u("f_sj"), [128, 32], F32))
    wr_sb = EN(nc.sbuf_tensor(_u("f_wr"), [128, EC, 8], BF16))
    wrstg = EN(nc.sbuf_tensor(_u("f_wrstg"), [128, EC, 8], F32))
    bmi_sb = EN(nc.sbuf_tensor(_u("f_bmi"), [128, 8], F32))
    slot_f32 = EN(nc.sbuf_tensor(_u("f_slot"), [128, 32], F32))
    srowb = EN(nc.sbuf_tensor(_u("f_srowb"), [1, TB], F32))
    slotT_sb = EN(nc.sbuf_tensor(_u("f_slotT"), [32, 128], F32))
    iotac_sb = EN(nc.sbuf_tensor(_u("f_iotac"), [128, CAP], F32))
    iota128_sb = EN(nc.sbuf_tensor(_u("f_io128"), [128, 1], F32))
    ln_tm = EN(nc.sbuf_tensor(_u("f_lntm"), [128, 32, E], BF16))
    offrow = EN(nc.sbuf_tensor(_u("f_off"), [1, 7], F32))
    negj_sb = EN(nc.sbuf_tensor(_u("f_negj"), [128, 7], F32))
    thr_sb = EN(nc.sbuf_tensor(_u("f_thr"), [128, 7], F32))
    ltri_sb = EN(nc.sbuf_tensor(_u("f_ltri"), [128, 128], BF16))
    oh_sb = EN(nc.sbuf_tensor(_u("f_oh"), [128, 7], BF16))
    t7a = EN(nc.sbuf_tensor(_u("f_t7a"), [128, 7], F32))
    t7b = EN(nc.sbuf_tensor(_u("f_t7b"), [128, 7], F32))
    ccol = EN(nc.sbuf_tensor(_u("f_ccol"), [128, 3], F32))
    c_ln = ExitStack()
    lnT = c_ln.enter_context(nc.sbuf_tensor(_u("f_lnT"), [128, EC, S], BF16))

    x3 = x_src.rearrange("(n q) e -> n q e", q=128)
    xd3 = x_dst.rearrange("(n q) e -> n q e", q=128)

    cb = ExitStack()
    EB = cb.enter_context
    xt = EB(nc.sbuf_tensor(_u("f_xt"), [128, 4, E], F32))
    bn6 = EB(nc.sbuf_tensor(_u("f_bn6"), [128, 2, 6], F32))
    mvv = EB(nc.sbuf_tensor(_u("f_mvv"), [128, 2], F32))
    lnv = EB(nc.sbuf_tensor(_u("f_lnv"), [128, 1], F32))
    rsc = EB(nc.sbuf_tensor(_u("f_rsc"), [128, 1], F32))
    rcol = EB(nc.sbuf_tensor(_u("f_rcol"), [128, 6], F32))
    esb = EB(nc.sbuf_tensor(_u("f_esb"), [128, 8], F32))
    eqsb = EB(nc.sbuf_tensor(_u("f_eqsb"), [128, 8], F32))

    sw = p.s(tag + "w")
    b_w = sw.n
    with nc.Block(no_gpsimd_drain=True) as blk:
        @blk.sync
        def _(eng):
            w3 = D["wr"].rearrange("(c k) j -> k c j", k=128)
            sw.inc(eng.dma_start(out=wrstg[:], in_=w3[:]), 16)
            sw.inc(eng.dma_start(out=bmi_sb[:], in_=D["bmi"][:]), 16)
            sw.inc(eng.dma_start(out=negj_sb[:], in_=D["negj"][:]), 16)
            sw.inc(eng.dma_start(out=thr_sb[:], in_=D["thr"][:]), 16)
            sw.inc(eng.dma_start(out=ltri_sb[:], in_=D["ltri"][:]), 16)
            sw.inc(eng.dma_start(out=offrow[:], in_=D["baserow"][:]), 16)
            sw.inc(eng.dma_start(out=iotac_sb[:], in_=D["iotac"][:]), 16)
            sw.inc(eng.dma_start(out=iota128_sb[:], in_=D["iota128"][:]), 16)

        @blk.vector
        def _(eng):
            eng.wait_ge(sw.h, b_w + 128)
            sw.inc(eng.tensor_copy(out=wr_sb[:], in_=wrstg[:]))

    with ExitStack() as pc:
        ps_t = pc.enter_context(nc.psum_tensor(_u("f_pst"), [128, 2, 1024], BF16))
        ps_l = pc.enter_context(nc.psum_tensor(_u("f_psl"), [128, 2, 512], F32))
        if sparse:
            ps_rk = pc.enter_context(nc.psum_tensor(_u("f_psrk"), [128, 2, 512], F32))
            ps_ct = pc.enter_context(nc.psum_tensor(_u("f_psct"), [1, 2, 512], F32))
        for tb in range(NTB):
            sld = p.s(tag + "ld"); sbn = p.s(tag + "bn"); smv = p.s(tag + "mv")
            srs = p.s(tag + "rs"); sln = p.s(tag + "ln"); stp = p.s(tag + "tp")
            slt = p.s(tag + "lt"); sro = p.s(tag + "ro"); srx = p.s(tag + "rx")
            sre = p.s(tag + "re")
            six = p.s(tag + "ix"); srk = p.s(tag + "rk"); ssct = p.s(tag + "sct")
            b = p.base(*(tag + nm for nm in ("ld", "bn", "mv", "rs", "ln", "tp",
                                             "lt", "ro", "rx", "re", "ix", "rk",
                                             "sct")))

            def B(nm, b=b):
                return b[tag + nm]

            with nc.Block(no_gpsimd_drain=True) as blk:
                @blk.sync
                def _(eng, tb=tb, B=B):
                    eng.wait_ge(sln.h, B("ln"))
                    for tc in range(4):
                        sld.inc(eng.dma_start(out=xt[:, tc, :],
                                              in_=x3[tb * 4 + tc]), 16)

                @blk.vector
                def _(eng, tb=tb, B=B):
                    eng.wait_ge(stp.h, B("tp"))
                    for tc in range(4):
                        eng.wait_ge(sld.h, B("ld") + 64)
                        if tc >= 1:
                            eng.wait_ge(srs.h, B("rs") + 2 * tc)
                        sbn.inc(eng.bn_stats(out=bn6[:, 0, :], in_=xt[:, tc, 0:512]))
                        sbn.inc(eng.bn_stats(out=bn6[:, 1, :], in_=xt[:, tc, 512:1024]))
                        eng.wait_ge(sbn.h, B("bn") + 2 * (tc + 1))
                        smv.inc(eng.bn_aggr(out=mvv[:], in_=bn6[:].rearrange("q a v -> q (a v)")))
                        eng.wait_ge(srs.h, B("rs") + 2 * (tc + 1))
                        sln.inc(eng.tensor_scalar(out=ln_tm[:, tb * 4 + tc, :],
                                                  in0=xt[:, tc, :],
                                                  scalar1=mvv[:, 0:1], op0=ALU.subtract,
                                                  scalar2=rsc[:, 0:1], op1=ALU.mult))
                    for k in range(32):
                        tc, e = divmod(k, EC)
                        eng.wait_ge(stp.h, B("tp") + k + 1)
                        slt.inc(eng.tensor_copy(
                            out=lnT[:, e, tb * TB + tc * 128:tb * TB + (tc + 1) * 128],
                            in_=ps_t[:, k % 2, 0:128]))
                    for tc in range(4):
                        tcg = tb * 4 + tc
                        eng.wait_ge(sro.h, B("ro") + tc + 1)
                        if tc >= 1:
                            eng.wait_ge(sre.h, B("re") + tc)
                        srx.inc(eng.tensor_reduce(out=rcol[:, 0:1], in_=ps_l[:, tc % 2, 0:8],
                                                  axis=AXX, op=ALU.max, negate=True))
                        srx.inc(eng.tensor_reduce(out=rcol[:, 1:2], in_=ps_l[:, tc % 2, 0:8],
                                                  axis=AXX, op=ALU.max))
                        eng.wait_ge(sre.h, B("re") + tc + 1)
                        srx.inc(eng.tensor_reduce(out=rcol[:, 2:3], in_=esb[:],
                                                  axis=AXX, op=ALU.add))
                        eng.wait_ge(srx.h, B("rx") + 8 * tc + 3)
                        srx.inc(eng.reciprocal(out=rcol[:, 3:4], in_=rcol[:, 2:3]))
                        eng.wait_ge(srx.h, B("rx") + 8 * tc + 4)
                        srx.inc(eng.tensor_scalar(out=scale_sb[:, tcg:tcg + 1],
                                                  in0=rcol[:, 3:4], scalar1=2.0,
                                                  scalar2=None, op0=ALU.mult))
                        srx.inc(eng.tensor_scalar(out=eqsb[:], in0=ps_l[:, tc % 2, 0:8],
                                                  scalar1=rcol[:, 1:2], op0=ALU.is_ge,
                                                  scalar2=None))
                        eng.wait_ge(srx.h, B("rx") + 8 * tc + 6)
                        srx.inc(eng.tensor_tensor(out=eqsb[:], in0=eqsb[:],
                                                  in1=bmi_sb[:], op=ALU.mult))
                        eng.wait_ge(srx.h, B("rx") + 8 * tc + 7)
                        srx.inc(eng.tensor_reduce(out=negidx_sb[:, tcg:tcg + 1],
                                                  in_=eqsb[:], axis=AXX, op=ALU.max))
                        eng.wait_ge(srx.h, B("rx") + 8 * tc + 8)
                        six.inc(eng.tensor_scalar(out=negidx_sb[:, tcg:tcg + 1],
                                                  in0=negidx_sb[:, tcg:tcg + 1],
                                                  scalar1=-64.0, scalar2=None,
                                                  op0=ALU.add))
                        if sparse:
                            IX = 12
                            bx = B("ix") + IX * tc
                            eng.wait_ge(six.h, bx + 1)
                            if tc >= 1:
                                eng.wait_ge(srk.h, B("rk") + 2 * tc - 1)
                            six.inc(eng.tensor_scalar(out=oh_sb[:], in0=negj_sb[:],
                                                      scalar1=negidx_sb[:, tcg:tcg + 1],
                                                      scalar2=None, op0=ALU.is_equal))
                            eng.wait_ge(srk.h, B("rk") + 2 * tc + 1)
                            six.inc(eng.tensor_tensor(out=t7a[:],
                                                      in0=ps_rk[:, tc % 2, 0:7],
                                                      in1=thr_sb[:], op=ALU.is_ge))
                            eng.wait_ge(six.h, bx + 3)
                            six.inc(eng.tensor_scalar(out=t7a[:], in0=t7a[:],
                                                      scalar1=-1.0, op0=ALU.mult,
                                                      scalar2=1.0, op1=ALU.add))
                            eng.wait_ge(six.h, bx + 4)
                            six.inc(eng.tensor_tensor(out=t7b[:], in0=oh_sb[:],
                                                      in1=t7a[:], op=ALU.mult))
                            eng.wait_ge(six.h, bx + 5)
                            six.inc(eng.tensor_tensor(out=t7a[:],
                                                      in0=ps_rk[:, tc % 2, 0:7],
                                                      in1=t7b[:], op=ALU.mult))
                            eng.wait_ge(six.h, bx + 6)
                            six.inc(eng.tensor_reduce(out=ccol[:, 0:1], in_=t7a[:],
                                                      axis=AXX, op=ALU.add))
                            eng.wait_ge(six.h, bx + 7)
                            six.inc(eng.tensor_reduce(out=ccol[:, 1:2], in_=t7b[:],
                                                      axis=AXX, op=ALU.add))
                            eng.wait_ge(six.h, bx + 8)
                            six.inc(eng.scalar_tensor_tensor(out=ccol[:, 2:3],
                                                             in0=ccol[:, 1:2],
                                                             scalar=-BIGSLOT,
                                                             in1=ccol[:, 0:1],
                                                             op0=ALU.mult, op1=ALU.add))
                            eng.wait_ge(six.h, bx + 9)
                            six.inc(eng.tensor_scalar(out=ccol[:, 2:3], in0=ccol[:, 2:3],
                                                      scalar1=BIGSLOT, scalar2=None,
                                                      op0=ALU.add))
                            eng.wait_ge(six.h, bx + 10)
                            six.inc(eng.tensor_copy(out=slot_f32[:, tcg:tcg + 1],
                                                    in_=ccol[:, 2:3]))
                            eng.wait_ge(srk.h, B("rk") + 2 * tc + 2)
                            eng.wait_ge(six.h, bx + 11)
                            six.inc(eng.tensor_tensor(out=offrow[:],
                                                      in0=offrow[:],
                                                      in1=ps_ct[0:1, tc % 2, 0:7],
                                                      op=ALU.add))

                @blk.scalar
                def _(eng, tb=tb, B=B):
                    for tc in range(4):
                        eng.wait_ge(smv.h, B("mv") + tc + 1)
                        if tc >= 1:
                            eng.wait_ge(srs.h, B("rs") + 2 * tc)
                        srs.inc(eng.activation(out=lnv[:], in_=mvv[:, 1:2], func=AF.Ln,
                                               bias=p.eps_col[:, 0:1], scale=1.0))
                        eng.wait_ge(srs.h, B("rs") + 2 * tc + 1)
                        eng.wait_ge(sln.h, B("ln") + tc)
                        srs.inc(eng.activation(out=rsc[:], in_=lnv[:], func=AF.Exp,
                                               scale=-0.5))
                    for tc in range(4):
                        eng.wait_ge(srx.h, B("rx") + 8 * tc + 1)
                        if tc >= 1:
                            eng.wait_ge(srx.h, B("rx") + 8 * (tc - 1) + 3)
                        sre.inc(eng.activation(out=esb[:], in_=ps_l[:, tc % 2, 0:8],
                                               func=AF.Exp, bias=rcol[:, 0:1],
                                               scale=1.0))

                @blk.tensor
                def _(eng, tb=tb, B=B):
                    for k in range(32):
                        tc, e = divmod(k, EC)
                        eng.wait_ge(sln.h, B("ln") + tc + 1)
                        if k >= 2:
                            eng.wait_ge(slt.h, B("lt") + k - 1)
                        stp.inc(eng.transpose(ps_t[:, k % 2, 0:128],
                                              ln_tm[:, tb * 4 + tc, e * 128:(e + 1) * 128],
                                              ident_sb[:]))
                    eng.wait_ge(slt.h, B("lt") + 32)
                    for tc in range(4):
                        tcg = tb * 4 + tc
                        if tc >= 2:
                            eng.wait_ge(srx.h, B("rx") + 8 * (tc - 1))
                        for e in range(EC):
                            mm = eng.matmul(ps_l[:, tc % 2, 0:8],
                                            lnT[:, e, tcg * 128:(tcg + 1) * 128],
                                            wr_sb[:, e, :],
                                            start=(e == 0), stop=(e == EC - 1))
                        sro.inc(mm)
                        if sparse:
                            eng.wait_ge(six.h, B("ix") + 12 * tc + 2)
                            if tc >= 2:
                                eng.wait_ge(six.h, B("ix") + 12 * (tc - 2) + 6)
                            eng.matmul(ps_rk[:, tc % 2, 0:7], ltri_sb[:], oh_sb[:],
                                       start=True, stop=False)
                            srk.inc(eng.matmul(ps_rk[:, tc % 2, 0:7],
                                               p.ones_f32[0:1, 0:128], offrow[:],
                                               start=False, stop=True))
                            srk.inc(eng.matmul(ps_ct[0:1, tc % 2, 0:7],
                                               ones_bf[:, 0:1], oh_sb[:],
                                               start=True, stop=True))

            assert srx.n - b[tag + "rx"] == 32 and sro.n - b[tag + "ro"] == 4
            if sparse:
                assert six.n - b[tag + "ix"] == 48 and srk.n - b[tag + "rk"] == 8

    if sparse:
        ssd = p.s(tag + "sd")
        ssd2 = p.s(tag + "pp")
        b_sd, b_sd2 = ssd.n, ssd2.n
        with ExitStack() as pc:
            ps_sd = pc.enter_context(nc.psum_tensor(_u("f_pssd"), [32, TB], F32))
            with nc.Block(no_gpsimd_drain=True) as blk:
                @blk.tensor
                def _(eng):
                    ssd2.inc(eng.transpose(ps_sd[0:32, 0:128], slot_f32[:],
                                           p.ident_f32[:]))

                @blk.vector
                def _(eng):
                    eng.wait_ge(ssd2.h, b_sd2 + 1)
                    ssd2.inc(eng.tensor_copy(out=slotT_sb[:],
                                             in_=ps_sd[0:32, 0:128]))

                @blk.gpsimd
                def _(eng):
                    eng.wait_ge(ssd2.h, b_sd2 + 2)
                    sd2 = D["slot_d"].rearrange("(a q) -> a q", q=128)
                    ssd.inc(eng.dma_start(out=sd2, in_=slotT_sb[:]), 16)
                    eng.wait_ge(ssd.h, b_sd + 16)

    cb.close()

    js = ([NEXP] + list(range(NEXP))) if sparse else list(range(NEXP + 1))
    for j in js:
        shared = (j == NEXP)
        if sparse and not shared:
            _emit_expert_sparse(p, D, tag, j, ln_tm, slot_f32, scale_sb, srowb,
                                iotac_sb, iota128_sb, x3, xd3,
                                final and (j == NEXP - 1))
            continue
        wu_src = D["wus"] if shared else D["wu"][j]
        wg_src = D["wgs"] if shared else D["wg"][j]
        wd_src = D["wds"] if shared else D["wd"][j]

        c1 = ExitStack()
        E1 = c1.enter_context
        wustg = E1(nc.sbuf_tensor(_u("f_wustg"), [128, 2, EC, 128], F32))
        wgstg = E1(nc.sbuf_tensor(_u("f_wgstg"), [128, 2, EC, 128], F32))
        wub = E1(nc.sbuf_tensor(_u("f_wub"), [128, 2, EC, 128], BF16))
        wgb = E1(nc.sbuf_tensor(_u("f_wgb"), [128, 2, EC, 128], BF16))
        gsil = E1(nc.sbuf_tensor(_u("f_gsil"), [128, 2, TB], BF16))
        hbf = E1(nc.sbuf_tensor(_u("f_hbf"), [128, 2, TB], BF16))
        s_l0 = p.s(tag + "l0"); s_l1 = p.s(tag + "l1")
        s_cs = p.s(tag + "cs"); s_mu = p.s(tag + "mu"); s_mg = p.s(tag + "mg")
        s_si = p.s(tag + "si"); s_hb = p.s(tag + "hb")
        s_h0 = p.s(tag + "h0"); s_h1 = p.s(tag + "h1")
        b = p.base(*(tag + nm for nm in ("l0", "l1", "cs", "mu", "mg", "si",
                                         "hb", "h0", "h1")))

        def B(nm, b=b):
            return b[tag + nm]

        with ExitStack() as pc:
            ps_u = pc.enter_context(nc.psum_tensor(_u("f_psu"), [128, 2, TB], F32))
            ps_g = pc.enter_context(nc.psum_tensor(_u("f_psg"), [128, 2, TB], F32))
            with nc.Block(no_gpsimd_drain=True) as blk:
                @blk.sync
                def _(eng, B=B):
                    sl = (s_l0, s_l1)
                    wu3 = wu_src.rearrange("(c k) h -> k c h", k=128)
                    wg3 = wg_src.rearrange("(c k) h -> k c h", k=128)
                    for hc in range(HCN):
                        if hc >= 2:
                            eng.wait_ge(s_cs.h, B("cs") + 2 * (hc - 1))
                        sl[hc % 2].inc(eng.dma_start(
                            out=wustg[:, hc % 2, :, :],
                            in_=wu3[:, :, hc * 128:(hc + 1) * 128]), 16)
                        sl[hc % 2].inc(eng.dma_start(
                            out=wgstg[:, hc % 2, :, :],
                            in_=wg3[:, :, hc * 128:(hc + 1) * 128]), 16)

                @blk.vector
                def _(eng, B=B):
                    sl = (s_l0, s_l1)
                    for hc in range(HCN):
                        eng.wait_ge(sl[hc % 2].h, B("l%d" % (hc % 2)) + 32 * (hc // 2 + 1))
                        if hc >= 2:
                            eng.wait_ge(s_mg.h, B("mg") + 8 * (hc - 1))
                        s_cs.inc(eng.tensor_copy(out=wub[:, hc % 2, :, :],
                                                  in_=wustg[:, hc % 2, :, :]))
                        s_cs.inc(eng.tensor_copy(out=wgb[:, hc % 2, :, :],
                                                 in_=wgstg[:, hc % 2, :, :]))
                        for tb in range(NTB):
                            g = hc * 8 + tb
                            eng.wait_ge(s_mu.h, B("mu") + g + 1)
                            eng.wait_ge(s_si.h, B("si") + g + 1)
                            if g >= 2:
                                eng.wait_ge(s_h0.h if g % 2 == 0 else s_h1.h,
                                            B("h%d" % (g % 2)) + 16 * (g // 2))
                            s_hb.inc(eng.tensor_tensor(out=hbf[:, g % 2, :],
                                                       in0=ps_u[:, tb % 2, :],
                                                       in1=gsil[:, g % 2, :],
                                                       op=ALU.mult))

                @blk.scalar
                def _(eng, B=B):
                    for hc in range(HCN):
                        for tb in range(NTB):
                            g = hc * 8 + tb
                            eng.wait_ge(s_mg.h, B("mg") + g + 1)
                            if g >= 2:
                                eng.wait_ge(s_hb.h, B("hb") + g - 1)
                            s_si.inc(eng.activation(out=gsil[:, g % 2, :],
                                                    in_=ps_g[:, tb % 2, :],
                                                    func=AF.Silu))

                @blk.tensor
                def _(eng, B=B):
                    for hc in range(HCN):
                        eng.wait_ge(s_cs.h, B("cs") + 2 * (hc + 1))
                        for tb in range(NTB):
                            g = hc * 8 + tb
                            if g >= 2:
                                eng.wait_ge(s_hb.h, B("hb") + g - 1)
                            for e in range(EC):
                                mm = eng.matmul(ps_u[:, tb % 2, :],
                                                wub[:, hc % 2, e, :],
                                                lnT[:, e, tb * TB:(tb + 1) * TB],
                                                start=(e == 0), stop=(e == EC - 1))
                            s_mu.inc(mm)
                            if g >= 2:
                                eng.wait_ge(s_si.h, B("si") + g - 1)
                            for e in range(EC):
                                mm = eng.matmul(ps_g[:, tb % 2, :],
                                                wgb[:, hc % 2, e, :],
                                                lnT[:, e, tb * TB:(tb + 1) * TB],
                                                start=(e == 0), stop=(e == EC - 1))
                            s_mg.inc(mm)

                @blk.gpsimd
                def _(eng, B=B):
                    sh = (s_h0, s_h1)
                    for hc in range(HCN):
                        for tb in range(NTB):
                            g = hc * 8 + tb
                            eng.wait_ge(s_hb.h, B("hb") + g + 1)
                            sh[g % 2].inc(eng.dma_start(
                                out=D["hid_d"][hc, :, tb * TB:(tb + 1) * TB],
                                in_=hbf[:, g % 2, :]), 16)
                    eng.wait_ge(s_h0.h, B("h0") + 16 * 128)
                    eng.wait_ge(s_h1.h, B("h1") + 16 * 128)

        c1.close()
        if sparse and shared:
            c_ln.close()
        assert s_mu.n - b[tag + "mu"] == 256 and s_hb.n - b[tag + "hb"] == 256

        c2 = ExitStack()
        E2 = c2.enter_context
        wdstg = E2(nc.sbuf_tensor(_u("f_wdstg"), [128, 2, 2, E], F32))
        wd_sb = E2(nc.sbuf_tensor(_u("f_wdsb"), [128, HCN, E], BF16))
        hid_sb = E2(nc.sbuf_tensor(_u("f_hid"), [128, HCN, TB], BF16))
        xe_sb = E2(nc.sbuf_tensor(_u("f_xe"), [128, 2, E], F32))
        xo_sb = E2(nc.sbuf_tensor(_u("f_xo"), [128, 2, E], F32))
        s_w0 = p.s(tag + "w0"); s_w1 = p.s(tag + "w1"); s_wc = p.s(tag + "wc")
        s_hl = p.s(tag + "hl"); s_md = p.s(tag + "md"); s_sj = p.s(tag + "sj")
        s_xa = p.s(tag + "xa")
        s_e0 = p.s(tag + "e0"); s_e1 = p.s(tag + "e1")
        s_o0 = p.s(tag + "o0"); s_o1 = p.s(tag + "o1")
        b = p.base(*(tag + nm for nm in ("w0", "w1", "wc", "hl", "md", "sj",
                                         "xa", "e0", "e1", "o0", "o1")))

        def B(nm, b=b):
            return b[tag + nm]

        with ExitStack() as pc:
            ps_d = pc.enter_context(nc.psum_tensor(_u("f_psd"), [128, 4, TB], F32))
            with nc.Block(no_gpsimd_drain=True) as blk:
                @blk.sync
                def _(eng, B=B):
                    swp = (s_w0, s_w1)
                    wd3 = wd_src.rearrange("(c k) e -> k c e", k=128)
                    for r in range(16):
                        if r >= 2:
                            eng.wait_ge(s_wc.h, B("wc") + r - 1)
                        swp[r % 2].inc(eng.dma_start(out=wdstg[:, r % 2, :, :],
                                                     in_=wd3[:, r * 2:r * 2 + 2, :]), 16)
                    for tb in range(NTB):
                        if tb >= 1:
                            eng.wait_ge(s_md.h, B("md") + 4 * tb)
                        for hc in range(HCN):
                            s_hl.inc(eng.dma_start(
                                out=hid_sb[:, hc, :],
                                in_=D["hid_d"][hc, :, tb * TB:(tb + 1) * TB]), 16)

                @blk.vector
                def _(eng, B=B):
                    swp = (s_w0, s_w1)
                    for r in range(16):
                        eng.wait_ge(swp[r % 2].h, B("w%d" % (r % 2)) + 16 * (r // 2 + 1))
                        s_wc.inc(eng.tensor_copy(out=wd_sb[:, r * 2:r * 2 + 2, :],
                                                 in_=wdstg[:, r % 2, :, :]))
                    if not shared:
                        s_sj.inc(eng.tensor_scalar(out=sj_sb[:], in0=negidx_sb[:],
                                                   scalar1=float(-j), scalar2=None,
                                                   op0=ALU.is_equal))
                        eng.wait_ge(s_sj.h, B("sj") + 1)
                        s_sj.inc(eng.tensor_tensor(out=sj_sb[:], in0=sj_sb[:],
                                                   in1=scale_sb[:], op=ALU.mult))
                    sxl = (s_e0, s_e1); sxs = (s_o0, s_o1)
                    for tb in range(NTB):
                        for tc in range(4):
                            gt = tb * 4 + tc
                            tcg = gt
                            eng.wait_ge(s_md.h, B("md") + gt + 1)
                            eng.wait_ge(sxl[gt % 2].h, B("e%d" % (gt % 2)) + 16 * (gt // 2 + 1))
                            if gt >= 2:
                                eng.wait_ge(sxs[gt % 2].h, B("o%d" % (gt % 2)) + 16 * (gt // 2))
                            for half in range(2):
                                slot = (tc % 2) * 2 + half
                                if shared:
                                    s_xa.inc(eng.tensor_tensor(
                                        out=xo_sb[:, gt % 2, half * TB:(half + 1) * TB],
                                        in0=xe_sb[:, gt % 2, half * TB:(half + 1) * TB],
                                        in1=ps_d[:, slot, :], op=ALU.add))
                                else:
                                    s_xa.inc(eng.scalar_tensor_tensor(
                                        out=xo_sb[:, gt % 2, half * TB:(half + 1) * TB],
                                        in0=ps_d[:, slot, :],
                                        scalar=sj_sb[:, tcg:tcg + 1],
                                        in1=xe_sb[:, gt % 2, half * TB:(half + 1) * TB],
                                        op0=ALU.mult, op1=ALU.add))

                @blk.tensor
                def _(eng, B=B):
                    eng.wait_ge(s_wc.h, B("wc") + 16)
                    if not shared:
                        eng.wait_ge(s_sj.h, B("sj") + 2)
                    for tb in range(NTB):
                        eng.wait_ge(s_hl.h, B("hl") + 16 * HCN * (tb + 1))
                        for tc in range(4):
                            gt = tb * 4 + tc
                            if gt >= 2:
                                eng.wait_ge(s_xa.h, B("xa") + 2 * (gt - 1))
                            for half in range(2):
                                slot = (tc % 2) * 2 + half
                                for hc in range(HCN):
                                    mm = eng.matmul(ps_d[:, slot, :],
                                                    hid_sb[:, hc, tc * 128:(tc + 1) * 128],
                                                    wd_sb[:, hc, half * TB:(half + 1) * TB],
                                                    start=(hc == 0), stop=(hc == HCN - 1))
                            s_md.inc(mm)

                @blk.scalar
                def _(eng, B=B):
                    sxl = (s_e0, s_e1)
                    for tb in range(NTB):
                        for tc in range(4):
                            gt = tb * 4 + tc
                            if gt >= 2:
                                eng.wait_ge(s_xa.h, B("xa") + 2 * (gt - 1))
                            sxl[gt % 2].inc(eng.dma_start(out=xe_sb[:, gt % 2, :],
                                                          in_=x3[gt]), 16)

                @blk.gpsimd
                def _(eng, B=B):
                    sxs = (s_o0, s_o1)
                    dst = xd3 if (shared and not sparse) else x3
                    for tb in range(NTB):
                        for tc in range(4):
                            gt = tb * 4 + tc
                            eng.wait_ge(s_xa.h, B("xa") + 2 * (gt + 1))
                            sxs[gt % 2].inc(eng.dma_start(out=dst[gt],
                                                          in_=xo_sb[:, gt % 2, :]), 16)
                    eng.wait_ge(s_o0.h, B("o0") + 16 * 16)
                    eng.wait_ge(s_o1.h, B("o1") + 16 * 16)

        c2.close()
        assert s_md.n - b[tag + "md"] == 32 and s_xa.n - b[tag + "xa"] == 64

    if not sparse:
        c_ln.close()
    c.close()


def _emit_expert_sparse(p, D, tag, j, ln_tm, slot_f32, scale_sb, srowb,
                        iotac_sb, iota128_sb, x3, xd3, final):
    nc = p.nc
    ones_f32 = p.ones_f32
    base = float(j * CAP)
    NTC = CAP // 128
    NB2 = CAP // TB

    ce = ExitStack()
    EE = ce.enter_context
    elnT = EE(nc.sbuf_tensor(_u("fs_elnT"), [128, EC, CAP], BF16))
    hid_sb = EE(nc.sbuf_tensor(_u("fs_hid"), [128, HCN, CAP], BF16))
    eout_sb = EE(nc.sbuf_tensor(_u("fs_eout"), [128, NTC, E], BF16))

    s_ohb = p.s(tag + "ohb"); s_gm = p.s(tag + "gm"); s_ge = p.s(tag + "ge")
    b = p.base(tag + "ohb", tag + "gm", tag + "ge")

    def B1_(nm, b=b):
        return b[tag + nm]

    c1 = ExitStack()
    oh_t = c1.enter_context(nc.sbuf_tensor(_u("fs_oh"), [128, 2, TB], BF16))
    with ExitStack() as pc:
        ps_gi = pc.enter_context(nc.psum_tensor(_u("fs_psgi"), [128, 8, TB], F32))
        with nc.Block(no_gpsimd_drain=True) as blk:
            @blk.vector
            def _(eng, B=B1_):
                for sh in range(2):
                    for tt in range(32):
                        k = sh * 32 + tt
                        if k >= 2:
                            eng.wait_ge(s_gm.h, B("gm") + k - 1)
                        s_ohb.inc(eng.tensor_scalar(
                            out=oh_t[:, k % 2, :],
                            in0=iotac_sb[:, sh * TB:(sh + 1) * TB],
                            scalar1=slot_f32[:, tt:tt + 1], op0=ALU.subtract,
                            scalar2=base, op1=ALU.add))
                        eng.wait_ge(s_ohb.h, B("ohb") + 2 * k + 1)
                        s_ohb.inc(eng.tensor_scalar(
                            out=oh_t[:, k % 2, :], in0=oh_t[:, k % 2, :],
                            scalar1=0.0, scalar2=None, op0=ALU.is_equal))
                    eng.wait_ge(s_gm.h, B("gm") + 32 * (sh + 1))
                    for ec in range(EC):
                        s_ge.inc(eng.tensor_copy(
                            out=elnT[:, ec, sh * TB:(sh + 1) * TB],
                            in_=ps_gi[:, ec, :]))

            @blk.tensor
            def _(eng, B=B1_):
                for sh in range(2):
                    if sh == 1:
                        eng.wait_ge(s_ge.h, B("ge") + EC)
                    for tt in range(32):
                        k = sh * 32 + tt
                        eng.wait_ge(s_ohb.h, B("ohb") + 2 * k + 2)
                        for ec in range(EC):
                            mm = eng.matmul(ps_gi[:, ec, :],
                                            ln_tm[:, tt, ec * 128:(ec + 1) * 128],
                                            oh_t[:, k % 2, :],
                                            start=(tt == 0), stop=(tt == 31))
                        s_gm.inc(mm)

    c1.close()
    if os.environ.get("DBG2"):
        sdb = p.s(tag + "xs0")
        with nc.Block(no_gpsimd_drain=True) as blk:
            @blk.gpsimd
            def _(eng, j=j):
                for ec in range(EC):
                    sdb.inc(eng.dma_start(
                        out=D["eln_dbg"][j, ec * 128:(ec + 1) * 128, :],
                        in_=elnT[:, ec, :]), 16)
                eng.wait_ge(sdb.h, sdb.n)
    assert s_gm.n - b[tag + "gm"] == 64 and s_ge.n - b[tag + "ge"] == 16

    c2 = ExitStack()
    E2_ = c2.enter_context
    wustg = E2_(nc.sbuf_tensor(_u("fs_wustg"), [128, 2, EC, 128], F32))
    wgstg = E2_(nc.sbuf_tensor(_u("fs_wgstg"), [128, 2, EC, 128], F32))
    wub = E2_(nc.sbuf_tensor(_u("fs_wub"), [128, 2, EC, 128], BF16))
    wgb = E2_(nc.sbuf_tensor(_u("fs_wgb"), [128, 2, EC, 128], BF16))
    gsil = E2_(nc.sbuf_tensor(_u("fs_gsil"), [128, 2, TB], BF16))
    s_l0 = p.s(tag + "l0"); s_l1 = p.s(tag + "l1")
    s_cs = p.s(tag + "cs"); s_mu = p.s(tag + "mu"); s_mg = p.s(tag + "mg")
    s_si = p.s(tag + "si"); s_hb = p.s(tag + "hb")
    b = p.base(*(tag + nm for nm in ("l0", "l1", "cs", "mu", "mg", "si", "hb")))

    def B2_(nm, b=b):
        return b[tag + nm]

    wu_src = D["wu"][j]
    wg_src = D["wg"][j]
    with ExitStack() as pc:
        ps_u = pc.enter_context(nc.psum_tensor(_u("fs_psu"), [128, 2, TB], F32))
        ps_g = pc.enter_context(nc.psum_tensor(_u("fs_psg"), [128, 2, TB], F32))
        with nc.Block(no_gpsimd_drain=True) as blk:
            @blk.sync
            def _(eng, B=B2_):
                sl = (s_l0, s_l1)
                wu3 = wu_src.rearrange("(c k) h -> k c h", k=128)
                wg3 = wg_src.rearrange("(c k) h -> k c h", k=128)
                for hc in range(HCN):
                    if hc >= 2:
                        eng.wait_ge(s_cs.h, B("cs") + 2 * (hc - 1))
                    sl[hc % 2].inc(eng.dma_start(
                        out=wustg[:, hc % 2, :, :],
                        in_=wu3[:, :, hc * 128:(hc + 1) * 128]), 16)
                    sl[hc % 2].inc(eng.dma_start(
                        out=wgstg[:, hc % 2, :, :],
                        in_=wg3[:, :, hc * 128:(hc + 1) * 128]), 16)

            @blk.vector
            def _(eng, B=B2_):
                sl = (s_l0, s_l1)
                for hc in range(HCN):
                    eng.wait_ge(sl[hc % 2].h, B("l%d" % (hc % 2)) + 32 * (hc // 2 + 1))
                    if hc >= 2:
                        eng.wait_ge(s_mg.h, B("mg") + NB2 * (hc - 1))
                    s_cs.inc(eng.tensor_copy(out=wub[:, hc % 2, :, :],
                                             in_=wustg[:, hc % 2, :, :]))
                    s_cs.inc(eng.tensor_copy(out=wgb[:, hc % 2, :, :],
                                             in_=wgstg[:, hc % 2, :, :]))
                    for tblk in range(NB2):
                        g = hc * NB2 + tblk
                        eng.wait_ge(s_mu.h, B("mu") + g + 1)
                        eng.wait_ge(s_si.h, B("si") + g + 1)
                        s_hb.inc(eng.tensor_tensor(
                            out=hid_sb[:, hc, tblk * TB:(tblk + 1) * TB],
                            in0=ps_u[:, tblk % 2, :],
                            in1=gsil[:, g % 2, :], op=ALU.mult))

            @blk.scalar
            def _(eng, B=B2_):
                for hc in range(HCN):
                    for tblk in range(NB2):
                        g = hc * NB2 + tblk
                        eng.wait_ge(s_mg.h, B("mg") + g + 1)
                        if g >= 2:
                            eng.wait_ge(s_hb.h, B("hb") + g - 1)
                        s_si.inc(eng.activation(out=gsil[:, g % 2, :],
                                                in_=ps_g[:, tblk % 2, :],
                                                func=AF.Silu))

            @blk.tensor
            def _(eng, B=B2_):
                for hc in range(HCN):
                    eng.wait_ge(s_cs.h, B("cs") + 2 * (hc + 1))
                    for tblk in range(NB2):
                        g = hc * NB2 + tblk
                        if g >= 2:
                            eng.wait_ge(s_hb.h, B("hb") + g - 1)
                        for e in range(EC):
                            mm = eng.matmul(ps_u[:, tblk % 2, :],
                                            wub[:, hc % 2, e, :],
                                            elnT[:, e, tblk * TB:(tblk + 1) * TB],
                                            start=(e == 0), stop=(e == EC - 1))
                        s_mu.inc(mm)
                        if g >= 2:
                            eng.wait_ge(s_si.h, B("si") + g - 1)
                        for e in range(EC):
                            mm = eng.matmul(ps_g[:, tblk % 2, :],
                                            wgb[:, hc % 2, e, :],
                                            elnT[:, e, tblk * TB:(tblk + 1) * TB],
                                            start=(e == 0), stop=(e == EC - 1))
                        s_mg.inc(mm)

    c2.close()

    c3 = ExitStack()
    E3_ = c3.enter_context
    wdstg = E3_(nc.sbuf_tensor(_u("fs_wdstg"), [128, 2, TB], F32))
    wdb = E3_(nc.sbuf_tensor(_u("fs_wdb"), [128, 2, TB], BF16))
    s_w0 = p.s(tag + "w0"); s_w1 = p.s(tag + "w1"); s_wc = p.s(tag + "wc")
    s_md = p.s(tag + "md"); s_ev = p.s(tag + "ev2")
    b = p.base(*(tag + nm for nm in ("w0", "w1", "wc", "md", "ev2")))

    def B3_(nm, b=b):
        return b[tag + nm]

    wd_src = D["wd"][j]
    with ExitStack() as pc:
        ps_d = pc.enter_context(nc.psum_tensor(_u("fs_psd"), [128, 8, TB], F32))
        with nc.Block(no_gpsimd_drain=True) as blk:
            @blk.sync
            def _(eng, B=B3_):
                swp = (s_w0, s_w1)
                wd3 = wd_src.rearrange("(c k) e -> k c e", k=128)
                for r in range(64):
                    eh, hc = divmod(r, HCN)
                    if r >= 2:
                        eng.wait_ge(s_wc.h, B("wc") + r - 1)
                    swp[r % 2].inc(eng.dma_start(
                        out=wdstg[:, r % 2, :],
                        in_=wd3[:, hc, eh * TB:(eh + 1) * TB]), 16)

            @blk.vector
            def _(eng, B=B3_):
                swp = (s_w0, s_w1)
                for eh in range(2):
                    for hc in range(HCN):
                        r = eh * HCN + hc
                        eng.wait_ge(swp[r % 2].h, B("w%d" % (r % 2)) + 16 * (r // 2 + 1))
                        if r >= 2:
                            eng.wait_ge(s_md.h, B("md") + r - 1)
                        s_wc.inc(eng.tensor_copy(out=wdb[:, r % 2, :],
                                                 in_=wdstg[:, r % 2, :]))
                    eng.wait_ge(s_md.h, B("md") + 32 * (eh + 1))
                    for tc in range(NTC):
                        s_ev.inc(eng.tensor_copy(
                            out=eout_sb[:, tc, eh * TB:(eh + 1) * TB],
                            in_=ps_d[:, tc, :]))

            @blk.tensor
            def _(eng, B=B3_):
                for eh in range(2):
                    if eh == 1:
                        eng.wait_ge(s_ev.h, B("ev2") + NTC)
                    for hc in range(HCN):
                        r = eh * HCN + hc
                        eng.wait_ge(s_wc.h, B("wc") + r + 1)
                        for tc in range(NTC):
                            mm = eng.matmul(ps_d[:, tc, :],
                                            hid_sb[:, hc, tc * 128:(tc + 1) * 128],
                                            wdb[:, r % 2, :],
                                            start=(hc == 0), stop=(hc == HCN - 1))
                        s_md.inc(mm)

    c3.close()
    if os.environ.get("DBG2"):
        sdb = p.s(tag + "xs0")
        with nc.Block(no_gpsimd_drain=True) as blk:
            @blk.gpsimd
            def _(eng, j=j):
                for tc in range(NTC):
                    sdb.inc(eng.dma_start(
                        out=D["eout_dbg"][j, tc * 128:(tc + 1) * 128, :],
                        in_=eout_sb[:, tc, :]), 16)
                eng.wait_ge(sdb.h, sdb.n)
    assert s_md.n - b[tag + "md"] == 64 and s_ev.n - b[tag + "ev2"] == 16

    c4 = ExitStack()
    E4_ = c4.enter_context
    oht = E4_(nc.sbuf_tensor(_u("fs_oht"), [128, 4, 128], BF16))
    xe_sb = E4_(nc.sbuf_tensor(_u("fs_xe"), [128, 2, E], F32))
    xo_sb = E4_(nc.sbuf_tensor(_u("fs_xo"), [128, 2, E], F32))
    s_sr = p.s(tag + "sr"); s_sbc = p.s(tag + "sbc"); s_oht = p.s(tag + "oht")
    s_om = p.s(tag + "om"); s_xa = p.s(tag + "xa")
    s_e0 = p.s(tag + "e0"); s_e1 = p.s(tag + "e1")
    s_o0 = p.s(tag + "o0"); s_o1 = p.s(tag + "o1")
    b = p.base(*(tag + nm for nm in ("sr", "sbc", "oht", "om", "xa",
                                     "e0", "e1", "o0", "o1")))

    def B4_(nm, b=b):
        return b[tag + nm]

    dst = xd3 if final else x3
    with ExitStack() as pc:
        ps_sb = pc.enter_context(nc.psum_tensor(_u("fs_pssb"), [128, 2, TB], F32))
        ps_o = pc.enter_context(nc.psum_tensor(_u("fs_pso"), [128, 4, TB], F32))
        with nc.Block(no_gpsimd_drain=True) as blk:
            @blk.sync
            def _(eng, B=B4_):
                for tb in range(8):
                    if tb >= 1:
                        eng.wait_ge(s_sbc.h, B("sbc") + 4 * tb)
                    s_sr.inc(eng.dma_start(out=srowb[:],
                                           in_=D["slot_d"][tb * TB:(tb + 1) * TB].unsqueeze(0)), 16)

            @blk.tensor
            def _(eng, B=B4_):
                for tt in range(32):
                    eng.wait_ge(s_sr.h, B("sr") + 16 * (tt // 4 + 1))
                    if tt >= 2:
                        eng.wait_ge(s_oht.h, B("oht") + 8 * (tt - 1))
                    s_sbc.inc(eng.matmul(ps_sb[:, tt % 2, 0:128],
                                         ones_f32[0:1, 0:128],
                                         srowb[0:1, (tt % 4) * 128:(tt % 4 + 1) * 128],
                                         start=True, stop=True))
                    if tt >= 2:
                        eng.wait_ge(s_xa.h, B("xa") + 2 * (tt - 1))
                    for sc in range(NTC):
                        q = tt * 8 + sc
                        eng.wait_ge(s_oht.h, B("oht") + q + 1)
                        for eh in range(2):
                            mm = eng.matmul(ps_o[:, (tt % 2) * 2 + eh, :],
                                            oht[:, q % 4, :],
                                            eout_sb[:, sc, eh * TB:(eh + 1) * TB],
                                            start=(sc == 0), stop=(sc == NTC - 1))
                        s_om.inc(mm)

            @blk.vector
            def _(eng, B=B4_):
                se = (s_e0, s_e1)
                so = (s_o0, s_o1)
                for tt in range(32):
                    eng.wait_ge(s_sbc.h, B("sbc") + tt + 1)
                    for sc in range(NTC):
                        q = tt * 8 + sc
                        if q >= 4:
                            eng.wait_ge(s_om.h, B("om") + q - 3)
                        s_oht.inc(eng.tensor_scalar(
                            out=oht[:, q % 4, :], in0=ps_sb[:, tt % 2, 0:128],
                            scalar1=iota128_sb[:, 0:1], op0=ALU.subtract,
                            scalar2=base + sc * 128.0, op1=ALU.is_equal))
                    eng.wait_ge(s_om.h, B("om") + 8 * (tt + 1))
                    eng.wait_ge(se[tt % 2].h, B("e%d" % (tt % 2)) + 16 * (tt // 2 + 1))
                    if tt >= 2:
                        eng.wait_ge(so[tt % 2].h, B("o%d" % (tt % 2)) + 16 * (tt // 2))
                    for eh in range(2):
                        s_xa.inc(eng.scalar_tensor_tensor(
                            out=xo_sb[:, tt % 2, eh * TB:(eh + 1) * TB],
                            in0=ps_o[:, (tt % 2) * 2 + eh, :],
                            scalar=scale_sb[:, tt:tt + 1],
                            in1=xe_sb[:, tt % 2, eh * TB:(eh + 1) * TB],
                            op0=ALU.mult, op1=ALU.add))

            @blk.scalar
            def _(eng, B=B4_):
                se = (s_e0, s_e1)
                for tt in range(32):
                    if tt >= 2:
                        eng.wait_ge(s_xa.h, B("xa") + 2 * (tt - 1))
                    se[tt % 2].inc(eng.dma_start(out=xe_sb[:, tt % 2, :],
                                                 in_=x3[tt]), 16)

            @blk.gpsimd
            def _(eng, B=B4_):
                so = (s_o0, s_o1)
                for tt in range(32):
                    eng.wait_ge(s_xa.h, B("xa") + 2 * (tt + 1))
                    so[tt % 2].inc(eng.dma_start(out=dst[tt],
                                                 in_=xo_sb[:, tt % 2, :]), 16)
                eng.wait_ge(s_o0.h, s_o0.n)
                eng.wait_ge(s_o1.h, s_o1.n)

    c4.close()
    assert s_oht.n - b[tag + "oht"] == 256 and s_om.n - b[tag + "om"] == 256
    assert s_xa.n - b[tag + "xa"] == 64
    ce.close()




def emit_attn_b(p, x_src, x_dst, final):
    nc, D = p.nc, p.D
    ident_sb, ones_bf, ones_f32 = p.ident_sb, p.ones_bf, p.ones_f32
    tag = "f_"

    c = ExitStack()
    EN = c.enter_context
    win_sb = EN(nc.sbuf_tensor(_u("c_win"), [128, EC, HID], BF16))
    wout_sb = EN(nc.sbuf_tensor(_u("c_wout"), [128, 16, E], BF16))
    wssm_sb = EN(nc.sbuf_tensor(_u("c_wssm"), [128, 16, 32], BF16))
    convw_sb = EN(nc.sbuf_tensor(_u("c_convw"), [128, 16, 4], F32))
    convb_sb = EN(nc.sbuf_tensor(_u("c_convb"), [128, 16], F32))
    dv_sb = EN(nc.sbuf_tensor(_u("c_dv"), [128, 16], F32))
    bout_sb = EN(nc.sbuf_tensor(_u("c_bout"), [1, E], BF16))
    boutf_sb = EN(nc.sbuf_tensor(_u("c_boutf"), [1, E], F32))
    amat_sb = EN(nc.sbuf_tensor(_u("c_amat"), [32, 32], F32))
    amatT_sb = EN(nc.sbuf_tensor(_u("c_amatT"), [32, 32], F32))
    bvc_sb = EN(nc.sbuf_tensor(_u("c_bvc"), [32, 2], F32))
    bssm_sb = EN(nc.sbuf_tensor(_u("c_bssm"), [32, 1], F32))
    binr_sb = EN(nc.sbuf_tensor(_u("c_binr"), [1, HID], BF16))
    binrf_sb = EN(nc.sbuf_tensor(_u("c_binrf"), [1, TB], F32))
    pw_M = EN(nc.sbuf_tensor(_u("c_pwM"), [32, 10, 32], F32))
    pw_L = EN(nc.sbuf_tensor(_u("c_pwL"), [32, 10, 32], F32))
    dcols = EN(nc.sbuf_tensor(_u("c_dcols"), [32, TB], F32))
    hc_sb = EN(nc.sbuf_tensor(_u("c_hc"), [32, 2], F32))
    g_sb = EN(nc.sbuf_tensor(_u("c_g"), [32, TB], F32))
    srow_sb = EN(nc.sbuf_tensor(_u("c_srow"), [1, TB], F32))
    halo = EN(nc.sbuf_tensor(_u("c_halo"), [128, 16, 3], BF16))
    stats_sb = EN(nc.sbuf_tensor(_u("c_stats"), [128, 4, 2], F32))

    x3 = x_src.rearrange("(n q) e -> n q e", q=128)
    xd3 = x_dst.rearrange("(n q) e -> n q e", q=128)

    cs = ExitStack()
    wstg = cs.enter_context(nc.sbuf_tensor(_u("c_wstg"), [128, 2, EC * TB], F32))
    swp = (p.s(tag + "l0"), p.s(tag + "l1"))
    swc = p.s(tag + "wc")
    b0 = p.base(tag + "l0", tag + "l1", tag + "wc")
    with nc.Block(no_gpsimd_drain=True) as blk:
        @blk.sync
        def _(eng):
            w3 = D["winT"].rearrange("(c k) h -> k c h", k=128)
            wo3 = D["woutT"].rearrange("(c k) e -> k c e", k=128)
            for q in range(12):
                if q >= 2:
                    eng.wait_ge(swc.h, b0[tag + "wc"] + q - 1)
                if q < EC:
                    swp[q % 2].inc(eng.dma_start(
                        out=wstg[:, q % 2, :].rearrange("k (c t) -> k c t", c=EC),
                        in_=w3[:, :, q * TB:(q + 1) * TB]), 16)
                else:
                    r = q - EC
                    swp[q % 2].inc(eng.dma_start(
                        out=wstg[:, q % 2, :].rearrange("k (c t) -> k c t", c=4),
                        in_=wo3[:, r * 4:(r + 1) * 4, :]), 16)

        @blk.vector
        def _(eng):
            for q in range(12):
                eng.wait_ge(swp[q % 2].h, b0[tag + "l%d" % (q % 2)] + 16 * (q // 2 + 1))
                if q < EC:
                    swc.inc(eng.tensor_copy(
                        out=win_sb[:, :, q * TB:(q + 1) * TB],
                        in_=wstg[:, q % 2, :].rearrange("k (c t) -> k c t", c=EC)))
                else:
                    r = q - EC
                    swc.inc(eng.tensor_copy(
                        out=wout_sb[:, r * 4:(r + 1) * 4, :],
                        in_=wstg[:, q % 2, :].rearrange("k (c t) -> k c t", c=4)))

    cs.close()

    s_sm = p.s(tag + "sm"); s_sc = p.s(tag + "sc")
    s_pp = p.s(tag + "pp"); s_pe = p.s(tag + "pe")
    b1 = p.base(tag + "sm", tag + "sc", tag + "pp", tag + "pe")
    cs2 = ExitStack()
    smstg = cs2.enter_context(nc.sbuf_tensor(_u("c_smstg"), [128, 16, 32], F32))
    with ExitStack() as pc:
        ps_p = pc.enter_context(nc.psum_tensor(_u("c_psp"), [32, 3, TB], F32))
        with nc.Block(no_gpsimd_drain=True) as blk:
            @blk.sync
            def _(eng):
                ws3 = D["wssmT"].rearrange("(c k) n -> k c n", k=128)
                s_sm.inc(eng.dma_start(out=smstg[:], in_=ws3[:]), 16)
                s_sm.inc(eng.dma_start(out=convw_sb[:], in_=D["convw_fm"][:]), 16)
                s_sm.inc(eng.dma_start(out=convb_sb[:], in_=D["convb_fm"][:]), 16)
                s_sm.inc(eng.dma_start(out=dv_sb[:], in_=D["Dv_fm"][:]), 16)
                s_sm.inc(eng.dma_start(out=boutf_sb[:], in_=D["b_out"].unsqueeze(0)), 16)
                s_sm.inc(eng.dma_start(out=amat_sb[:], in_=D["Amat"][:]), 16)
                s_sm.inc(eng.dma_start(out=amatT_sb[:], in_=D["AmatT"][:]), 16)
                s_sm.inc(eng.dma_start(out=bvc_sb[:], in_=D["BvC"][:]), 16)
                s_sm.inc(eng.dma_start(out=bssm_sb[:], in_=D["b_ssm"].unsqueeze(1)), 16)
                for r in range(8):
                    eng.wait_ge(s_sc.h, b1[tag + "sc"] + 4 + r)
                    s_sm.inc(eng.dma_start(out=binrf_sb[:],
                                           in_=D["b_in"][r * TB:(r + 1) * TB].unsqueeze(0)), 16)

            @blk.vector
            def _(eng):
                eng.wait_ge(s_sm.h, b1[tag + "sm"] + 144)
                s_sc.inc(eng.tensor_copy(out=wssm_sb[:], in_=smstg[:]))
                s_sc.inc(eng.tensor_copy(out=pw_M[:, 0, :], in_=amat_sb[:]))
                s_sc.inc(eng.tensor_copy(out=pw_L[:, 0, :], in_=amatT_sb[:]))
                s_sc.inc(eng.tensor_copy(out=bout_sb[:], in_=boutf_sb[:]))
                for r in range(8):
                    eng.wait_ge(s_sm.h, b1[tag + "sm"] + 144 + 16 * (r + 1))
                    s_sc.inc(eng.tensor_copy(out=binr_sb[0:1, r * TB:(r + 1) * TB],
                                             in_=binrf_sb[:]))
                eng.wait_ge(s_pe.h, b1[tag + "pe"] + 1)
                s_pp.inc(eng.tensor_copy(out=dcols[:, 0:1], in_=ps_p[0:32, 2, 0:1]))
                for s in range(9):
                    n = 1 << s
                    eng.wait_ge(s_pe.h, b1[tag + "pe"] + 3 * s + 3)
                    s_pp.inc(eng.tensor_copy(out=pw_M[:, s + 1, :],
                                             in_=ps_p[0:32, 0, 0:32]))
                    s_pp.inc(eng.tensor_copy(out=pw_L[:, s + 1, :],
                                             in_=ps_p[0:32, 1, 0:32]))
                    eng.wait_ge(s_pe.h, b1[tag + "pe"] + 3 * s + 4)
                    s_pp.inc(eng.tensor_copy(out=dcols[:, n:2 * n],
                                             in_=ps_p[0:32, 2, 0:n]))

            @blk.gpsimd
            def _(eng):
                eng.memset(hc_sb[:], 0.0)
                eng.memset(halo[:], 0.0)

            @blk.tensor
            def _(eng):
                eng.wait_ge(s_sc.h, b1[tag + "sc"] + 12)
                s_pe.inc(eng.matmul(ps_p[0:32, 2, 0:1], amat_sb[:],
                                    bvc_sb[:, 1:2], start=True, stop=True))
                for s in range(9):
                    n = 1 << s
                    eng.wait_ge(s_pp.h, b1[tag + "pp"] + 3 * s + 1)
                    s_pe.inc(eng.matmul(ps_p[0:32, 0, 0:32], pw_L[:, s, :],
                                        pw_M[:, s, :], start=True, stop=True))
                    s_pe.inc(eng.matmul(ps_p[0:32, 1, 0:32], pw_M[:, s, :],
                                        pw_L[:, s, :], start=True, stop=True))
                    s_pe.inc(eng.matmul(ps_p[0:32, 2, 0:n], pw_M[:, s, :],
                                        dcols[:, 0:n], start=True, stop=True))

    cs2.close()
    assert s_pe.n - b1[tag + "pe"] == 28 and s_pp.n - b1[tag + "pp"] == 28

    lnT = EN(nc.sbuf_tensor(_u("c_lnT"), [128, EC, TB], BF16))
    xt = EN(nc.sbuf_tensor(_u("c_xt"), [128, 4, E], F32))
    lnb = EN(nc.sbuf_tensor(_u("c_lnb"), [128, 4, E], BF16))
    bn6 = EN(nc.sbuf_tensor(_u("c_bn6"), [128, 2, 6], F32))
    mvv = EN(nc.sbuf_tensor(_u("c_mvv"), [128, 2], F32))
    lnv = EN(nc.sbuf_tensor(_u("c_lnv"), [128, 1], F32))
    xzh = EN(nc.sbuf_tensor(_u("c_xzh"), [128, 2, 515], BF16))
    cvt = EN(nc.sbuf_tensor(_u("c_cvt"), [128, 2, TB], F32))
    xh_act = EN(nc.sbuf_tensor(_u("c_xha"), [128, 16, TB], BF16))
    gate_sil = EN(nc.sbuf_tensor(_u("c_gsl"), [128, 16, TB], BF16))
    gtmp = EN(nc.sbuf_tensor(_u("c_gtmp"), [128, 2, TB], F32))
    xo_sb = EN(nc.sbuf_tensor(_u("c_xo"), [128, 2, E], F32))

    for tb in range(NTB):
        names_a = ("ld", "bn", "mv", "rs", "ln", "tp", "lt", "mz", "ez", "cv",
                   "sh", "sm2", "g2", "sc2")
        sems = {nm: p.s(tag + nm) for nm in names_a}
        b = {nm: sems[nm].n for nm in names_a}
        sld = sems["ld"]; sbn = sems["bn"]; smv = sems["mv"]; srs = sems["rs"]
        sln = sems["ln"]; stp = sems["tp"]; slt = sems["lt"]; smz = sems["mz"]
        sez = sems["ez"]; scv = sems["cv"]; ssh = sems["sh"]
        ssm2 = sems["sm2"]; sg2 = sems["g2"]; ssc2 = sems["sc2"]

        act_ops = []
        for cc in range(32):
            act_ops.append(("evac", cc))
            if 2 <= cc <= 17:
                act_ops.append(("silu", cc - 2))
        evac_done = {}
        silu_done = {}
        k = 0
        for op, i in act_ops:
            k += 1
            if op == "evac":
                evac_done[i] = k
            else:
                silu_done[i] = k

        with ExitStack() as pc:
            ps_t = pc.enter_context(nc.psum_tensor(_u("c_pst"), [128, 2, 1024], BF16))
            ps_z = pc.enter_context(nc.psum_tensor(_u("c_psz"), [128, 2, TB], F32))
            ps_sm = pc.enter_context(nc.psum_tensor(_u("c_pssm"), [32, 1, TB], F32))
            ps_ms = pc.enter_context(nc.psum_tensor(_u("c_psms"), [32, 3, TB], F32))
            with nc.Block(no_gpsimd_drain=True) as blk:
                @blk.sync
                def _(eng, tb=tb, b=b):
                    eng.wait_ge(sln.h, b["ln"])
                    for tc in range(4):
                        sld.inc(eng.dma_start(out=xt[:, tc, :],
                                              in_=x3[tb * 4 + tc]), 16)

                @blk.vector
                def _(eng, tb=tb, b=b, evac_done=evac_done, silu_done=silu_done):
                    eng.wait_ge(stp.h, b["tp"])
                    for tc in range(4):
                        eng.wait_ge(sld.h, b["ld"] + 64)
                        if tc >= 1:
                            eng.wait_ge(srs.h, b["rs"] + 2 * tc)
                        sbn.inc(eng.bn_stats(out=bn6[:, 0, :], in_=xt[:, tc, 0:512]))
                        sbn.inc(eng.bn_stats(out=bn6[:, 1, :], in_=xt[:, tc, 512:1024]))
                        eng.wait_ge(sbn.h, b["bn"] + 2 * (tc + 1))
                        smv.inc(eng.bn_aggr(out=mvv[:], in_=bn6[:].rearrange("q a v -> q (a v)")))
                        eng.wait_ge(smv.h, b["mv"] + tc + 1)
                        eng.tensor_copy(out=stats_sb[:, tc, 0:1], in_=mvv[:, 0:1])
                        eng.wait_ge(srs.h, b["rs"] + 2 * (tc + 1))
                        sln.inc(eng.tensor_scalar(out=lnb[:, tc, :], in0=xt[:, tc, :],
                                                  scalar1=mvv[:, 0:1], op0=ALU.subtract,
                                                  scalar2=stats_sb[:, tc, 1:2], op1=ALU.mult))
                    for k in range(32):
                        tc, e = divmod(k, EC)
                        eng.wait_ge(stp.h, b["tp"] + k + 1)
                        slt.inc(eng.tensor_copy(out=lnT[:, e, tc * 128:(tc + 1) * 128],
                                                in_=ps_t[:, k % 2, 0:128]))
                    for cc in range(16):
                        eng.wait_ge(sez.h, b["ez"] + evac_done[cc])
                        if cc >= 2:
                            eng.wait_ge(sez.h, b["ez"] + silu_done[cc - 2])
                        w = convw_sb[:, cc, :]
                        sl = cc % 2
                        scv.inc(eng.tensor_copy(out=xzh[:, sl, 0:3],
                                                in_=halo[:, cc, :]))
                        eng.wait_ge(scv.h, b["cv"] + 5 * cc + 1)
                        scv.inc(eng.tensor_scalar(out=cvt[:, sl, :],
                                                  in0=xzh[:, sl, 0:512],
                                                  scalar1=w[:, 0:1], scalar2=None,
                                                  op0=ALU.mult))
                        for kk in range(1, 4):
                            eng.wait_ge(scv.h, b["cv"] + 5 * cc + kk + 1)
                            scv.inc(eng.scalar_tensor_tensor(
                                out=cvt[:, sl, :], in0=xzh[:, sl, kk:512 + kk],
                                scalar=w[:, kk:kk + 1], in1=cvt[:, sl, :],
                                op0=ALU.mult, op1=ALU.add))
                        eng.wait_ge(scv.h, b["cv"] + 5 * cc + 5)
                        ssh.inc(eng.tensor_copy(out=halo[:, cc, :],
                                                in_=xzh[:, sl, 512:515]))
                    eng.wait_ge(ssm2.h, b["sm2"] + 1)
                    sg2.inc(eng.tensor_scalar(out=g_sb[:], in0=ps_sm[0:32, 0, :],
                                              scalar1=bssm_sb[:, 0:1], op0=ALU.add,
                                              scalar2=bvc_sb[:, 0:1], op1=ALU.mult))
                    for s in range(9):
                        n = 1 << s
                        eng.wait_ge(ssc2.h, b["sc2"] + s + 1)
                        sg2.inc(eng.tensor_tensor(out=g_sb[:, n:TB],
                                                  in0=g_sb[:, n:TB],
                                                  in1=ps_ms[0:32, s % 2, 0:TB - n],
                                                  op=ALU.add))
                    eng.wait_ge(ssc2.h, b["sc2"] + 11)
                    sg2.inc(eng.tensor_copy(out=srow_sb[:], in_=ps_ms[0:1, 2, :]))
                    sg2.inc(eng.tensor_tensor(out=hc_sb[:, (tb + 1) % 2:(tb + 1) % 2 + 1],
                                              in0=ps_ms[0:32, 1, 0:1],
                                              in1=g_sb[:, TB - 1:TB], op=ALU.add))

                @blk.scalar
                def _(eng, tb=tb, b=b, act_ops=act_ops):
                    for tc in range(4):
                        eng.wait_ge(smv.h, b["mv"] + tc + 1)
                        if tc >= 1:
                            eng.wait_ge(srs.h, b["rs"] + 2 * tc)
                        srs.inc(eng.activation(out=lnv[:], in_=mvv[:, 1:2], func=AF.Ln,
                                               bias=p.eps_col[:, 0:1], scale=1.0))
                        eng.wait_ge(srs.h, b["rs"] + 2 * tc + 1)
                        eng.wait_ge(sln.h, b["ln"] + tc)
                        srs.inc(eng.activation(out=stats_sb[:, tc, 1:2], in_=lnv[:],
                                               func=AF.Exp, scale=-0.5))
                    for op, i in act_ops:
                        if op == "evac":
                            cc = i
                            eng.wait_ge(smz.h, b["mz"] + cc + 1)
                            if cc < 16:
                                if cc >= 2:
                                    eng.wait_ge(ssh.h, b["sh"] + cc - 1)
                                sez.inc(eng.copy(out=xzh[:, cc % 2, 3:515],
                                                 in_=ps_z[:, cc % 2, :]))
                            else:
                                sez.inc(eng.activation(out=gate_sil[:, cc - 16, :],
                                                       in_=ps_z[:, cc % 2, :],
                                                       func=AF.Silu))
                        else:
                            j = i
                            eng.wait_ge(scv.h, b["cv"] + 5 * (j + 1))
                            sez.inc(eng.activation(out=xh_act[:, j, :],
                                                   in_=cvt[:, j % 2, :], func=AF.Silu,
                                                   bias=convb_sb[:, j:j + 1]))

                @blk.tensor
                def _(eng, tb=tb, b=b, evac_done=evac_done, silu_done=silu_done):
                    for k in range(32):
                        tc, e = divmod(k, EC)
                        eng.wait_ge(sln.h, b["ln"] + tc + 1)
                        if k >= 2:
                            eng.wait_ge(slt.h, b["lt"] + k - 1)
                        stp.inc(eng.transpose(ps_t[:, k % 2, 0:128],
                                              lnb[:, tc, e * 128:(e + 1) * 128],
                                              ident_sb[:]))
                    eng.wait_ge(slt.h, b["lt"] + 32)
                    for cc in range(32):
                        if cc >= 2:
                            eng.wait_ge(sez.h, b["ez"] + evac_done[cc - 2])
                        for e in range(EC):
                            eng.matmul(ps_z[:, cc % 2, :],
                                       win_sb[:, e, cc * 128:(cc + 1) * 128],
                                       lnT[:, e, :],
                                       start=(e == 0), stop=False)
                        mm = eng.matmul(ps_z[:, cc % 2, :],
                                        binr_sb[0:1, cc * 128:(cc + 1) * 128],
                                        ones_bf[0:1, 0:TB],
                                        start=False, stop=True)
                        smz.inc(mm)
                    for cc in range(16):
                        eng.wait_ge(sez.h, b["ez"] + silu_done[cc])
                        mm = eng.matmul(ps_sm[0:32, 0, :], wssm_sb[:, cc, :],
                                        xh_act[:, cc, :],
                                        start=(cc == 0), stop=(cc == 15))
                    ssm2.inc(mm)
                    for s in range(9):
                        n = 1 << s
                        eng.wait_ge(sg2.h, b["g2"] + s + 1)
                        ssc2.inc(eng.matmul(ps_ms[0:32, s % 2, 0:TB - n],
                                            pw_L[:, s, :], g_sb[:, 0:TB - n],
                                            start=True, stop=True))
                    eng.wait_ge(sg2.h, b["g2"] + 10)
                    eng.matmul(ps_ms[0:1, 2, :], bvc_sb[:, 1:2], g_sb[:],
                               start=True, stop=False)
                    ssc2.inc(eng.matmul(ps_ms[0:1, 2, :],
                                        hc_sb[:, tb % 2:tb % 2 + 1],
                                        dcols[:], start=False, stop=True))
                    ssc2.inc(eng.matmul(ps_ms[0:32, 1, 0:1], pw_L[:, 9, :],
                                        hc_sb[:, tb % 2:tb % 2 + 1],
                                        start=True, stop=True))

            assert smz.n - b["mz"] == 32 and sez.n - b["ez"] == 48
            assert scv.n - b["cv"] == 80 and ssc2.n - b["sc2"] == 11
            assert sg2.n - b["g2"] == 12

        names_b = ("bc", "gt", "yb", "mo", "xp", "xa", "xs0", "xs1")
        semsb = {nm: p.s(tag + nm) for nm in names_b}
        bb = {nm: semsb[nm].n for nm in names_b}
        sbc = semsb["bc"]; sgt = semsb["gt"]; syb = semsb["yb"]
        smo = semsb["mo"]; sxp = semsb["xp"]; sxa = semsb["xa"]
        sxs = (semsb["xs0"], semsb["xs1"])
        with ExitStack() as pc:
            ps_bc = pc.enter_context(nc.psum_tensor(_u("c_psbc"), [128, TB], F32))
            ps_o = pc.enter_context(nc.psum_tensor(_u("c_pso"), [128, 4, TB], F32))
            with nc.Block(no_gpsimd_drain=True) as blk:
                @blk.tensor
                def _(eng, tb=tb, bb=bb):
                    sbc.inc(eng.matmul(ps_bc[:], ones_f32[0:1, 0:128], srow_sb[:],
                                       start=True, stop=True))
                    eng.wait_ge(syb.h, bb["yb"] + 16)
                    for tc in range(4):
                        if tc >= 2:
                            eng.wait_ge(sxa.h, bb["xa"] + 2 * (tc - 1))
                        for half in range(2):
                            slot = (tc % 2) * 2 + half
                            for cc in range(16):
                                mm = eng.matmul(ps_o[:, slot, :],
                                                gate_sil[:, cc, tc * 128:(tc + 1) * 128],
                                                wout_sb[:, cc, half * TB:(half + 1) * TB],
                                                start=(cc == 0), stop=False)
                            mm = eng.matmul(ps_o[:, slot, :], ones_bf[0:1, 0:128],
                                            bout_sb[0:1, half * TB:(half + 1) * TB],
                                            start=False, stop=True)
                        smo.inc(mm)

                @blk.vector
                def _(eng, tb=tb, bb=bb):
                    eng.wait_ge(sbc.h, bb["bc"] + 1)
                    for cc in range(16):
                        sl = cc % 2
                        if cc >= 2:
                            eng.wait_ge(syb.h, bb["yb"] + cc - 1)
                        sgt.inc(eng.tensor_scalar(out=gtmp[:, sl, :], in0=ps_bc[:],
                                                  scalar1=dv_sb[:, cc:cc + 1],
                                                  scalar2=None, op0=ALU.add))
                        eng.wait_ge(sgt.h, bb["gt"] + 2 * cc + 1)
                        sgt.inc(eng.tensor_tensor(out=gtmp[:, sl, :],
                                                  in0=gtmp[:, sl, :],
                                                  in1=xh_act[:, cc, :], op=ALU.mult))
                        eng.wait_ge(sgt.h, bb["gt"] + 2 * cc + 2)
                        syb.inc(eng.tensor_tensor(out=gate_sil[:, cc, :],
                                                  in0=gtmp[:, sl, :],
                                                  in1=gate_sil[:, cc, :], op=ALU.mult))
                    xpl_sb = gtmp[:].rearrange("q a t -> q (a t)")
                    eng.wait_ge(syb.h, bb["yb"] + 16)
                    for tc in range(4):
                        if tc >= 1:
                            eng.wait_ge(sxa.h, bb["xa"] + 2 * tc)
                        sxp.inc(eng.tensor_scalar(out=xpl_sb[:], in0=xt[:, tc, :],
                                                  scalar1=stats_sb[:, tc, 0:1],
                                                  op0=ALU.subtract,
                                                  scalar2=stats_sb[:, tc, 1:2],
                                                  op1=ALU.mult))
                        eng.wait_ge(sxp.h, bb["xp"] + 2 * tc + 1)
                        sxp.inc(eng.tensor_tensor(out=xpl_sb[:], in0=xpl_sb[:],
                                                  in1=xt[:, tc, :], op=ALU.add))
                        eng.wait_ge(smo.h, bb["mo"] + tc + 1)
                        eng.wait_ge(sxp.h, bb["xp"] + 2 * tc + 2)
                        if tc >= 2:
                            eng.wait_ge(sxs[tc % 2].h, bb["xs%d" % (tc % 2)] + 16 * (tc // 2))
                        for half in range(2):
                            slot = (tc % 2) * 2 + half
                            sxa.inc(eng.tensor_tensor(
                                out=xo_sb[:, tc % 2, half * TB:(half + 1) * TB],
                                in0=xpl_sb[:, half * TB:(half + 1) * TB],
                                in1=ps_o[:, slot, :], op=ALU.add))

                @blk.gpsimd
                def _(eng, tb=tb, bb=bb):
                    for tc in range(4):
                        eng.wait_ge(sxa.h, bb["xa"] + 2 * (tc + 1))
                        sxs[tc % 2].inc(eng.dma_start(out=xd3[tb * 4 + tc],
                                                      in_=xo_sb[:, tc % 2, :]), 16)
                    eng.wait_ge(sxs[0].h, sxs[0].n)
                    eng.wait_ge(sxs[1].h, sxs[1].n)

            assert sgt.n - bb["gt"] == 32 and syb.n - bb["yb"] == 16
            assert sxa.n - bb["xa"] == 8 and smo.n - bb["mo"] == 4

    c.close()



def build_core_program():
    nc = bass.Bass("TRN2", target_bir_lowering=False, debug=False)
    ctx = ExitStack()
    p = P(nc, ctx)
    D = p.D

    def din(name, shape, dtype=F32):
        D[name] = nc.dram_tensor(name, list(shape), dtype, kind="ExternalInput").ap()

    def dint(name, shape, dtype=F32):
        D[name] = nc.dram_tensor(name, list(shape), dtype, kind="Internal").ap()

    din("x", (S, E))
    din("wqT", (E, E)); din("wkT", (E, E)); din("wvpT", (E, E)); din("woT", (E, E))
    din("gamma_fm", (128, EC)); din("beta_fm", (128, EC))
    din("ident", (128, 128), BF16)
    din("identf", (128, 128))
    din("cmask", (128, 4, TB), BF16)
    din("wr", (E, 8)); din("bmi", (128, 8))
    din("wu", (NEXP, E, HID)); din("wg", (NEXP, E, HID)); din("wd", (NEXP, HID, E))
    din("wus", (E, HID)); din("wgs", (E, HID)); din("wds", (HID, E))
    din("winT", (E, HID)); din("woutT", (2048, E)); din("wssmT", (2048, 32))
    din("convw_fm", (128, 16, 4)); din("convb_fm", (128, 16))
    din("Dv_fm", (128, 16)); din("b_in_fm", (128, 32)); din("b_out", (E,))
    din("Amat", (32, 32)); din("AmatT", (32, 32)); din("BvC", (32, 2))
    din("b_in", (HID,))
    din("negj", (128, 7)); din("thr", (128, 7)); din("ltri", (128, 128), BF16)
    din("baserow", (1, 7)); din("iotac", (128, CAP)); din("iota128", (128, 1))
    din("b_ssm", (32,))
    D["y"] = nc.dram_tensor("y", [S, E], F32, kind="ExternalOutput").ap()

    dint("x_cur", (S, E))
    dint("qkv_dbg", (E, S), BF16)
    dint("lnq_dbg", (E, S), BF16)
    dint("qT_d", (E, S), BF16)
    dint("kT_d", (E, S), BF16)
    dint("v_d", (NHP, S, 130), BF16)
    dint("hid_d", (HCN, 128, S), BF16)
    dint("slot_d", (S,))
    dint("eln_dbg", (NEXP, E, CAP), BF16)
    dint("eout_dbg", (NEXP, CAP, E), BF16)

    p.ident_sb = ctx.enter_context(nc.sbuf_tensor("ident_g", [128, 128], BF16))
    p.ident_f32 = ctx.enter_context(nc.sbuf_tensor("identf_g", [128, 128], F32))
    p.eps_col = ctx.enter_context(nc.sbuf_tensor("eps_g", [128, 1], F32))
    p.ones_bf = ctx.enter_context(nc.sbuf_tensor("ones_g", [128, TB], BF16))
    p.ones_f32 = ctx.enter_context(nc.sbuf_tensor("onesf_g", [128, 128], F32))
    g = p.s("g_init")
    with nc.Block(no_gpsimd_drain=True) as blk:
        @blk.sync
        def _(eng):
            g.inc(eng.dma_start(out=p.ident_sb[:], in_=D["ident"][:]), 16)
            g.inc(eng.dma_start(out=p.ident_f32[:], in_=D["identf"][:]), 16)

        @blk.gpsimd
        def _(eng):
            eng.memset(p.ones_bf[:], 1.0)
            eng.memset(p.ones_f32[:], 1.0)
            eng.memset(p.eps_col[:], 1e-5)
            eng.wait_ge(g.h, 32)

    emit_attn_a(p, D["x"], D["y"] if OUT_STAGE == 1 else D["x_cur"])
    if OUT_STAGE >= 2:
        emit_ffn(p, D["x_cur"], D["y"] if OUT_STAGE == 2 else D["x_cur"],
                 tag="f_", final=(OUT_STAGE == 2))
    if OUT_STAGE >= 3:
        emit_attn_b(p, D["x_cur"], D["y"] if OUT_STAGE == 3 else D["x_cur"],
                    final=(OUT_STAGE == 3))
    if OUT_STAGE >= 4:
        emit_ffn(p, D["x_cur"], D["y"], tag="f_", final=True)

    ctx.close()
    return nc



def _host_consts():
    ident = np.eye(128, dtype=ml_dtypes.bfloat16)
    cmask = np.zeros((128, 4, TB), dtype=ml_dtypes.bfloat16)
    for ri, r in enumerate((0, 128, 256, 384)):
        cmask[:, ri, :] = (np.arange(128)[:, None] + r <= np.arange(TB)[None, :])
    return ident, cmask


def host_prep(inp):
    def T(a):
        return np.ascontiguousarray(np.asarray(a, dtype=np.float32).T)

    perm = (np.arange(E) % 64) * 16 + (np.arange(E) // 64)
    wq = np.asarray(inp["L0_attn_a_Wq"], np.float32)
    wk = np.asarray(inp["L0_attn_a_Wk"], np.float32)
    wv = np.asarray(inp["L0_attn_a_Wv"], np.float32)
    wo = np.asarray(inp["L0_attn_a_Wo"], np.float32)
    gamma = np.asarray(inp["L0_attn_a_gamma"], np.float32)
    beta = np.asarray(inp["L0_attn_a_beta"], np.float32)

    ident, cmask = _host_consts()
    common = {
        "wqT": T(wq), "wkT": T(wk), "wvpT": np.ascontiguousarray(wv[perm].T),
        "woT": T(wo),
        "gamma_fm": np.ascontiguousarray(gamma.reshape(EC, 128).T),
        "beta_fm": np.ascontiguousarray(beta.reshape(EC, 128).T),
        "ident": ident, "cmask": cmask,
        "identf": np.eye(128, dtype=np.float32),
        "wr": np.ascontiguousarray(np.asarray(inp["L0_ffn_Wr"], np.float32)),
        "bmi": np.ascontiguousarray(
            np.tile(64.0 - np.arange(8, dtype=np.float32), (128, 1))),
        "wu": np.ascontiguousarray(np.asarray(inp["L0_ffn_Wu"], np.float32)),
        "wg": np.ascontiguousarray(np.asarray(inp["L0_ffn_Wg"], np.float32)),
        "wd": np.ascontiguousarray(np.asarray(inp["L0_ffn_Wd"], np.float32)),
        "wus": np.ascontiguousarray(np.asarray(inp["L0_ffn_Wu_s"], np.float32)),
        "wgs": np.ascontiguousarray(np.asarray(inp["L0_ffn_Wg_s"], np.float32)),
        "wds": np.ascontiguousarray(np.asarray(inp["L0_ffn_Wd_s"], np.float32)),
        "winT": T(inp["L0_attn_b_W_in"]),
        "woutT": T(inp["L0_attn_b_W_out"]),
        "wssmT": T(inp["L0_attn_b_W_ssm"]),
        "convw_fm": np.ascontiguousarray(
            np.asarray(inp["L0_attn_b_conv_w"], np.float32).reshape(16, 128, 4)
            .transpose(1, 0, 2)),
        "convb_fm": np.ascontiguousarray(
            np.asarray(inp["L0_attn_b_conv_b"], np.float32).reshape(16, 128).T),
        "Dv_fm": np.ascontiguousarray(
            np.asarray(inp["L0_attn_b_Dv"], np.float32).reshape(16, 128).T),
        "b_in_fm": np.ascontiguousarray(
            np.asarray(inp["L0_attn_b_b_in"], np.float32).reshape(32, 128).T),
        "b_out": np.ascontiguousarray(np.asarray(inp["L0_attn_b_b_out"], np.float32)),
        "b_in": np.ascontiguousarray(np.asarray(inp["L0_attn_b_b_in"], np.float32)),
        "negj": np.ascontiguousarray(np.tile(-np.arange(7, dtype=np.float32), (128, 1))),
        "thr": np.ascontiguousarray(np.tile((np.arange(7, dtype=np.float32) + 1) * CAP,
                                            (128, 1))),
        "ltri": np.triu(np.ones((128, 128), dtype=ml_dtypes.bfloat16), 1),
        "baserow": np.ascontiguousarray(
            (np.arange(7, dtype=np.float32) * CAP).reshape(1, 7)),
        "iotac": np.ascontiguousarray(
            np.tile(np.arange(CAP, dtype=np.float32), (128, 1))),
        "iota128": np.ascontiguousarray(
            np.arange(128, dtype=np.float32).reshape(128, 1)),
        "Amat": np.ascontiguousarray(np.asarray(inp["L0_attn_b_A"], np.float32)),
        "AmatT": T(inp["L0_attn_b_A"]),
        "BvC": np.ascontiguousarray(np.stack(
            [np.asarray(inp["L0_attn_b_Bv"], np.float32),
             np.asarray(inp["L0_attn_b_Cv"], np.float32)], axis=1)),
        "b_ssm": np.ascontiguousarray(np.asarray(inp["L0_attn_b_b_ssm"], np.float32)),
    }
    return common


def kernel(**inp):
    x = np.ascontiguousarray(inp["L0_attn_a_x"], dtype=np.float32)
    common = host_prep(inp)
    nc = build_core_program()
    in_maps = []
    for bb in range(8):
        m = dict(common)
        m["x"] = np.ascontiguousarray(x[bb])
        in_maps.append(m)
    ncore = int(os.environ.get("N_CORES", "8"))
    res = run_bass_kernel_spmd(nc, in_maps[:ncore], core_ids=list(range(ncore)))
    return np.stack([r["y"] for r in res.results]).astype(np.float32)


